# version 02
# baseline (speedup 1.0000x reference)
import numpy as np
import ml_dtypes
import concourse.bass as bass
import concourse.mybir as mybir
from concourse.bass_utils import run_bass_kernel_spmd

F32 = mybir.dt.float32
BF16 = mybir.dt.bfloat16
AF = mybir.ActivationFunctionType
ALU = mybir.AluOpType
AX = mybir.AxisListType
BF = ml_dtypes.bfloat16

S, D, H, CH = 4096, 1024, 16, 64
NCHUNK = S // CH
TS1, TS2 = 512, 256
NS1, NS2 = S // TS1, S // TS2
FF = 4096
EPS = 1e-5
ENGINES = ("sync", "scalar", "vector", "tensor", "gpsimd")


class Plan:

    def __init__(self):
        self.ops = []
        self.cnt = {}
        self.res = {}

    def emit(self, stage, engine, fn, reads=(), writes=(), dma=False, extra_waits=(),
             dsem=None):
        if dma:
            sem = "q_" + (dsem if dsem is not None else engine)
        else:
            sem = "e_" + engine
        waits = {}

        def addw(mark, war=False):
            if mark is None:
                return
            s, v = mark
            if False and war and (not dma) and s == "e_" + engine:
                return
            if v > waits.get(s, -1):
                waits[s] = v

        wset = set(writes)
        for r in reads:
            rr = self.res.get(r)
            if rr:
                addw(rr["w"])
        for r in writes:
            rr = self.res.get(r)
            if rr:
                addw(rr["w"])
                for m in rr["r"]:
                    addw(m, war=True)
        for m in extra_waits:
            addw(m)
        if engine == "gpsimd" and not dma:
            prev = self.cnt.get("e_gpsimd", 0)
            if prev > waits.get("e_gpsimd", -1):
                waits["e_gpsimd"] = prev
        inc_by = 16 if dma else 1
        self.cnt[sem] = self.cnt.get(sem, 0) + inc_by
        mark = (sem, self.cnt[sem])
        self.ops.append((stage, engine, sorted(waits.items()), fn, sem, inc_by))
        for r in reads:
            if r in wset:
                continue
            self.res.setdefault(r, {"w": None, "r": []})["r"].append(mark)
        for r in writes:
            self.res[r] = {"w": mark, "r": []}
        return mark

    def snapshot(self):
        return [(s, v) for s, v in self.cnt.items()]


def build_program(debug=()):
    nc = bass.Bass("TRN2", target_bir_lowering=False)
    p = Plan()

    x0_h = nc.dram_tensor("x0", [S, D], F32, kind="ExternalInput").ap()
    x0T_h = nc.dram_tensor("x0T", [D, S], F32, kind="ExternalInput").ap()
    winT_h = nc.dram_tensor("winT", [D, D], F32, kind="ExternalInput").ap()
    winwT_h = nc.dram_tensor("winwT", [D, H], F32, kind="ExternalInput").ap()
    woT_h = nc.dram_tensor("woT", [D, D], F32, kind="ExternalInput").ap()
    w1T_h = nc.dram_tensor("w1T", [D, FF], F32, kind="ExternalInput").ap()
    w2T_h = nc.dram_tensor("w2T", [FF, D], F32, kind="ExternalInput").ap()
    convw_h = nc.dram_tensor("convw_sb", [128, 32], F32, kind="ExternalInput").ap()
    convb_h = nc.dram_tensor("convb_sb", [128, 8], F32, kind="ExternalInput").ap()
    wbase_h = nc.dram_tensor("wbase_sb", [H, 1], F32, kind="ExternalInput").ap()
    b1_h = nc.dram_tensor("b1_sb", [128, 32], F32, kind="ExternalInput").ap()
    b2_h = nc.dram_tensor("b2_sb", [128, 8], F32, kind="ExternalInput").ap()
    x2_h = nc.dram_tensor("x2", [S, D], F32, kind="ExternalOutput").ap()
    x1_h = nc.dram_tensor("x1_scr", [S, D], F32, kind="Internal").ap()
    w1t_h = nc.dram_tensor("w1t_scr", [D, FF], BF16, kind="Internal").ap()
    w2t_h = nc.dram_tensor("w2t_scr", [FF, D], BF16, kind="Internal").ap()
    rm_h = nc.dram_tensor("rm_scr", [H * 8], F32, kind="Internal").ap()
    nd_h = nc.dram_tensor("nd_scr", [128], F32, kind="Internal").ap()
    dbg_h = {}
    for k, shape, dt in (
        ("d_xi", [D, S], BF16), ("d_X", [S, D], F32), ("d_A", [H, S], F32),
        ("d_Y", [S, D], F32), ("d_attn", [S, D], F32), ("d_x1", [S, D], F32),
        ("d_un", [H, S], F32), ("d_sn", [H, S], F32), ("d_st", [NCHUNK, D], BF16),
        ("d_stp", [NCHUNK, D], BF16), ("d_w1t", [128, 2048], BF16),
        ("d_x1ln", [D, S], BF16), ("d_h", [128, 2048], BF16),
        ("d_m1", [128, 32], F32), ("d_r1", [128, 32], F32),
        ("d_mr1b", [128, TS1], F32), ("d_r1b", [128, TS1], F32),
        ("d_x0fmb", [128, 8 * TS1], BF16), ("d_acs", [H, S], F32),
        ("d_vv", [H, S], F32), ("d_cvv", [H, S], F32), ("d_sdo", [H, S], F32),
        ("d_nrm", [H, S], F32), ("d_wtok", [S, D], BF16), ("d_ytmp", [TS1, D], F32),
    ):
        if k in debug:
            dbg_h[k] = nc.dram_tensor(k, shape, dt, kind="ExternalOutput").ap()

    from contextlib import ExitStack
    stack0 = ExitStack()
    s1stack = ExitStack()
    s2stack = ExitStack()
    DSEMS = ("sync", "gpsimd", "x0tm", "x0fmb", "wstage0", "wstage1", "winbf",
             "winwbf", "wobf", "convw", "convb", "wbase", "b1", "b2", "x1l",
             "w1tb", "w2tb", "stc", "rmw", "rmr", "ndw", "ndr", "st", "dbg",
             "x1st0", "x1st1", "x2st0", "x2st1", "ternst")
    sems = {}
    for en in ENGINES:
        sems["e_" + en] = stack0.enter_context(nc.semaphore("e_" + en))
    for dn in DSEMS:
        sems["q_" + dn] = stack0.enter_context(nc.semaphore("q_" + dn))

    ST1, ST2 = 1, 2

    def sb0(name, shape, dt):
        return stack0.enter_context(nc.sbuf_tensor(name, shape, dt))

    def sb1(name, shape, dt):
        return s1stack.enter_context(nc.sbuf_tensor(name, shape, dt))

    def sb2(name, shape, dt):
        return s2stack.enter_context(nc.sbuf_tensor(name, shape, dt))

    def ps1(name, shape, dt):
        return s1stack.enter_context(nc.psum_tensor(name, shape, dt))

    def ps2(name, shape, dt):
        return s2stack.enter_context(nc.psum_tensor(name, shape, dt))

    block = stack0.enter_context(nc.Block())
    hw = {e: {} for e in ENGINES}

    def run_stage(stage):
        def mk(engine):
            def runner(eng):
                for stg, en, waits, fn, sem, inc_by in p.ops:
                    if stg != stage or en != engine:
                        continue
                    for sname, val in waits:
                        if hw[engine].get(sname, 0) >= val:
                            continue
                        hw[engine][sname] = val
                        eng.wait_ge(sems[sname], val)
                    fn(eng).then_inc(sems[sem], inc_by)
            return runner
        block.sync(mk("sync"))
        block.scalar(mk("scalar"))
        block.vector(mk("vector"))
        block.tensor(mk("tensor"))
        block.gpsimd(mk("gpsimd"))

    ones_f = sb0("ones_f", [128, 1], F32)
    onesr = sb0("onesr", [1, 128], F32)
    ones_b = sb0("ones_b", [128, 1], BF16)
    iden_f = sb0("iden_f", [128, 128], F32)
    convw_r = sb0("convw_r", [128, 32], F32)
    convb_c = sb0("convb_c", [128, 8], F32)
    wbase_c = sb0("wbase_c", [H, 1], F32)
    b1_c = sb0("b1_c", [128, 32], F32)
    b2_c = sb0("b2_c", [128, 8], F32)
    m1s = sb0("m1s", [128, 32], F32)
    junkf = sb0("junkf", [128, 1024], F32)
    junkb = sb0("junkb", [128, 1024], BF16)

    triones = sb1("triones", [128, 128], BF16)
    indstrip = sb1("indstrip", [128, 32], BF16)
    indloc = sb1("indloc", [8, 4 * 128], BF16)
    e4 = sb1("e4", [4, 4 * 128], F32)
    convdg = sb1("convdg", [128, 32 * 128], BF16)
    winbf = sb1("winbf", [128, 8 * D], BF16)
    winwbf = sb1("winwbf", [128, 8 * H], BF16)
    wobf = sb1("wobf", [128, 8 * D], BF16)
    nwinsum = sb1("nwinsum", [128, 8], F32)
    nwinwsum = sb1("nwinwsum", [H, 1], F32)
    r1s = sb1("r1s", [128, 32], F32)
    states_cm = sb1("states_cm", [NCHUNK, D], BF16)
    cend_t = sb1("cend_t", [H, 1 + NCHUNK], F32)
    mrc_t = sb1("mrc_t", [H, 1 + NCHUNK], F32)
    c1p_t = sb1("c1p_t", [H, NCHUNK], F32)
    ndu_t = sb1("ndu_t", [H, NCHUNK], F32)
    x0tm = sb1("x0tm", [128, 4 * D], F32)
    x0fmb = sb1("x0fmb", [128, 8 * TS1], BF16)
    xib = sb1("xib", [128, 8 * (TS1 + 3)], BF16)
    xstg = sb1("xstg", [128, 4 * 128], F32)
    xtok = sb1("xtok", [128, 4 * D], BF16)
    wtok = sb1("wtok", [128, 4 * D], BF16)
    ytok = sb1("ytok", [128, 2 * D], F32)
    ytmp = sb1("ytmp", [128, D], F32)
    yfmb = sb1("yfmb", [128, 8 * TS1], BF16)
    attfm = sb1("attfm", [128, TS1], F32)
    atttm = sb1("atttm", [128, 4 * D], F32)
    ln2t = sb1("ln2t", [128, D], F32)
    x1t = sb1("x1t", [128, 2 * D], F32)
    mr1b = sb1("mr1b", [128, TS1], F32)
    r1b = sb1("r1b", [128, TS1], F32)
    e1t = sb1("e1t", [128, 2 * TS1], F32)
    wfmA = sb1("wfmA", [H, TS1], F32)
    acs = sb1("acs", [H, TS1], F32)
    rmin = sb1("rmin", [H, TS1], F32)
    vv = sb1("vv", [H, TS1], F32)
    cvv = sb1("cvv", [H, TS1], F32)
    uu = sb1("uu", [H, TS1], F32)
    sdo = sb1("sdo", [H, TS1], F32)
    nrm = sb1("nrm", [H, TS1], F32)
    noff = sb1("noff", [H, TS1], F32)
    unq = sb1("unq", [H, TS1], F32)
    snq = sb1("snq", [H, TS1], F32)
    zer = sb1("zer", [H, CH], F32)
    sm8a = sb1("sm8a", [H, 8], F32)
    sm8b = sb1("sm8b", [H, 8], F32)
    ndf = sb1("ndf", [H, 8], F32)
    vqt = sb1("vqt", [128, 4 * H], F32)
    unqt = sb1("unqt", [128, 4 * H], F32)
    snqt = sb1("snqt", [128, 4 * H], F32)
    c1pT = sb1("c1pT", [NCHUNK, H], F32)
    nduT = sb1("nduT", [NCHUNK, H], F32)
    rmrow = sb1("rmrow", [1, 128], F32)
    dD = sb1("dD", [NCHUNK, 128], F32)
    dchb = sb1("dchb", [NCHUNK, 128], BF16)
    dchm = sb1("dchm", [NCHUNK, 128], BF16)
    prod = sb1("prod", [NCHUNK, 128], F32)
    ndcol = sb1("ndcol", [128, 1], F32)
    stsb = sb1("stsb", [8, D], BF16)
    stpsb = sb1("stpsb", [8, D], BF16)
    strow = sb1("strow", [128, 8], F32)
    mrT = sb1("mrT", [4, 128], F32)
    rT = sb1("rT", [4, 128], F32)
    sc = sb1("sc", [128, 24], F32)
    sc2 = sb1("sc2", [1, 8], F32)
    wstage = sb1("wstage", [128, 2 * 1024], F32)
    tsg1 = sb1("tsg1", [128, 1024], BF16)
    tsg2 = sb1("tsg2", [128, 1024], BF16)
    tout = sb1("tout", [128, 1024], BF16)
    absacc = sb1("absacc", [128, 2], F32)
    abscol = sb1("abscol", [128, 2], F32)
    invsc = sb1("invsc", [128, 2], F32)

    pA = ps1("pA", [128, 512], F32)
    pB = ps1("pB", [128, 512], F32)
    pC = ps1("pC", [128, 512], F32)
    pD = ps1("pD", [128, 512], F32)
    pW1 = ps1("pW1", [128, 512], F32)
    pW2 = ps1("pW2", [128, 512], F32)
    pSa = ps1("pSa", [8, 512], F32)
    pSb = ps1("pSb", [8, 512], F32)

    def mm(stage, out_ap, lhsT_ap, rhs_ap, reads, pres, start=True, stop=True,
           extra=()):
        return p.emit(stage, "tensor",
                      lambda t, o=out_ap, l=lhsT_ap, r=rhs_ap, s1=start, s2=stop:
                      t.matmul(o, l, r, start=s1, stop=s2, skip_group_check=True),
                      reads=reads, writes=[pres], extra_waits=extra)

    def tp(stage, out_ap, in_ap, reads, pres):
        k = in_ap.shape[0]
        return p.emit(stage, "tensor",
                      lambda t, o=out_ap, i=in_ap, k=k: t.transpose(o, i, iden_f[0:k, 0:k]),
                      reads=list(reads) + ["iden_f"], writes=[pres])

    def rsqrt_cols(stage, io_ap, rname):
        p.emit(stage, "scalar",
               lambda se: se.activation(out=io_ap, in_=io_ap, func=AF.Ln, bias=np.float32(EPS)),
               reads=[rname], writes=[rname])
        p.emit(stage, "scalar",
               lambda se: se.activation(out=io_ap, in_=io_ap, func=AF.Exp, scale=np.float32(-0.5)),
               reads=[rname], writes=[rname])

    PH = ST1
    for dst, src, res, dsn in ((convw_r[:], convw_h, "convw_r", "convw"),
                               (convb_c[:], convb_h, "convb_c", "convb"),
                               (wbase_c[:], wbase_h, "wbase_c", "wbase"),
                               (b1_c[:], b1_h, "b1_c", "b1"),
                               (b2_c[:], b2_h, "b2_c", "b2")):
        p.emit(PH, "sync", lambda e, d=dst, s=src: e.dma_start(out=d, in_=s),
               writes=[res], dma=True, dsem=dsn)
    for tt, rr, val in ((ones_f, "ones_f", 1.0), (onesr, "onesr", 1.0),
                        (ones_b, "ones_b", 1.0), (zer, "zer", 0.0),
                        (cend_t, "cend_t", 0.0), (mrc_t, "mrc_t", 0.0),
                        (c1p_t, "c1p_t", 0.0), (ndu_t, "ndu_t", 0.0),
                        (absacc, "absacc", 0.0), (indstrip, "indstrip", 0.0)):
        p.emit(PH, "vector", lambda v, t=tt, x=val: v.memset(t[:], x), writes=[rr])
    p.emit(PH, "vector", lambda v: v.memset(states_cm[:], 0.0), writes=["states_cm"])
    p.emit(PH, "vector", lambda v: v.memset(xib[:], 0.0),
           writes=[f"xi{m}" for m in range(8)] + [f"xihalo{m}" for m in range(8)])
    p.emit(PH, "gpsimd",
           lambda g: g.affine_select(out=iden_f[:], in_=ones_f[:, 0:1].to_broadcast((128, 128)),
                                     pattern=[[1, 128]], compare_op=ALU.is_equal,
                                     fill=0.0, base=0, channel_multiplier=-1),
           reads=["ones_f"], writes=["iden_f"])
    p.emit(PH, "gpsimd",
           lambda g: g.affine_select(out=triones[:], in_=ones_b[:, 0:1].to_broadcast((128, 128)),
                                     pattern=[[1, 128]], compare_op=ALU.is_ge,
                                     fill=0.0, base=0, channel_multiplier=-1),
           reads=["ones_b"], writes=["triones"])
    p.emit(PH, "vector", lambda v: v.memset(triones[0:64, 64:128], 0.0), writes=["triones"])
    for cp in range(4):
        p.emit(PH, "vector",
               lambda v, cp=cp: v.memset(indstrip[0:64, cp * 8 + 2 * cp:cp * 8 + 2 * cp + 1], 1.0),
               writes=["indstrip"])
        p.emit(PH, "vector",
               lambda v, cp=cp: v.memset(indstrip[64:128, cp * 8 + 2 * cp + 1:cp * 8 + 2 * cp + 2], 1.0),
               writes=["indstrip"])
        p.emit(PH, "gpsimd",
               lambda g, cp=cp: g.affine_select(
                   out=indloc[:, cp * 128:(cp + 1) * 128].rearrange("p (a b) -> p a b", a=2),
                   in_=ones_b[0:8, 0:1].to_broadcast((8, 2, 64)),
                   pattern=[[1, 2], [0, 64]], compare_op=ALU.is_equal,
                   fill=0.0, base=2 * cp, channel_multiplier=-1),
               reads=["ones_b"], writes=["indloc"])
    for j in range(4):
        p.emit(PH, "gpsimd",
               lambda g, j=j: g.affine_select(out=e4[:, j * 128:(j + 1) * 128],
                                              in_=ones_f[0:4, 0:1].to_broadcast((4, 128)),
                                              pattern=[[0, 128]], compare_op=ALU.is_equal,
                                              fill=0.0, base=-j, channel_multiplier=1),
               reads=["ones_f"], writes=["e4"])
    for ck in range(32):
        p.emit(PH, "vector",
               lambda v, ck=ck: v.tensor_tensor(out=convdg[:, ck * 128:(ck + 1) * 128],
                                                in0=iden_f[:],
                                                in1=convw_r[:, ck:ck + 1].to_broadcast((128, 128)),
                                                op=ALU.mult),
               reads=["iden_f", "convw_r"], writes=["convdg"])
    p.emit(PH, "gpsimd",
           lambda g: g.dma_start(out=winbf.rearrange("p (k e) -> p k e", k=8),
                                 in_=winT_h.rearrange("(k p) e -> p k e", p=128)),
           writes=["winbf"], dma=True, dsem="winbf")
    p.emit(PH, "gpsimd",
           lambda g: g.dma_start(out=winwbf.rearrange("p (k e) -> p k e", k=8),
                                 in_=winwT_h.rearrange("(k p) e -> p k e", p=128)),
           writes=["winwbf"], dma=True, dsem="winwbf")
    p.emit(PH, "gpsimd",
           lambda g: g.dma_start(out=wobf.rearrange("p (k e) -> p k e", k=8),
                                 in_=woT_h.rearrange("(k p) e -> p k e", p=128)),
           writes=["wobf"], dma=True, dsem="wobf")
    for m in range(8):
        for k in range(8):
            mm(PH, pC[:, m:m + 1], winbf[:, k * D + m * 128:k * D + (m + 1) * 128],
               ones_b[:], reads=["winbf", "ones_b"], pres="pC",
               start=(k == 0), stop=(k == 7))
    p.emit(PH, "vector",
           lambda v: v.tensor_scalar(out=nwinsum[:], in0=pC[:, 0:8], scalar1=-1.0,
                                     scalar2=None, op0=ALU.mult),
           reads=["pC"], writes=["nwinsum"])
    for k in range(8):
        mm(PH, pD[0:H, 0:1], winwbf[:, k * H:(k + 1) * H], ones_b[:],
           reads=["winwbf", "ones_b"], pres="pD", start=(k == 0), stop=(k == 7))
    p.emit(PH, "vector",
           lambda v: v.tensor_scalar(out=nwinwsum[:], in0=pD[0:H, 0:1], scalar1=-1.0,
                                     scalar2=None, op0=ALU.mult),
           reads=["pD"], writes=["nwinwsum"])

    tern_tiles = [(0, k) for k in range(32)] + [(1, k) for k in range(32)]

    def tern_abs(wi, k):
        wh = w1T_h if wi == 0 else w2T_h
        if wi == 0:
            src = wh[(k // 4) * 128:(k // 4 + 1) * 128, (k % 4) * 1024:(k % 4 + 1) * 1024]
        else:
            src = wh[k * 128:(k + 1) * 128, :]
        hb_ = k % 2
        p.emit(ST1, "sync",
               lambda e, s=src, hb_=hb_: e.dma_start(out=wstage[:, hb_ * 1024:(hb_ + 1) * 1024], in_=s),
               writes=[f"wstage{hb_}"], dma=True, dsem=f"wstage{hb_}")
        p.emit(ST1, "vector",
               lambda v, hb_=hb_: v.tensor_reduce(out=abscol[:, hb_:hb_ + 1],
                                                  in_=wstage[:, hb_ * 1024:(hb_ + 1) * 1024],
                                                  axis=AX.X, op=ALU.add,
                                                  apply_absolute_value=True),
               reads=[f"wstage{hb_}"], writes=[f"abscol{hb_}"])
        p.emit(ST1, "vector",
               lambda v, wi=wi, hb_=hb_: v.tensor_tensor(out=absacc[:, wi:wi + 1],
                                                         in0=absacc[:, wi:wi + 1],
                                                         in1=abscol[:, hb_:hb_ + 1], op=ALU.add),
               reads=[f"abscol{hb_}", "absacc"], writes=["absacc"])

    def tern_scale(wi):
        mm(ST1, pC[0:1, 0:1], absacc[:, wi:wi + 1], ones_f[:],
           reads=["absacc", "ones_f"], pres="pC")
        p.emit(ST1, "scalar",
               lambda se: se.activation(out=sc2[0:1, 0:1], in_=pC[0:1, 0:1],
                                        func=AF.Identity, scale=np.float32(1.0 / (4096.0 * 1024.0))),
               reads=["pC"], writes=["sc2"])
        p.emit(ST1, "vector",
               lambda v: v.tensor_scalar(out=sc2[0:1, 0:1], in0=sc2[0:1, 0:1],
                                         scalar1=1e-8, scalar2=None, op0=ALU.max),
               reads=["sc2"], writes=["sc2"])
        p.emit(ST1, "scalar",
               lambda se: se.activation(out=sc2[0:1, 1:2], in_=sc2[0:1, 0:1], func=AF.Ln),
               reads=["sc2"], writes=["sc2"])
        p.emit(ST1, "scalar",
               lambda se: se.activation(out=sc2[0:1, 2:3], in_=sc2[0:1, 1:2],
                                        func=AF.Exp, scale=np.float32(-1.0)),
               reads=["sc2"], writes=["sc2"])
        mm(ST1, pC[:, 0:1], onesr[:], sc2[0:1, 2:3], reads=["sc2", "onesr"], pres="pC")
        p.emit(ST1, "vector",
               lambda v, wi=wi: v.tensor_copy(invsc[:, wi:wi + 1], pC[:, 0:1]),
               reads=["pC"], writes=["invsc"])

    def tern_sign(wi, k):
        wh = w1T_h if wi == 0 else w2T_h
        wth = w1t_h if wi == 0 else w2t_h
        if wi == 0:
            sl = ((k // 4) * 128, (k % 4) * 1024)
        else:
            sl = (k * 128, 0)
        src = wh[sl[0]:sl[0] + 128, sl[1]:sl[1] + 1024]
        dst = wth[sl[0]:sl[0] + 128, sl[1]:sl[1] + 1024]
        hb_ = k % 2
        o = hb_ * 1024
        p.emit(ST1, "sync",
               lambda e, s=src, o=o: e.dma_start(out=wstage[:, o:o + 1024], in_=s),
               writes=[f"wstage{hb_}"], dma=True, dsem=f"wstage{hb_}")
        p.emit(ST1, "scalar",
               lambda se, o=o, wi=wi: se.activation(out=tsg1[:], in_=wstage[:, o:o + 1024],
                                                    func=AF.Sign,
                                                    scale=invsc[:, wi:wi + 1], bias=np.float32(-0.5)),
               reads=[f"wstage{hb_}", "invsc"], writes=["tsg1"])
        p.emit(ST1, "scalar",
               lambda se, o=o, wi=wi: se.activation(out=tsg2[:], in_=wstage[:, o:o + 1024],
                                                    func=AF.Sign,
                                                    scale=invsc[:, wi:wi + 1], bias=np.float32(0.5)),
               reads=[f"wstage{hb_}", "invsc"], writes=["tsg2"])
        p.emit(ST1, "vector",
               lambda v: v.tensor_tensor(out=tout[:], in0=tsg1[:], in1=tsg2[:], op=ALU.add),
               reads=["tsg1", "tsg2"], writes=["tout"])
        p.emit(ST1, "gpsimd",
               lambda g, d=dst: g.dma_start(out=d, in_=tout[:]),
               reads=["tout"], dma=True, dsem="ternst")

    def tern_step(s):
        if s in (0, 1):
            for k in range(16 * s, 16 * s + 16):
                tern_abs(0, k)
        if s == 2:
            tern_scale(0)
        if s in (2, 3):
            for k in range(16 * (s - 2), 16 * (s - 2) + 16):
                tern_abs(1, k)
        if 2 <= s <= 5:
            for k in range(8 * (s - 2), 8 * (s - 2) + 8):
                tern_sign(0, k)
        if s == 4:
            tern_scale(1)
        if 4 <= s <= 7:
            for k in range(8 * (s - 4), 8 * (s - 4) + 8):
                tern_sign(1, k)

    for s in range(NS1):
        t0 = s * TS1
        tern_step(s)

        p.emit(ST1, "sync",
               lambda e, t0=t0: e.dma_start(out=x0tm.rearrange("p (j d) -> p j d", j=4),
                                            in_=x0_h[t0:t0 + TS1, :].rearrange("(j p) d -> p j d", p=128)),
               writes=["x0tm"], dma=True, dsem="x0tm")
        p.emit(ST1, "gpsimd",
               lambda g, t0=t0: g.dma_start(out=x0fmb.rearrange("p (k t) -> p k t", k=8),
                                            in_=x0T_h[:, t0:t0 + TS1].rearrange("(k p) t -> p k t", p=128)),
               writes=["x0fmb"], dma=True, dsem="x0fmb")

        for j in range(4):
            p.emit(ST1, "vector",
                   lambda v, j=j: v.tensor_scalar(out=junkf[:, 0:D], in0=x0tm[:, j * D:(j + 1) * D],
                                                  scalar1=1.0, scalar2=0.0, op0=ALU.mult,
                                                  op1=ALU.add, accum_out=sc[:, j:j + 1]),
                   reads=["x0tm"], writes=["sc_s1"])
            p.emit(ST1, "vector",
                   lambda v, j=j: v.scalar_tensor_tensor(out=junkf[:, 0:D],
                                                         in0=x0tm[:, j * D:(j + 1) * D], scalar=1.0,
                                                         in1=x0tm[:, j * D:(j + 1) * D],
                                                         op0=ALU.mult, op1=ALU.mult,
                                                         accum_out=sc[:, 4 + j:5 + j]),
                   reads=["x0tm"], writes=["sc_q1"])
        p.emit(ST1, "vector",
               lambda v, s=s: v.tensor_scalar(out=m1s[:, 4 * s:4 * s + 4], in0=sc[:, 0:4],
                                              scalar1=1.0 / 1024.0, scalar2=None, op0=ALU.mult),
               reads=["sc_s1"], writes=["m1s"])
        p.emit(ST1, "vector",
               lambda v, s=s: v.scalar_tensor_tensor(out=r1s[:, 4 * s:4 * s + 4],
                                                     in0=m1s[:, 4 * s:4 * s + 4], scalar=-1.0,
                                                     in1=m1s[:, 4 * s:4 * s + 4],
                                                     op0=ALU.mult, op1=ALU.mult),
               reads=["m1s"], writes=["r1s"])
        p.emit(ST1, "vector",
               lambda v, s=s: v.scalar_tensor_tensor(out=r1s[:, 4 * s:4 * s + 4],
                                                     in0=sc[:, 4:8], scalar=1.0 / 1024.0,
                                                     in1=r1s[:, 4 * s:4 * s + 4],
                                                     op0=ALU.mult, op1=ALU.add),
               reads=["sc_q1", "r1s"], writes=["r1s"])
        rsqrt_cols(ST1, r1s[:, 4 * s:4 * s + 4], "r1s")
        p.emit(ST1, "vector",
               lambda v, s=s: v.tensor_tensor(out=strow[:, 0:4], in0=m1s[:, 4 * s:4 * s + 4],
                                              in1=r1s[:, 4 * s:4 * s + 4], op=ALU.mult),
               reads=["m1s", "r1s"], writes=["strow"])
        p.emit(ST1, "vector",
               lambda v, s=s: v.tensor_copy(strow[:, 4:8], r1s[:, 4 * s:4 * s + 4]),
               reads=["r1s", "strow"], writes=["strow"])
        tp(ST1, pC[0:4, 0:128], strow[:, 0:4], reads=["strow"], pres="pC")
        p.emit(ST1, "vector", lambda v: v.tensor_copy(mrT[:], pC[0:4, 0:128]),
               reads=["pC"], writes=["mrT"])
        tp(ST1, pD[0:4, 0:128], strow[:, 4:8], reads=["strow"], pres="pD")
        p.emit(ST1, "vector", lambda v: v.tensor_copy(rT[:], pD[0:4, 0:128]),
               reads=["pD"], writes=["rT"])
        for j in range(4):
            mm(ST1, pA[:, j * 128:(j + 1) * 128], e4[:, j * 128:(j + 1) * 128], mrT[:],
               reads=["e4", "mrT"], pres="pA", start=(j == 0), stop=(j == 3))
        for j in range(4):
            mm(ST1, pB[:, j * 128:(j + 1) * 128], e4[:, j * 128:(j + 1) * 128], rT[:],
               reads=["e4", "rT"], pres="pB", start=(j == 0), stop=(j == 3))
        p.emit(ST1, "scalar", lambda se: se.activation(out=mr1b[:], in_=pA[:], func=AF.Identity),
               reads=["pA"], writes=["mr1b"])
        p.emit(ST1, "scalar", lambda se: se.activation(out=r1b[:], in_=pB[:], func=AF.Identity),
               reads=["pB"], writes=["r1b"])

        if s == 0:
            if "d_mr1b" in dbg_h:
                p.emit(ST1, "gpsimd", lambda g: g.dma_start(out=dbg_h["d_mr1b"], in_=mr1b[:]),
                       reads=["mr1b"], dma=True, dsem="dbg")
            if "d_r1b" in dbg_h:
                p.emit(ST1, "gpsimd", lambda g: g.dma_start(out=dbg_h["d_r1b"], in_=r1b[:]),
                       reads=["r1b"], dma=True, dsem="dbg")
            if "d_x0fmb" in dbg_h:
                p.emit(ST1, "gpsimd", lambda g: g.dma_start(out=dbg_h["d_x0fmb"], in_=x0fmb[:]),
                       reads=["x0fmb"], dma=True, dsem="dbg")
        if "d_m1" in dbg_h and s == NS1 - 1:
            p.emit(ST1, "gpsimd", lambda g: g.dma_start(out=dbg_h["d_m1"], in_=m1s[:]),
                   reads=["m1s"], dma=True, dsem="dbg")
        if "d_r1" in dbg_h and s == NS1 - 1:
            p.emit(ST1, "gpsimd", lambda g: g.dma_start(out=dbg_h["d_r1"], in_=r1s[:]),
                   reads=["r1s"], dma=True, dsem="dbg")

        for k in range(8):
            mm(ST1, pC[0:H, :], winwbf[:, k * H:(k + 1) * H], x0fmb[:, k * TS1:(k + 1) * TS1],
               reads=["winwbf", "x0fmb"], pres="pC", start=(k == 0), stop=(k == 7))
        p.emit(ST1, "vector",
               lambda v: v.scalar_tensor_tensor(out=wfmA[:], in0=mr1b[0:H, :],
                                                scalar=nwinwsum[:, 0:1], op0=ALU.mult,
                                                in1=pC[0:H, :], op1=ALU.add),
               reads=["mr1b", "nwinwsum", "pC"], writes=["wfmA"])
        p.emit(ST1, "vector",
               lambda v: v.scalar_tensor_tensor(out=wfmA[:], in0=wfmA[:], scalar=wbase_c[:, 0:1],
                                                op0=ALU.mult, in1=r1b[0:H, :], op1=ALU.mult),
               reads=["wfmA", "wbase_c", "r1b"], writes=["wfmA"])
        if "d_A" in dbg_h:
            p.emit(ST1, "gpsimd",
                   lambda g, t0=t0: g.dma_start(out=dbg_h["d_A"][:, t0:t0 + TS1], in_=wfmA[:]),
                   reads=["wfmA"], dma=True, dsem="dbg")

        for c in range(8):
            p.emit(ST1, "vector",
                   lambda v, c=c: v.tensor_tensor_scan(out=acs[:, c * CH:(c + 1) * CH],
                                                       data0=wfmA[:, c * CH:(c + 1) * CH],
                                                       data1=zer[:], initial=0.0,
                                                       op0=ALU.add, op1=ALU.add),
                   reads=["wfmA", "zer"], writes=["acs"])
        for c in range(8):
            p.emit(ST1, "vector",
                   lambda v, c=c: v.tensor_tensor_scan(out=rmin[:, c * CH:(c + 1) * CH],
                                                       data0=acs[:, c * CH:(c + 1) * CH],
                                                       data1=zer[:], initial=1e30,
                                                       op0=ALU.min, op1=ALU.add),
                   reads=["acs", "zer"], writes=["rmin"])
        p.emit(ST1, "scalar", lambda se: se.activation(out=vv[:], in_=acs[:], func=AF.Exp, scale=np.float32(-1.0)),
               reads=["acs"], writes=["vv"])
        p.emit(ST1, "scalar", lambda se: se.activation(out=uu[:], in_=rmin[:], func=AF.Exp),
               reads=["rmin"], writes=["uu"])
        for c in range(8):
            p.emit(ST1, "vector",
                   lambda v, c=c: v.tensor_tensor_scan(out=cvv[:, c * CH:(c + 1) * CH],
                                                       data0=vv[:, c * CH:(c + 1) * CH],
                                                       data1=zer[:], initial=0.0,
                                                       op0=ALU.add, op1=ALU.add),
                   reads=["vv", "zer"], writes=["cvv"])
        for c in range(8):
            p.emit(ST1, "vector",
                   lambda v, c=c: v.tensor_reduce(out=sm8a[:, c:c + 1],
                                                  in_=acs[:, c * CH:(c + 1) * CH],
                                                  axis=AX.X, op=ALU.max, negate=True),
                   reads=["acs"], writes=["sm8a"])
        for c in range(8):
            p.emit(ST1, "scalar",
                   lambda se, c=c: se.activation(out=sdo[:, c * CH:(c + 1) * CH],
                                                 in_=acs[:, c * CH:(c + 1) * CH],
                                                 func=AF.Exp, bias=sm8a[:, c:c + 1]),
                   reads=["acs", "sm8a"], writes=["sdo"])
        p.emit(ST1, "vector",
               lambda v: v.tensor_copy(sm8b[:], acs[:, CH - 1::CH]),
               reads=["acs"], writes=["sm8b"])
        p.emit(ST1, "vector",
               lambda v, s=s: v.tensor_tensor_scan(out=cend_t[:, 1 + 8 * s:9 + 8 * s],
                                                   data0=sm8b[:], data1=zer[:, 0:8],
                                                   initial=cend_t[:, 8 * s:8 * s + 1],
                                                   op0=ALU.add, op1=ALU.add),
               reads=["sm8b", "zer", "cend_t"], writes=["cend_t"])
        p.emit(ST1, "vector",
               lambda v, s=s: v.tensor_tensor_scan(out=mrc_t[:, 1 + 8 * s:9 + 8 * s],
                                                   data0=cend_t[:, 8 * s:8 * s + 8],
                                                   data1=zer[:, 0:8],
                                                   initial=mrc_t[:, 8 * s:8 * s + 1],
                                                   op0=ALU.min, op1=ALU.add),
               reads=["cend_t", "mrc_t", "zer"], writes=["mrc_t"])
        p.emit(ST1, "vector",
               lambda v: v.tensor_copy(sm8b[:], rmin[:, CH - 1::CH]),
               reads=["rmin", "sm8b"], writes=["sm8b"])
        p.emit(ST1, "vector",
               lambda v, s=s: v.tensor_tensor(out=c1p_t[:, 8 * s:8 * s + 8],
                                              in0=cend_t[:, 1 + 8 * s:9 + 8 * s],
                                              in1=sm8b[:], op=ALU.subtract),
               reads=["cend_t", "sm8b"], writes=["c1p_t"])
        p.emit(ST1, "vector",
               lambda v, s=s: v.tensor_copy(ndu_t[:, 8 * s:8 * s + 8], cvv[:, CH - 1::CH]),
               reads=["cvv"], writes=["ndu_t"])
        p.emit(ST1, "gpsimd",
               lambda g, s=s: g.dma_start(out=rm_h.rearrange("(h z) -> h z", h=H),
                                          in_=mrc_t[:, 1 + 8 * s:9 + 8 * s]),
               reads=["mrc_t"], writes=["rm_hbm"], dma=True, dsem="rmw")
        p.emit(ST1, "gpsimd",
               lambda g: g.dma_start(out=rmrow[:], in_=rm_h.rearrange("(a b) -> a b", a=1)),
               reads=["rm_hbm"], writes=["rmrow"], dma=True, dsem="rmr")

        for m in range(8):
            pm, pmr = (pA, "pA") if m % 2 == 0 else (pB, "pB")
            for k in range(8):
                mm(ST1, pm[:], winbf[:, k * D + m * 128:k * D + (m + 1) * 128],
                   x0fmb[:, k * TS1:(k + 1) * TS1],
                   reads=["winbf", "x0fmb"], pres=pmr, start=(k == 0), stop=(k == 7))
            eb = m % 2
            p.emit(ST1, "vector",
                   lambda v, m=m, pm=pm, eb=eb: v.scalar_tensor_tensor(
                       out=e1t[:, eb * TS1:(eb + 1) * TS1], in0=mr1b[:],
                       scalar=nwinsum[:, m:m + 1], op0=ALU.mult, in1=pm[:], op1=ALU.add),
                   reads=["mr1b", "nwinsum", pmr], writes=[f"e1t{eb}"])
            p.emit(ST1, "gpsimd",
                   lambda g, m=m, eb=eb: g.tensor_tensor(
                       out=xib[:, m * (TS1 + 3) + 3:(m + 1) * (TS1 + 3)],
                       in0=e1t[:, eb * TS1:(eb + 1) * TS1], in1=r1b[:], op=ALU.mult),
                   reads=[f"e1t{eb}", "r1b"], writes=[f"xi{m}"])
        if "d_xi" in dbg_h:
            for m in range(8):
                p.emit(ST1, "gpsimd",
                       lambda g, m=m, t0=t0: g.dma_start(
                           out=dbg_h["d_xi"][m * 128:(m + 1) * 128, t0:t0 + TS1],
                           in_=xib[:, m * (TS1 + 3) + 3:(m + 1) * (TS1 + 3)]),
                       reads=[f"xi{m}"], dma=True, dsem="dbg")

        for c in range(8):
            pm, pmr = (pC, "pC") if c % 2 == 0 else (pD, "pD")
            for k in range(4):
                mm(ST1, pm[:], convdg[:, (c * 4 + k) * 128:(c * 4 + k + 1) * 128],
                   xib[:, c * (TS1 + 3) + k:c * (TS1 + 3) + k + TS1],
                   reads=["convdg", f"xi{c}", f"xihalo{c}"], pres=pmr,
                   start=(k == 0), stop=(k == 3))
            for j in range(4):
                r = j % 4
                p.emit(ST1, "scalar",
                       lambda se, j=j, pm=pm, r=r, c=c: se.activation(
                           out=xstg[:, r * 128:(r + 1) * 128],
                           in_=pm[:, j * 128:(j + 1) * 128], func=AF.Identity,
                           bias=convb_c[:, c:c + 1]),
                       reads=[pmr, "convb_c"], writes=[f"xstg{r}"])
                pj, pjr = (pA, "pA") if (c * 4 + j) % 2 == 0 else (pB, "pB")
                tp(ST1, pj[:, 0:128], xstg[:, r * 128:(r + 1) * 128],
                   reads=[f"xstg{r}"], pres=pjr)
                p.emit(ST1, "scalar",
                       lambda se, c=c, j=j, pj=pj: se.activation(
                           out=xtok[:, j * D + c * 128:j * D + (c + 1) * 128],
                           in_=pj[:, 0:128], func=AF.Identity),
                       reads=[pjr], writes=["xtok"])
        if "d_X" in dbg_h:
            p.emit(ST1, "gpsimd",
                   lambda g, t0=t0: g.dma_start(
                       out=dbg_h["d_X"][t0:t0 + TS1, :].rearrange("(j p) d -> p j d", p=128),
                       in_=xtok.rearrange("p (j d) -> p j d", j=4)),
                   reads=["xtok"], dma=True, dsem="dbg")
        p.emit(ST1, "vector",
               lambda v: v.tensor_copy(xib.rearrange("p (c t) -> p c t", c=8)[:, :, 0:3],
                                       xib.rearrange("p (c t) -> p c t", c=8)[:, :, TS1:TS1 + 3]),
               reads=[f"xi{m}" for m in range(8)],
               writes=[f"xihalo{m}" for m in range(8)])

        tp(ST1, pC[0:64, 0:H], c1p_t[:, 0:NCHUNK], reads=["c1p_t"], pres="pC")
        p.emit(ST1, "vector", lambda v: v.tensor_copy(c1pT[:], pC[0:64, 0:H]),
               reads=["pC"], writes=["c1pT"])
        tp(ST1, pD[0:64, 0:H], ndu_t[:, 0:NCHUNK], reads=["ndu_t"], pres="pD")
        p.emit(ST1, "vector", lambda v: v.tensor_copy(nduT[:], pD[0:64, 0:H]),
               reads=["pD"], writes=["nduT"])
        mm(ST1, pW1[0:64, 0:128], onesr[0:1, 0:64], rmrow[:],
           reads=["onesr", "rmrow"], pres="pW1")
        p.emit(ST1, "vector",
               lambda v: v.tensor_tensor(out=dD.rearrange("p (h z) -> p h z", h=H),
                                         in0=pW1[0:64, 0:128].rearrange("p (h z) -> p h z", h=H),
                                         in1=c1pT[:, :, None].to_broadcast((64, H, 8)),
                                         op=ALU.subtract),
               reads=["pW1", "c1pT"], writes=["dD"])
        p.emit(ST1, "scalar",
               lambda se: se.activation(out=dchb[:], in_=dD[:], func=AF.Exp),
               reads=["dD"], writes=["dchb"])
        p.emit(ST1, "gpsimd",
               lambda g, s=s: g.affine_select(out=dchm.rearrange("p (h z) -> p h z", h=H),
                                              in_=dchb.rearrange("p (h z) -> p h z", h=H),
                                              pattern=[[0, H], [1, 8]], compare_op=ALU.is_ge,
                                              fill=0.0, base=8 * s - 1, channel_multiplier=-1),
               reads=["dchb"], writes=["dchm"])
        p.emit(ST1, "vector",
               lambda v: v.tensor_tensor(out=prod.rearrange("p (h z) -> p h z", h=H),
                                         in0=dchm.rearrange("p (h z) -> p h z", h=H),
                                         in1=nduT[:, :, None].to_broadcast((64, H, 8)),
                                         op=ALU.mult),
               reads=["dchm", "nduT"], writes=["prod"])
        mm(ST1, pC[:, 0:1], prod[:], ones_f[0:64, 0:1], reads=["prod", "ones_f"], pres="pC")
        p.emit(ST1, "vector", lambda v: v.tensor_copy(ndcol[:], pC[:, 0:1]),
               reads=["pC"], writes=["ndcol"])
        p.emit(ST1, "gpsimd",
               lambda g: g.dma_start(out=nd_h.rearrange("(a b) -> a b", b=1), in_=ndcol[:]),
               reads=["ndcol"], writes=["nd_hbm"], dma=True, dsem="ndw")
        p.emit(ST1, "gpsimd",
               lambda g: g.dma_start(out=ndf[:], in_=nd_h.rearrange("(h z) -> h z", h=H)),
               reads=["nd_hbm"], writes=["ndf"], dma=True, dsem="ndr")
        p.emit(ST1, "vector",
               lambda v: v.tensor_tensor(out=nrm[:], in0=uu[:], in1=cvv[:], op=ALU.mult),
               reads=["uu", "cvv"], writes=["nrm"])
        p.emit(ST1, "vector",
               lambda v: v.tensor_tensor(out=noff.rearrange("p (c l) -> p c l", c=8),
                                         in0=sdo.rearrange("p (c l) -> p c l", c=8),
                                         in1=ndf[:, :, None].to_broadcast((H, 8, CH)),
                                         op=ALU.mult),
               reads=["sdo", "ndf"], writes=["noff"])
        p.emit(ST1, "vector",
               lambda v: v.tensor_tensor(out=nrm[:], in0=nrm[:], in1=noff[:], op=ALU.add),
               reads=["nrm", "noff"], writes=["nrm"])
        p.emit(ST1, "scalar", lambda se: se.activation(out=nrm[:], in_=nrm[:], func=AF.Ln),
               reads=["nrm"], writes=["nrm"])
        p.emit(ST1, "scalar", lambda se: se.activation(out=nrm[:], in_=nrm[:], func=AF.Exp, scale=np.float32(-1.0)),
               reads=["nrm"], writes=["nrm"])
        p.emit(ST1, "vector",
               lambda v: v.tensor_tensor(out=unq[:], in0=uu[:], in1=nrm[:], op=ALU.mult),
               reads=["uu", "nrm"], writes=["unq"])
        p.emit(ST1, "vector",
               lambda v: v.tensor_tensor(out=snq[:], in0=sdo[:], in1=nrm[:], op=ALU.mult),
               reads=["sdo", "nrm"], writes=["snq"])
        if "d_un" in dbg_h:
            p.emit(ST1, "gpsimd", lambda g, t0=t0: g.dma_start(out=dbg_h["d_un"][:, t0:t0 + TS1], in_=unq[:]),
                   reads=["unq"], dma=True, dsem="dbg")
            p.emit(ST1, "gpsimd", lambda g, t0=t0: g.dma_start(out=dbg_h["d_sn"][:, t0:t0 + TS1], in_=snq[:]),
                   reads=["snq"], dma=True, dsem="dbg")
        for nm, tt, rr in (("d_acs", acs, "acs"), ("d_vv", vv, "vv"), ("d_cvv", cvv, "cvv"),
                           ("d_sdo", sdo, "sdo"), ("d_nrm", nrm, "nrm")):
            if nm in dbg_h:
                p.emit(ST1, "gpsimd",
                       lambda g, nm=nm, tt=tt, t0=t0: g.dma_start(out=dbg_h[nm][:, t0:t0 + TS1], in_=tt[:]),
                       reads=[rr], dma=True, dsem="dbg")
        for qsrc, qdst, rs, ws in ((vv, vqt, "vv", "vqt"), (unq, unqt, "unq", "unqt"),
                                   (snq, snqt, "snq", "snqt")):
            for j in range(4):
                pj, pjr = (pC, "pC") if j % 2 == 0 else (pD, "pD")
                tp(ST1, pj[0:128, 0:H], qsrc[:, j * 128:(j + 1) * 128], reads=[rs], pres=pjr)
                p.emit(ST1, "vector",
                       lambda v, j=j, qd=qdst, pj=pj: v.tensor_copy(qd[:, j * H:(j + 1) * H],
                                                                    pj[0:128, 0:H]),
                       reads=[pjr], writes=[ws])

        for j in range(4):
            p.emit(ST1, "vector",
                   lambda v, j=j: v.tensor_tensor(
                       out=wtok[:, j * D:(j + 1) * D].rearrange("p (h n) -> p h n", h=H),
                       in0=xtok[:, j * D:(j + 1) * D].rearrange("p (h n) -> p h n", h=H),
                       in1=vqt[:, j * H:(j + 1) * H, None].to_broadcast((128, H, 64)),
                       op=ALU.mult),
                   reads=["xtok", "vqt"], writes=["wtok"])
        if "d_wtok" in dbg_h:
            p.emit(ST1, "gpsimd",
                   lambda g, t0=t0: g.dma_start(
                       out=dbg_h["d_wtok"][t0:t0 + TS1, :].rearrange("(j p) d -> p j d", p=128),
                       in_=wtok.rearrange("p (j d) -> p j d", j=4)),
                   reads=["wtok"], dma=True, dsem="dbg")
        for j in range(4):
            mm(ST1, pSa[:], indstrip[:, j * 8:(j + 1) * 8], wtok[:, j * D:j * D + 512],
               reads=["indstrip", "wtok"], pres="pSa", start=(j == 0), stop=(j == 3))
            mm(ST1, pSb[:], indstrip[:, j * 8:(j + 1) * 8], wtok[:, j * D + 512:(j + 1) * D],
               reads=["indstrip", "wtok"], pres="pSb", start=(j == 0), stop=(j == 3))
        p.emit(ST1, "vector", lambda v: v.tensor_copy(stsb[:, 0:512], pSa[:]),
               reads=["pSa"], writes=["stsb"])
        p.emit(ST1, "vector", lambda v: v.tensor_copy(stsb[:, 512:1024], pSb[:]),
               reads=["pSb"], writes=["stsb"])
        p.emit(ST1, "gpsimd",
               lambda g, s=s: g.dma_start(out=states_cm[8 * s:8 * s + 8, :], in_=stsb[:]),
               reads=["stsb"], writes=["states_cm"], dma=True, dsem="stc")
        if "d_st" in dbg_h:
            p.emit(ST1, "gpsimd",
                   lambda g, s=s: g.dma_start(out=dbg_h["d_st"][8 * s:8 * s + 8, :], in_=stsb[:]),
                   reads=["stsb"], dma=True, dsem="dbg")
        for h in range(H):
            pj, pjr = (pSa, "pSa") if h < 8 else (pSb, "pSb")
            mm(ST1, pj[:, (h % 8) * 64:(h % 8) * 64 + 64], dchm[:, h * 8:(h + 1) * 8],
               states_cm[:, h * 64:(h + 1) * 64],
               reads=["dchm", "states_cm"], pres=pjr,
               start=(h % 8 == 0), stop=(h % 8 == 7))
        p.emit(ST1, "vector", lambda v: v.tensor_copy(stpsb[:, 0:512], pSa[:]),
               reads=["pSa"], writes=["stpsb"])
        p.emit(ST1, "vector", lambda v: v.tensor_copy(stpsb[:, 512:1024], pSb[:]),
               reads=["pSb"], writes=["stpsb"])
        if "d_stp" in dbg_h:
            p.emit(ST1, "gpsimd",
                   lambda g, s=s: g.dma_start(out=dbg_h["d_stp"][8 * s:8 * s + 8, :], in_=stpsb[:]),
                   reads=["stpsb"], dma=True, dsem="dbg")
        for j in range(4):
            yo = (j % 2) * D
            yres = f"ytok{j % 2}"
            pc1, rc1, pc2, rc2 = (pW1, "pW1", pW2, "pW2") if j % 2 == 0 else (pC, "pC", pD, "pD")
            mm(ST1, pc1[:], triones[:], wtok[:, j * D:j * D + 512],
               reads=["triones", "wtok"], pres=rc1)
            mm(ST1, pc2[:], triones[:], wtok[:, j * D + 512:(j + 1) * D],
               reads=["triones", "wtok"], pres=rc2)
            for half, pc, rcn in ((0, pc1, rc1), (1, pc2, rc2)):
                p.emit(ST1, "vector",
                       lambda v, j=j, half=half, pc=pc, yo=yo: v.tensor_tensor(
                           out=ytok[:, yo + half * 512:yo + (half + 1) * 512].rearrange("p (h n) -> p h n", h=8),
                           in0=pc[:].rearrange("p (h n) -> p h n", h=8),
                           in1=unqt[:, j * H + half * 8:j * H + (half + 1) * 8, None].to_broadcast((128, 8, 64)),
                           op=ALU.mult),
                       reads=[rcn, "unqt"], writes=[yres])
            mm(ST1, pA[:], indloc[:, j * 128:(j + 1) * 128], stpsb[:, 0:512],
               reads=["indloc", "stpsb"], pres="pA")
            mm(ST1, pB[:], indloc[:, j * 128:(j + 1) * 128], stpsb[:, 512:1024],
               reads=["indloc", "stpsb"], pres="pB")
            for half, pc, rcn in ((0, pA, "pA"), (1, pB, "pB")):
                p.emit(ST1, "vector",
                       lambda v, j=j, half=half, pc=pc: v.tensor_tensor(
                           out=ytmp[:, half * 512:(half + 1) * 512].rearrange("p (h n) -> p h n", h=8),
                           in0=pc[:].rearrange("p (h n) -> p h n", h=8),
                           in1=snqt[:, j * H + half * 8:j * H + (half + 1) * 8, None].to_broadcast((128, 8, 64)),
                           op=ALU.mult),
                       reads=[rcn, "snqt"], writes=["ytmp"])
            p.emit(ST1, "gpsimd",
                   lambda g, yo=yo: g.tensor_tensor(out=ytok[:, yo:yo + D],
                                                    in0=ytok[:, yo:yo + D],
                                                    in1=ytmp[:], op=ALU.add),
                   reads=["ytmp", yres], writes=[yres])
            if "d_Y" in dbg_h:
                p.emit(ST1, "gpsimd",
                       lambda g, j=j, t0=t0, yo=yo: g.dma_start(
                           out=dbg_h["d_Y"][t0 + j * 128:t0 + (j + 1) * 128, :],
                           in_=ytok[:, yo:yo + D]),
                       reads=[yres], dma=True, dsem="dbg")
            for c in range(8):
                pj, pjr = (pW1, "pW1") if (j * 8 + c) % 2 == 0 else (pW2, "pW2")
                tp(ST1, pj[:, 0:128], ytok[:, yo + c * 128:yo + (c + 1) * 128],
                   reads=[yres], pres=pjr)
                p.emit(ST1, "scalar",
                       lambda se, j=j, c=c, pj=pj: se.activation(
                           out=yfmb[:, c * TS1 + j * 128:c * TS1 + (j + 1) * 128],
                           in_=pj[:, 0:128], func=AF.Identity),
                       reads=[pjr], writes=["yfmb"])

        for m in range(8):
            pm, pmr = (pA, "pA") if m % 2 == 0 else (pB, "pB")
            for k in range(8):
                mm(ST1, pm[:], wobf[:, k * D + m * 128:k * D + (m + 1) * 128],
                   yfmb[:, k * TS1:(k + 1) * TS1],
                   reads=["wobf", "yfmb"], pres=pmr, start=(k == 0), stop=(k == 7))
            p.emit(ST1, "scalar",
                   lambda se, pm=pm: se.activation(out=attfm[:], in_=pm[:], func=AF.Identity),
                   reads=[pmr], writes=["attfm"])
            for j in range(4):
                pj, pjr = (pC, "pC") if j % 2 == 0 else (pD, "pD")
                tp(ST1, pj[:, 0:128], attfm[:, j * 128:(j + 1) * 128],
                   reads=["attfm"], pres=pjr)
                p.emit(ST1, "vector",
                       lambda v, m=m, j=j, pj=pj: v.tensor_scalar(
                           out=atttm[:, j * D + m * 128:j * D + (m + 1) * 128],
                           in0=pj[:, 0:128], scalar1=1.0, scalar2=0.0,
                           op0=ALU.mult, op1=ALU.add, accum_out=sc[:, 8:9]),
                       reads=[pjr], writes=["atttm", "sc_p2"])
                if m == 0:
                    p.emit(ST1, "vector",
                           lambda v, j=j: v.tensor_copy(sc[:, 9 + j:10 + j], sc[:, 8:9]),
                           reads=["sc_p2"], writes=["sc_s2"])
                else:
                    p.emit(ST1, "vector",
                           lambda v, j=j: v.tensor_tensor(out=sc[:, 9 + j:10 + j],
                                                          in0=sc[:, 9 + j:10 + j],
                                                          in1=sc[:, 8:9], op=ALU.add),
                           reads=["sc_p2", "sc_s2"], writes=["sc_s2"])
        if "d_attn" in dbg_h:
            p.emit(ST1, "gpsimd",
                   lambda g, t0=t0: g.dma_start(
                       out=dbg_h["d_attn"][t0:t0 + TS1, :].rearrange("(j p) d -> p j d", p=128),
                       in_=atttm.rearrange("p (j d) -> p j d", j=4)),
                   reads=["atttm"], dma=True, dsem="dbg")
        for j in range(4):
            p.emit(ST1, "vector",
                   lambda v, j=j: v.scalar_tensor_tensor(out=junkf[:, 0:D],
                                                         in0=atttm[:, j * D:(j + 1) * D], scalar=1.0,
                                                         in1=atttm[:, j * D:(j + 1) * D],
                                                         op0=ALU.mult, op1=ALU.mult,
                                                         accum_out=sc[:, 13:14]),
                   reads=["atttm"], writes=["sc_q2"])
            p.emit(ST1, "vector",
                   lambda v, j=j: v.tensor_scalar(out=sc[:, 14:15], in0=sc[:, 9 + j:10 + j],
                                                  scalar1=1.0 / 1024.0, scalar2=None, op0=ALU.mult),
                   reads=["sc_s2"], writes=["sc_m2"])
            p.emit(ST1, "vector",
                   lambda v: v.scalar_tensor_tensor(out=sc[:, 15:16], in0=sc[:, 14:15], scalar=-1.0,
                                                    in1=sc[:, 14:15], op0=ALU.mult, op1=ALU.mult),
                   reads=["sc_m2"], writes=["sc_r2"])
            p.emit(ST1, "vector",
                   lambda v: v.scalar_tensor_tensor(out=sc[:, 15:16], in0=sc[:, 13:14],
                                                    scalar=1.0 / 1024.0, in1=sc[:, 15:16],
                                                    op0=ALU.mult, op1=ALU.add),
                   reads=["sc_q2", "sc_r2"], writes=["sc_r2"])
            rsqrt_cols(ST1, sc[:, 15:16], "sc_r2")
            p.emit(ST1, "vector",
                   lambda v: v.scalar_tensor_tensor(out=sc[:, 16:17], in0=sc[:, 14:15], scalar=-1.0,
                                                    in1=sc[:, 15:16], op0=ALU.mult, op1=ALU.mult),
                   reads=["sc_m2", "sc_r2"], writes=["sc_nm2"])
            p.emit(ST1, "scalar",
                   lambda se, j=j: se.activation(out=ln2t[:], in_=atttm[:, j * D:(j + 1) * D],
                                                 func=AF.Identity, scale=sc[:, 15:16],
                                                 bias=sc[:, 16:17]),
                   reads=["atttm", "sc_r2", "sc_nm2"], writes=["ln2t"])
            xb = j % 2
            p.emit(ST1, "gpsimd",
                   lambda g, j=j, xb=xb: g.tensor_tensor(out=x1t[:, xb * D:(xb + 1) * D],
                                                         in0=x0tm[:, j * D:(j + 1) * D],
                                                         in1=ln2t[:], op=ALU.add),
                   reads=["x0tm", "ln2t"], writes=[f"x1t{xb}"])
            p.emit(ST1, "gpsimd",
                   lambda g, j=j, xb=xb, t0=t0: g.dma_start(
                       out=x1_h[t0 + j * 128:t0 + (j + 1) * 128, :],
                       in_=x1t[:, xb * D:(xb + 1) * D]),
                   reads=[f"x1t{xb}"], dma=True, dsem=f"x1st{xb}")
            if "d_x1" in dbg_h:
                p.emit(ST1, "gpsimd",
                       lambda g, j=j, xb=xb, t0=t0: g.dma_start(
                           out=dbg_h["d_x1"][t0 + j * 128:t0 + (j + 1) * 128, :],
                           in_=x1t[:, xb * D:(xb + 1) * D]),
                       reads=[f"x1t{xb}"], dma=True, dsem="dbg")

    if "d_w1t" in dbg_h:
        p.emit(ST1, "sync", lambda e: e.dma_start(out=junkb[:], in_=w1t_h[0:128, 0:1024]),
               writes=["junkb_dbg"], dma=True, dsem="dbg")
        p.emit(ST1, "gpsimd", lambda g: g.dma_start(out=dbg_h["d_w1t"][:, 0:1024], in_=junkb[:]),
               reads=["junkb_dbg"], dma=True, dsem="dbg")
        p.emit(ST1, "sync", lambda e: e.dma_start(out=junkb[:], in_=w1t_h[0:128, 1024:2048]),
               writes=["junkb_dbg"], dma=True, dsem="dbg")
        p.emit(ST1, "gpsimd", lambda g: g.dma_start(out=dbg_h["d_w1t"][:, 1024:2048], in_=junkb[:]),
               reads=["junkb_dbg"], dma=True, dsem="dbg")

    run_stage(ST1)
    s1stack.close()

    w1tb = sb2("w1tb", [128, 8 * FF], BF16)
    w2tb = sb2("w2tb", [128, 32 * D], BF16)
    x1l = sb2("x1l", [128, 2 * D], F32)
    x1lnf = sb2("x1lnf", [128, 2 * D], F32)
    x1fmb = sb2("x1fmb", [128, 8 * TS2], BF16)
    hbuf = sb2("hbuf", [128, 2 * TS2], BF16)
    ffmf = sb2("ffmf", [128, TS2], F32)
    ffntm = sb2("ffntm", [128, 2 * D], F32)
    ln4t = sb2("ln4t", [128, D], F32)
    x2t = sb2("x2t", [128, 2 * D], F32)
    st4 = sb2("st4", [128, 16], F32)
    pF = [ps2(f"pF{i}", [128, 512], F32) for i in range(4)]
    pG1 = ps2("pG1", [128, 512], F32)
    pG2 = ps2("pG2", [128, 512], F32)
    pT1 = ps2("pT1", [128, 512], F32)
    pT2 = ps2("pT2", [128, 512], F32)

    barrier = p.snapshot()
    for en in ENGINES:
        if en in ("vector", "gpsimd"):
            p.emit(ST2, en, lambda e: e.engine_nop(), extra_waits=barrier)
        else:
            p.emit(ST2, en, lambda e: e.nop(), extra_waits=barrier)

    p.emit(ST2, "sync", lambda e: e.dma_start(out=w1tb.rearrange("p (k f) -> p k f", k=8),
                                              in_=w1t_h.rearrange("(k p) f -> p k f", p=128)),
           writes=["w1tb"], dma=True, dsem="w1tb")
    p.emit(ST2, "sync", lambda e: e.dma_start(out=w2tb.rearrange("p (k d) -> p k d", k=32),
                                              in_=w2t_h.rearrange("(k p) d -> p k d", p=128)),
           writes=["w2tb"], dma=True, dsem="w2tb")

    for s in range(NS2):
        t0 = s * TS2
        p.emit(ST2, "sync",
               lambda e, t0=t0: e.dma_start(out=x1l.rearrange("p (j d) -> p j d", j=2),
                                            in_=x1_h[t0:t0 + TS2, :].rearrange("(j p) d -> p j d", p=128)),
               writes=["x1l"], dma=True, dsem="x1l")
        for j in range(2):
            jj = s * 2 + j
            p.emit(ST2, "vector",
                   lambda v, j=j: v.scalar_tensor_tensor(out=junkf[:, 0:D],
                                                         in0=x1l[:, j * D:(j + 1) * D], scalar=1.0,
                                                         in1=x1l[:, j * D:(j + 1) * D],
                                                         op0=ALU.mult, op1=ALU.mult,
                                                         accum_out=st4[:, 0:1]),
                   reads=["x1l"], writes=["st_q3"])
            p.emit(ST2, "vector",
                   lambda v, jj=jj: v.scalar_tensor_tensor(out=st4[:, 1:2],
                                                           in0=m1s[:, jj:jj + 1], scalar=-1.0,
                                                           in1=m1s[:, jj:jj + 1],
                                                           op0=ALU.mult, op1=ALU.mult),
                   reads=["m1s"], writes=["st_v3"])
            p.emit(ST2, "vector",
                   lambda v: v.scalar_tensor_tensor(out=st4[:, 2:3], in0=st4[:, 0:1],
                                                    scalar=1.0 / 1024.0, in1=st4[:, 1:2],
                                                    op0=ALU.mult, op1=ALU.add),
                   reads=["st_q3", "st_v3"], writes=["st_r3"])
            rsqrt_cols(ST2, st4[:, 2:3], "st_r3")
            p.emit(ST2, "vector",
                   lambda v, jj=jj: v.scalar_tensor_tensor(out=st4[:, 3:4],
                                                           in0=m1s[:, jj:jj + 1], scalar=-1.0,
                                                           in1=st4[:, 2:3],
                                                           op0=ALU.mult, op1=ALU.mult),
                   reads=["m1s", "st_r3"], writes=["st_nm3"])
            p.emit(ST2, "scalar",
                   lambda se, j=j: se.activation(out=x1lnf[:, j * D:(j + 1) * D],
                                                 in_=x1l[:, j * D:(j + 1) * D],
                                                 func=AF.Identity, scale=st4[:, 2:3],
                                                 bias=st4[:, 3:4]),
                   reads=["x1l", "st_r3", "st_nm3"], writes=["x1lnf"])
        for j in range(2):
            for c in range(8):
                pj, pjr = (pT1, "pT1") if (j * 8 + c) % 2 == 0 else (pT2, "pT2")
                tp(ST2, pj[:, 0:128], x1lnf[:, j * D + c * 128:j * D + (c + 1) * 128],
                   reads=["x1lnf"], pres=pjr)
                p.emit(ST2, "scalar",
                       lambda se, j=j, c=c, pj=pj: se.activation(
                           out=x1fmb[:, c * TS2 + j * 128:c * TS2 + (j + 1) * 128],
                           in_=pj[:, 0:128], func=AF.Identity),
                       reads=[pjr], writes=["x1fmb"])
        if "d_x1ln" in dbg_h:
            p.emit(ST2, "gpsimd",
                   lambda g, t0=t0: g.dma_start(
                       out=dbg_h["d_x1ln"][:, t0:t0 + TS2].rearrange("(c p) t -> p c t", p=128),
                       in_=x1fmb.rearrange("p (c t) -> p c t", c=8)),
                   reads=["x1fmb"], dma=True, dsem="dbg")
        for f in range(32):
            pg, pgr = (pG1, "pG1") if f % 2 == 0 else (pG2, "pG2")
            for k in range(8):
                mm(ST2, pg[:, 0:TS2], w1tb[:, k * FF + f * 128:k * FF + (f + 1) * 128],
                   x1fmb[:, k * TS2:(k + 1) * TS2],
                   reads=["w1tb", "x1fmb"], pres=pgr, start=(k == 0), stop=(k == 7))
            fb = f % 2
            p.emit(ST2, "scalar",
                   lambda se, f=f, pg=pg, fb=fb: se.activation(
                       out=hbuf[:, fb * TS2:(fb + 1) * TS2], in_=pg[:, 0:TS2],
                       func=AF.Gelu, scale=np.float32(0.5), bias=b1_c[:, f:f + 1]),
                   reads=[pgr, "b1_c"], writes=[f"hb{fb}"])
            if "d_h" in dbg_h and s == 0 and f == 0:
                p.emit(ST2, "gpsimd",
                       lambda g, fb=fb: g.dma_start(out=dbg_h["d_h"][:, 0:TS2],
                                                    in_=hbuf[:, fb * TS2:(fb + 1) * TS2]),
                       reads=[f"hb{fb}"], dma=True, dsem="dbg")
            for m in range(8):
                mm(ST2, pF[m // 2][:, (m % 2) * TS2:(m % 2 + 1) * TS2],
                   w2tb[:, f * D + m * 128:f * D + (m + 1) * 128],
                   hbuf[:, fb * TS2:(fb + 1) * TS2],
                   reads=["w2tb", f"hb{fb}"], pres=f"pF{m // 2}",
                   start=(f == 0 and m % 2 == 0), stop=(f == 31))
        for m in range(8):
            p.emit(ST2, "scalar",
                   lambda se, m=m: se.activation(out=ffmf[:],
                                                 in_=pF[m // 2][:, (m % 2) * TS2:(m % 2 + 1) * TS2],
                                                 func=AF.Identity, scale=np.float32(0.5),
                                                 bias=b2_c[:, m:m + 1]),
                   reads=[f"pF{m // 2}", "b2_c"], writes=["ffmf"])
            for j in range(2):
                pj, pjr = (pT1, "pT1") if (m * 2 + j) % 2 == 0 else (pT2, "pT2")
                tp(ST2, pj[:, 0:128], ffmf[:, j * 128:(j + 1) * 128], reads=["ffmf"], pres=pjr)
                p.emit(ST2, "vector",
                       lambda v, m=m, j=j, pj=pj: v.tensor_scalar(
                           out=ffntm[:, j * D + m * 128:j * D + (m + 1) * 128],
                           in0=pj[:, 0:128], scalar1=1.0, scalar2=0.0,
                           op0=ALU.mult, op1=ALU.add, accum_out=st4[:, 8:9]),
                       reads=[pjr], writes=["ffntm", "st_p4"])
                if m == 0:
                    p.emit(ST2, "vector",
                           lambda v, j=j: v.tensor_copy(st4[:, 4 + j:5 + j], st4[:, 8:9]),
                           reads=["st_p4"], writes=["st_s4"])
                else:
                    p.emit(ST2, "vector",
                           lambda v, j=j: v.tensor_tensor(out=st4[:, 4 + j:5 + j],
                                                          in0=st4[:, 4 + j:5 + j],
                                                          in1=st4[:, 8:9], op=ALU.add),
                           reads=["st_p4", "st_s4"], writes=["st_s4"])
        for j in range(2):
            p.emit(ST2, "vector",
                   lambda v, j=j: v.scalar_tensor_tensor(out=junkf[:, 0:D],
                                                         in0=ffntm[:, j * D:(j + 1) * D], scalar=1.0,
                                                         in1=ffntm[:, j * D:(j + 1) * D],
                                                         op0=ALU.mult, op1=ALU.mult,
                                                         accum_out=st4[:, 6:7]),
                   reads=["ffntm"], writes=["st_q4"])
            p.emit(ST2, "vector",
                   lambda v, j=j: v.tensor_scalar(out=st4[:, 9:10], in0=st4[:, 4 + j:5 + j],
                                                  scalar1=1.0 / 1024.0, scalar2=None, op0=ALU.mult),
                   reads=["st_s4"], writes=["st_m4"])
            p.emit(ST2, "vector",
                   lambda v: v.scalar_tensor_tensor(out=st4[:, 10:11], in0=st4[:, 9:10],
                                                    scalar=-1.0, in1=st4[:, 9:10],
                                                    op0=ALU.mult, op1=ALU.mult),
                   reads=["st_m4"], writes=["st_r4"])
            p.emit(ST2, "vector",
                   lambda v: v.scalar_tensor_tensor(out=st4[:, 10:11], in0=st4[:, 6:7],
                                                    scalar=1.0 / 1024.0, in1=st4[:, 10:11],
                                                    op0=ALU.mult, op1=ALU.add),
                   reads=["st_q4", "st_r4"], writes=["st_r4"])
            rsqrt_cols(ST2, st4[:, 10:11], "st_r4")
            p.emit(ST2, "vector",
                   lambda v: v.scalar_tensor_tensor(out=st4[:, 11:12], in0=st4[:, 9:10],
                                                    scalar=-1.0, in1=st4[:, 10:11],
                                                    op0=ALU.mult, op1=ALU.mult),
                   reads=["st_m4", "st_r4"], writes=["st_nm4"])
            p.emit(ST2, "scalar",
                   lambda se, j=j: se.activation(out=ln4t[:], in_=ffntm[:, j * D:(j + 1) * D],
                                                 func=AF.Identity, scale=st4[:, 10:11],
                                                 bias=st4[:, 11:12]),
                   reads=["ffntm", "st_r4", "st_nm4"], writes=["ln4t"])
            p.emit(ST2, "gpsimd",
                   lambda g, j=j: g.tensor_tensor(out=x2t[:, j * D:(j + 1) * D],
                                                  in0=x1l[:, j * D:(j + 1) * D],
                                                  in1=ln4t[:], op=ALU.add),
                   reads=["x1l", "ln4t"], writes=["x2t"])
            p.emit(ST2, "gpsimd",
                   lambda g, j=j, t0=t0: g.dma_start(out=x2_h[t0 + j * 128:t0 + (j + 1) * 128, :],
                                                     in_=x2t[:, j * D:(j + 1) * D]),
                   reads=["x2t"], dma=True, dsem=f"x2st{j}")

    final_waits = [(sem, v) for sem, v in p.cnt.items() if sem.startswith("q_")]
    p.emit(ST2, "gpsimd", lambda g: g.engine_nop(), extra_waits=final_waits)

    run_stage(ST2)
    s2stack.close()
    stack0.close()
    return nc


def _host_inputs(inputs):
    F = np.float32
    x = np.asarray(inputs["L0_attn_x"], F)
    ipw = np.asarray(inputs["L0_attn_in_proj_w"], F)
    base = {
        "winT": np.ascontiguousarray(ipw[:1024].T),
        "winwT": np.ascontiguousarray(ipw[1024:].T),
        "woT": np.ascontiguousarray(np.asarray(inputs["L0_attn_out_proj_w"], F).T),
        "w1T": np.ascontiguousarray(np.asarray(inputs["L0_ffn_W1"], F).T),
        "w2T": np.ascontiguousarray(np.asarray(inputs["L0_ffn_W2"], F).T),
        "convw_sb": np.ascontiguousarray(
            np.asarray(inputs["L0_attn_conv_w"], F).reshape(8, 128, 4)
            .transpose(1, 0, 2).reshape(128, 32)),
        "convb_sb": np.ascontiguousarray(np.asarray(inputs["L0_attn_conv_b"], F).reshape(8, 128).T),
        "wbase_sb": np.ascontiguousarray(np.asarray(inputs["L0_attn_w_base"], F).reshape(H, 1)),
        "b1_sb": np.ascontiguousarray(np.asarray(inputs["L0_ffn_b1"], F).reshape(32, 128).T),
        "b2_sb": np.ascontiguousarray(np.asarray(inputs["L0_ffn_b2"], F).reshape(8, 128).T),
    }
    maps = []
    for b in range(x.shape[0]):
        m = dict(base)
        m["x0"] = np.ascontiguousarray(x[b])
        m["x0T"] = np.ascontiguousarray(x[b].T)
        maps.append(m)
    return maps


def kernel(**inputs):
    in_maps = _host_inputs(inputs)
    nc = build_program()
    res = run_bass_kernel_spmd(nc, in_maps, core_ids=list(range(8)))
    return np.stack([np.asarray(r["x2"], np.float32) for r in res.results])


# version 03
# speedup vs baseline: 1.6701x; 1.6701x over previous
import numpy as np
import ml_dtypes
import concourse.bass as bass
import concourse.mybir as mybir
from concourse.bass_utils import run_bass_kernel_spmd

F32 = mybir.dt.float32
BF16 = mybir.dt.bfloat16
AF = mybir.ActivationFunctionType
ALU = mybir.AluOpType
AX = mybir.AxisListType
BF = ml_dtypes.bfloat16

S, D, H, CH = 4096, 1024, 16, 64
NCHUNK = S // CH
TS1, TS2 = 512, 256
NS1, NS2 = S // TS1, S // TS2
FF = 4096
EPS = 1e-5
ENGINES = ("sync", "scalar", "vector", "tensor", "gpsimd")


class Plan:

    def __init__(self):
        self.ops = []
        self.cnt = {}
        self.res = {}

    def emit(self, stage, engine, fn, reads=(), writes=(), dma=False, extra_waits=(),
             dsem=None):
        if dma:
            sem = "q_" + (dsem if dsem is not None else engine)
        else:
            sem = "e_" + engine
        waits = {}

        def addw(mark, war=False):
            if mark is None:
                return
            s, v = mark
            if war and (not dma) and s == "e_" + engine:
                return
            if v > waits.get(s, -1):
                waits[s] = v

        wset = set(writes)
        for r in reads:
            rr = self.res.get(r)
            if rr:
                addw(rr["w"])
        for r in writes:
            rr = self.res.get(r)
            if rr:
                addw(rr["w"], war=True)
                for m in rr["r"]:
                    addw(m, war=True)
        for m in extra_waits:
            addw(m)
        if engine == "gpsimd" and not dma:
            prev = self.cnt.get("e_gpsimd", 0)
            if prev > waits.get("e_gpsimd", -1):
                waits["e_gpsimd"] = prev
        inc_by = 16 if dma else 1
        self.cnt[sem] = self.cnt.get(sem, 0) + inc_by
        mark = (sem, self.cnt[sem])
        self.ops.append((stage, engine, sorted(waits.items()), fn, sem, inc_by))
        for r in reads:
            if r in wset:
                continue
            self.res.setdefault(r, {"w": None, "r": []})["r"].append(mark)
        for r in writes:
            self.res[r] = {"w": mark, "r": []}
        return mark

    def snapshot(self):
        return [(s, v) for s, v in self.cnt.items()]


def build_program(debug=()):
    nc = bass.Bass("TRN2", target_bir_lowering=False)
    p = Plan()

    x0_h = nc.dram_tensor("x0", [S, D], F32, kind="ExternalInput").ap()
    x0T_h = nc.dram_tensor("x0T", [D, S], F32, kind="ExternalInput").ap()
    winT_h = nc.dram_tensor("winT", [D, D], F32, kind="ExternalInput").ap()
    winwT_h = nc.dram_tensor("winwT", [D, H], F32, kind="ExternalInput").ap()
    woT_h = nc.dram_tensor("woT", [D, D], F32, kind="ExternalInput").ap()
    w1T_h = nc.dram_tensor("w1T", [D, FF], F32, kind="ExternalInput").ap()
    w2T_h = nc.dram_tensor("w2T", [FF, D], F32, kind="ExternalInput").ap()
    convw_h = nc.dram_tensor("convw_sb", [128, 32], F32, kind="ExternalInput").ap()
    convb_h = nc.dram_tensor("convb_sb", [128, 8], F32, kind="ExternalInput").ap()
    wbase_h = nc.dram_tensor("wbase_sb", [H, 1], F32, kind="ExternalInput").ap()
    b1_h = nc.dram_tensor("b1_sb", [128, 32], F32, kind="ExternalInput").ap()
    b2_h = nc.dram_tensor("b2_sb", [128, 8], F32, kind="ExternalInput").ap()
    x2_h = nc.dram_tensor("x2", [S, D], F32, kind="ExternalOutput").ap()
    x1_h = nc.dram_tensor("x1_scr", [S, D], F32, kind="Internal").ap()
    w1t_h = nc.dram_tensor("w1t_scr", [D, FF], BF16, kind="Internal").ap()
    w2t_h = nc.dram_tensor("w2t_scr", [FF, D], BF16, kind="Internal").ap()
    rm_h = nc.dram_tensor("rm_scr", [H * 8], F32, kind="Internal").ap()
    nd_h = nc.dram_tensor("nd_scr", [128], F32, kind="Internal").ap()
    dbg_h = {}
    for k, shape, dt in (
        ("d_xi", [D, S], BF16), ("d_X", [S, D], F32), ("d_A", [H, S], F32),
        ("d_Y", [S, D], F32), ("d_attn", [S, D], F32), ("d_x1", [S, D], F32),
        ("d_un", [H, S], F32), ("d_sn", [H, S], F32), ("d_st", [NCHUNK, D], BF16),
        ("d_stp", [NCHUNK, D], BF16), ("d_w1t", [128, 2048], BF16),
        ("d_x1ln", [D, S], BF16), ("d_h", [128, 2048], BF16),
        ("d_m1", [128, 32], F32), ("d_r1", [128, 32], F32),
        ("d_mr1b", [128, TS1], F32), ("d_r1b", [128, TS1], F32),
        ("d_x0fmb", [128, 8 * TS1], BF16), ("d_acs", [H, S], F32),
        ("d_vv", [H, S], F32), ("d_cvv", [H, S], F32), ("d_sdo", [H, S], F32),
        ("d_nrm", [H, S], F32), ("d_wtok", [S, D], BF16), ("d_ytmp", [TS1, D], F32),
    ):
        if k in debug:
            dbg_h[k] = nc.dram_tensor(k, shape, dt, kind="ExternalOutput").ap()

    from contextlib import ExitStack
    stack0 = ExitStack()
    s1stack = ExitStack()
    s2stack = ExitStack()
    DSEMS = ("sync", "gpsimd", "x0tm", "x0fmb", "wstage0", "wstage1", "winbf",
             "winwbf", "wobf", "convw", "convb", "wbase", "b1", "b2", "x1l",
             "w1tb", "w2tb", "stc", "rmw", "rmr", "ndw", "ndr", "st", "dbg",
             "x1st0", "x1st1", "x2st0", "x2st1", "ternst")
    sems = {}
    for en in ENGINES:
        sems["e_" + en] = stack0.enter_context(nc.semaphore("e_" + en))
    for dn in DSEMS:
        sems["q_" + dn] = stack0.enter_context(nc.semaphore("q_" + dn))

    ST1, ST2 = 1, 2

    def sb0(name, shape, dt):
        return stack0.enter_context(nc.sbuf_tensor(name, shape, dt))

    def sb1(name, shape, dt):
        return s1stack.enter_context(nc.sbuf_tensor(name, shape, dt))

    def sb2(name, shape, dt):
        return s2stack.enter_context(nc.sbuf_tensor(name, shape, dt))

    def ps1(name, shape, dt):
        return s1stack.enter_context(nc.psum_tensor(name, shape, dt))

    def ps2(name, shape, dt):
        return s2stack.enter_context(nc.psum_tensor(name, shape, dt))

    block = stack0.enter_context(nc.Block())
    hw = {e: {} for e in ENGINES}

    def run_stage(stage):
        def mk(engine):
            def runner(eng):
                for stg, en, waits, fn, sem, inc_by in p.ops:
                    if stg != stage or en != engine:
                        continue
                    for sname, val in waits:
                        if hw[engine].get(sname, 0) >= val:
                            continue
                        hw[engine][sname] = val
                        eng.wait_ge(sems[sname], val)
                    fn(eng).then_inc(sems[sem], inc_by)
            return runner
        block.sync(mk("sync"))
        block.scalar(mk("scalar"))
        block.vector(mk("vector"))
        block.tensor(mk("tensor"))
        block.gpsimd(mk("gpsimd"))

    ones_f = sb0("ones_f", [128, 1], F32)
    onesr = sb0("onesr", [1, 128], F32)
    ones_b = sb0("ones_b", [128, 1], BF16)
    iden_f = sb0("iden_f", [128, 128], F32)
    convw_r = sb0("convw_r", [128, 32], F32)
    convb_c = sb0("convb_c", [128, 8], F32)
    wbase_c = sb0("wbase_c", [H, 1], F32)
    b1_c = sb0("b1_c", [128, 32], F32)
    b2_c = sb0("b2_c", [128, 8], F32)
    m1s = sb0("m1s", [128, 32], F32)
    junkf = sb0("junkf", [128, 1024], F32)
    junkb = sb0("junkb", [128, 1024], BF16)

    triones = sb1("triones", [128, 128], BF16)
    indstrip = sb1("indstrip", [128, 32], BF16)
    indloc = sb1("indloc", [8, 4 * 128], BF16)
    e4 = sb1("e4", [4, 4 * 128], F32)
    convdg = sb1("convdg", [128, 32 * 128], BF16)
    winbf = sb1("winbf", [128, 8 * D], BF16)
    winwbf = sb1("winwbf", [128, 8 * H], BF16)
    wobf = sb1("wobf", [128, 8 * D], BF16)
    nwinsum = sb1("nwinsum", [128, 8], F32)
    nwinwsum = sb1("nwinwsum", [H, 1], F32)
    r1s = sb1("r1s", [128, 32], F32)
    states_cm = sb1("states_cm", [NCHUNK, D], BF16)
    cend_t = sb1("cend_t", [H, 1 + NCHUNK], F32)
    mrc_t = sb1("mrc_t", [H, 1 + NCHUNK], F32)
    c1p_t = sb1("c1p_t", [H, NCHUNK], F32)
    ndu_t = sb1("ndu_t", [H, NCHUNK], F32)
    x0tm = sb1("x0tm", [128, 4 * D], F32)
    x0fmb = sb1("x0fmb", [128, 8 * TS1], BF16)
    xib = sb1("xib", [128, 8 * (TS1 + 3)], BF16)
    xstg = sb1("xstg", [128, 4 * 128], F32)
    xtok = sb1("xtok", [128, 4 * D], BF16)
    wtok = sb1("wtok", [128, 4 * D], BF16)
    ytok = sb1("ytok", [128, 2 * D], F32)
    ytmp = sb1("ytmp", [128, D], F32)
    yfmb = sb1("yfmb", [128, 8 * TS1], BF16)
    attfm = sb1("attfm", [128, TS1], F32)
    atttm = sb1("atttm", [128, 4 * D], F32)
    ln2t = sb1("ln2t", [128, D], F32)
    x1t = sb1("x1t", [128, 2 * D], F32)
    mr1b = sb1("mr1b", [128, TS1], F32)
    r1b = sb1("r1b", [128, TS1], F32)
    e1t = sb1("e1t", [128, 2 * TS1], F32)
    wfmA = sb1("wfmA", [H, TS1], F32)
    acs = sb1("acs", [H, TS1], F32)
    rmin = sb1("rmin", [H, TS1], F32)
    vv = sb1("vv", [H, TS1], F32)
    cvv = sb1("cvv", [H, TS1], F32)
    uu = sb1("uu", [H, TS1], F32)
    sdo = sb1("sdo", [H, TS1], F32)
    nrm = sb1("nrm", [H, TS1], F32)
    noff = sb1("noff", [H, TS1], F32)
    unq = sb1("unq", [H, TS1], F32)
    snq = sb1("snq", [H, TS1], F32)
    zer = sb1("zer", [H, CH], F32)
    sm8a = sb1("sm8a", [H, 8], F32)
    sm8b = sb1("sm8b", [H, 8], F32)
    ndf = sb1("ndf", [H, 8], F32)
    vqt = sb1("vqt", [128, 4 * H], F32)
    unqt = sb1("unqt", [128, 4 * H], F32)
    snqt = sb1("snqt", [128, 4 * H], F32)
    c1pT = sb1("c1pT", [NCHUNK, H], F32)
    nduT = sb1("nduT", [NCHUNK, H], F32)
    rmrow = sb1("rmrow", [1, 128], F32)
    dD = sb1("dD", [NCHUNK, 128], F32)
    dchb = sb1("dchb", [NCHUNK, 128], BF16)
    dchm = sb1("dchm", [NCHUNK, 128], BF16)
    prod = sb1("prod", [NCHUNK, 128], F32)
    ndcol = sb1("ndcol", [128, 1], F32)
    stsb = sb1("stsb", [8, D], BF16)
    stpsb = sb1("stpsb", [8, D], BF16)
    strow = sb1("strow", [128, 8], F32)
    mrT = sb1("mrT", [4, 128], F32)
    rT = sb1("rT", [4, 128], F32)
    sc = sb1("sc", [128, 24], F32)
    sc2 = sb1("sc2", [1, 8], F32)
    wstage = sb1("wstage", [128, 2 * 1024], F32)
    tsg1 = sb1("tsg1", [128, 1024], BF16)
    tsg2 = sb1("tsg2", [128, 1024], BF16)
    tout = sb1("tout", [128, 1024], BF16)
    absacc = sb1("absacc", [128, 2], F32)
    abscol = sb1("abscol", [128, 2], F32)
    invsc = sb1("invsc", [128, 2], F32)

    pA = ps1("pA", [128, 512], F32)
    pB = ps1("pB", [128, 512], F32)
    pC = ps1("pC", [128, 512], F32)
    pD = ps1("pD", [128, 512], F32)
    pW1 = ps1("pW1", [128, 512], F32)
    pW2 = ps1("pW2", [128, 512], F32)
    pSa = ps1("pSa", [8, 512], F32)
    pSb = ps1("pSb", [8, 512], F32)

    def mm(stage, out_ap, lhsT_ap, rhs_ap, reads, pres, start=True, stop=True,
           extra=()):
        return p.emit(stage, "tensor",
                      lambda t, o=out_ap, l=lhsT_ap, r=rhs_ap, s1=start, s2=stop:
                      t.matmul(o, l, r, start=s1, stop=s2, skip_group_check=True),
                      reads=reads, writes=[pres], extra_waits=extra)

    def tp(stage, out_ap, in_ap, reads, pres):
        k = in_ap.shape[0]
        return p.emit(stage, "tensor",
                      lambda t, o=out_ap, i=in_ap, k=k: t.transpose(o, i, iden_f[0:k, 0:k]),
                      reads=list(reads) + ["iden_f"], writes=[pres])

    def rsqrt_cols(stage, io_ap, rname):
        p.emit(stage, "scalar",
               lambda se: se.activation(out=io_ap, in_=io_ap, func=AF.Ln, bias=np.float32(EPS)),
               reads=[rname], writes=[rname])
        p.emit(stage, "scalar",
               lambda se: se.activation(out=io_ap, in_=io_ap, func=AF.Exp, scale=np.float32(-0.5)),
               reads=[rname], writes=[rname])

    PH = ST1
    for dst, src, res, dsn in ((convw_r[:], convw_h, "convw_r", "convw"),
                               (convb_c[:], convb_h, "convb_c", "convb"),
                               (wbase_c[:], wbase_h, "wbase_c", "wbase"),
                               (b1_c[:], b1_h, "b1_c", "b1"),
                               (b2_c[:], b2_h, "b2_c", "b2")):
        p.emit(PH, "sync", lambda e, d=dst, s=src: e.dma_start(out=d, in_=s),
               writes=[res], dma=True, dsem=dsn)
    for tt, rr, val in ((ones_f, "ones_f", 1.0), (onesr, "onesr", 1.0),
                        (ones_b, "ones_b", 1.0), (zer, "zer", 0.0),
                        (cend_t, "cend_t", 0.0), (mrc_t, "mrc_t", 0.0),
                        (c1p_t, "c1p_t", 0.0), (ndu_t, "ndu_t", 0.0),
                        (absacc, "absacc", 0.0), (indstrip, "indstrip", 0.0)):
        p.emit(PH, "vector", lambda v, t=tt, x=val: v.memset(t[:], x), writes=[rr])
    p.emit(PH, "vector", lambda v: v.memset(states_cm[:], 0.0), writes=["states_cm"])
    p.emit(PH, "vector", lambda v: v.memset(xib[:], 0.0),
           writes=[f"xi{m}" for m in range(8)] + [f"xihalo{m}" for m in range(8)])
    p.emit(PH, "gpsimd",
           lambda g: g.affine_select(out=iden_f[:], in_=ones_f[:, 0:1].to_broadcast((128, 128)),
                                     pattern=[[1, 128]], compare_op=ALU.is_equal,
                                     fill=0.0, base=0, channel_multiplier=-1),
           reads=["ones_f"], writes=["iden_f"])
    p.emit(PH, "gpsimd",
           lambda g: g.affine_select(out=triones[:], in_=ones_b[:, 0:1].to_broadcast((128, 128)),
                                     pattern=[[1, 128]], compare_op=ALU.is_ge,
                                     fill=0.0, base=0, channel_multiplier=-1),
           reads=["ones_b"], writes=["triones"])
    p.emit(PH, "vector", lambda v: v.memset(triones[0:64, 64:128], 0.0), writes=["triones"])
    for cp in range(4):
        p.emit(PH, "vector",
               lambda v, cp=cp: v.memset(indstrip[0:64, cp * 8 + 2 * cp:cp * 8 + 2 * cp + 1], 1.0),
               writes=["indstrip"])
        p.emit(PH, "vector",
               lambda v, cp=cp: v.memset(indstrip[64:128, cp * 8 + 2 * cp + 1:cp * 8 + 2 * cp + 2], 1.0),
               writes=["indstrip"])
        p.emit(PH, "gpsimd",
               lambda g, cp=cp: g.affine_select(
                   out=indloc[:, cp * 128:(cp + 1) * 128].rearrange("p (a b) -> p a b", a=2),
                   in_=ones_b[0:8, 0:1].to_broadcast((8, 2, 64)),
                   pattern=[[1, 2], [0, 64]], compare_op=ALU.is_equal,
                   fill=0.0, base=2 * cp, channel_multiplier=-1),
               reads=["ones_b"], writes=["indloc"])
    for j in range(4):
        p.emit(PH, "gpsimd",
               lambda g, j=j: g.affine_select(out=e4[:, j * 128:(j + 1) * 128],
                                              in_=ones_f[0:4, 0:1].to_broadcast((4, 128)),
                                              pattern=[[0, 128]], compare_op=ALU.is_equal,
                                              fill=0.0, base=-j, channel_multiplier=1),
               reads=["ones_f"], writes=["e4"])
    for ck in range(32):
        p.emit(PH, "vector",
               lambda v, ck=ck: v.tensor_tensor(out=convdg[:, ck * 128:(ck + 1) * 128],
                                                in0=iden_f[:],
                                                in1=convw_r[:, ck:ck + 1].to_broadcast((128, 128)),
                                                op=ALU.mult),
               reads=["iden_f", "convw_r"], writes=["convdg"])
    p.emit(PH, "gpsimd",
           lambda g: g.dma_start(out=winbf.rearrange("p (k e) -> p k e", k=8),
                                 in_=winT_h.rearrange("(k p) e -> p k e", p=128)),
           writes=["winbf"], dma=True, dsem="winbf")
    p.emit(PH, "gpsimd",
           lambda g: g.dma_start(out=winwbf.rearrange("p (k e) -> p k e", k=8),
                                 in_=winwT_h.rearrange("(k p) e -> p k e", p=128)),
           writes=["winwbf"], dma=True, dsem="winwbf")
    p.emit(PH, "gpsimd",
           lambda g: g.dma_start(out=wobf.rearrange("p (k e) -> p k e", k=8),
                                 in_=woT_h.rearrange("(k p) e -> p k e", p=128)),
           writes=["wobf"], dma=True, dsem="wobf")
    for m in range(8):
        for k in range(8):
            mm(PH, pC[:, m:m + 1], winbf[:, k * D + m * 128:k * D + (m + 1) * 128],
               ones_b[:], reads=["winbf", "ones_b"], pres="pC",
               start=(k == 0), stop=(k == 7))
    p.emit(PH, "vector",
           lambda v: v.tensor_scalar(out=nwinsum[:], in0=pC[:, 0:8], scalar1=-1.0,
                                     scalar2=None, op0=ALU.mult),
           reads=["pC"], writes=["nwinsum"])
    for k in range(8):
        mm(PH, pD[0:H, 0:1], winwbf[:, k * H:(k + 1) * H], ones_b[:],
           reads=["winwbf", "ones_b"], pres="pD", start=(k == 0), stop=(k == 7))
    p.emit(PH, "vector",
           lambda v: v.tensor_scalar(out=nwinwsum[:], in0=pD[0:H, 0:1], scalar1=-1.0,
                                     scalar2=None, op0=ALU.mult),
           reads=["pD"], writes=["nwinwsum"])

    tern_tiles = [(0, k) for k in range(32)] + [(1, k) for k in range(32)]

    def tern_abs(wi, k):
        wh = w1T_h if wi == 0 else w2T_h
        if wi == 0:
            src = wh[(k // 4) * 128:(k // 4 + 1) * 128, (k % 4) * 1024:(k % 4 + 1) * 1024]
        else:
            src = wh[k * 128:(k + 1) * 128, :]
        hb_ = k % 2
        p.emit(ST1, "sync",
               lambda e, s=src, hb_=hb_: e.dma_start(out=wstage[:, hb_ * 1024:(hb_ + 1) * 1024], in_=s),
               writes=[f"wstage{hb_}"], dma=True, dsem=f"wstage{hb_}")
        p.emit(ST1, "vector",
               lambda v, hb_=hb_: v.tensor_reduce(out=abscol[:, hb_:hb_ + 1],
                                                  in_=wstage[:, hb_ * 1024:(hb_ + 1) * 1024],
                                                  axis=AX.X, op=ALU.add,
                                                  apply_absolute_value=True),
               reads=[f"wstage{hb_}"], writes=[f"abscol{hb_}"])
        p.emit(ST1, "vector",
               lambda v, wi=wi, hb_=hb_: v.tensor_tensor(out=absacc[:, wi:wi + 1],
                                                         in0=absacc[:, wi:wi + 1],
                                                         in1=abscol[:, hb_:hb_ + 1], op=ALU.add),
               reads=[f"abscol{hb_}", "absacc"], writes=["absacc"])

    def tern_scale(wi):
        mm(ST1, pC[0:1, 0:1], absacc[:, wi:wi + 1], ones_f[:],
           reads=["absacc", "ones_f"], pres="pC")
        p.emit(ST1, "scalar",
               lambda se: se.activation(out=sc2[0:1, 0:1], in_=pC[0:1, 0:1],
                                        func=AF.Identity, scale=np.float32(1.0 / (4096.0 * 1024.0))),
               reads=["pC"], writes=["sc2"])
        p.emit(ST1, "vector",
               lambda v: v.tensor_scalar(out=sc2[0:1, 0:1], in0=sc2[0:1, 0:1],
                                         scalar1=1e-8, scalar2=None, op0=ALU.max),
               reads=["sc2"], writes=["sc2"])
        p.emit(ST1, "scalar",
               lambda se: se.activation(out=sc2[0:1, 1:2], in_=sc2[0:1, 0:1], func=AF.Ln),
               reads=["sc2"], writes=["sc2"])
        p.emit(ST1, "scalar",
               lambda se: se.activation(out=sc2[0:1, 2:3], in_=sc2[0:1, 1:2],
                                        func=AF.Exp, scale=np.float32(-1.0)),
               reads=["sc2"], writes=["sc2"])
        mm(ST1, pC[:, 0:1], onesr[:], sc2[0:1, 2:3], reads=["sc2", "onesr"], pres="pC")
        p.emit(ST1, "vector",
               lambda v, wi=wi: v.tensor_copy(invsc[:, wi:wi + 1], pC[:, 0:1]),
               reads=["pC"], writes=["invsc"])

    def tern_sign(wi, k):
        wh = w1T_h if wi == 0 else w2T_h
        wth = w1t_h if wi == 0 else w2t_h
        if wi == 0:
            sl = ((k // 4) * 128, (k % 4) * 1024)
        else:
            sl = (k * 128, 0)
        src = wh[sl[0]:sl[0] + 128, sl[1]:sl[1] + 1024]
        dst = wth[sl[0]:sl[0] + 128, sl[1]:sl[1] + 1024]
        hb_ = k % 2
        o = hb_ * 1024
        p.emit(ST1, "sync",
               lambda e, s=src, o=o: e.dma_start(out=wstage[:, o:o + 1024], in_=s),
               writes=[f"wstage{hb_}"], dma=True, dsem=f"wstage{hb_}")
        p.emit(ST1, "scalar",
               lambda se, o=o, wi=wi: se.activation(out=tsg1[:], in_=wstage[:, o:o + 1024],
                                                    func=AF.Sign,
                                                    scale=invsc[:, wi:wi + 1], bias=np.float32(-0.5)),
               reads=[f"wstage{hb_}", "invsc"], writes=["tsg1"])
        p.emit(ST1, "scalar",
               lambda se, o=o, wi=wi: se.activation(out=tsg2[:], in_=wstage[:, o:o + 1024],
                                                    func=AF.Sign,
                                                    scale=invsc[:, wi:wi + 1], bias=np.float32(0.5)),
               reads=[f"wstage{hb_}", "invsc"], writes=["tsg2"])
        p.emit(ST1, "vector",
               lambda v: v.tensor_tensor(out=tout[:], in0=tsg1[:], in1=tsg2[:], op=ALU.add),
               reads=["tsg1", "tsg2"], writes=["tout"])
        p.emit(ST1, "gpsimd",
               lambda g, d=dst: g.dma_start(out=d, in_=tout[:]),
               reads=["tout"], dma=True, dsem="ternst")

    def tern_step(s):
        if s in (0, 1):
            for k in range(16 * s, 16 * s + 16):
                tern_abs(0, k)
        if s == 2:
            tern_scale(0)
        if s in (2, 3):
            for k in range(16 * (s - 2), 16 * (s - 2) + 16):
                tern_abs(1, k)
        if 2 <= s <= 5:
            for k in range(8 * (s - 2), 8 * (s - 2) + 8):
                tern_sign(0, k)
        if s == 4:
            tern_scale(1)
        if 4 <= s <= 7:
            for k in range(8 * (s - 4), 8 * (s - 4) + 8):
                tern_sign(1, k)

    for s in range(NS1):
        t0 = s * TS1
        tern_step(s)

        p.emit(ST1, "sync",
               lambda e, t0=t0: e.dma_start(out=x0tm.rearrange("p (j d) -> p j d", j=4),
                                            in_=x0_h[t0:t0 + TS1, :].rearrange("(j p) d -> p j d", p=128)),
               writes=["x0tm"], dma=True, dsem="x0tm")
        p.emit(ST1, "gpsimd",
               lambda g, t0=t0: g.dma_start(out=x0fmb.rearrange("p (k t) -> p k t", k=8),
                                            in_=x0T_h[:, t0:t0 + TS1].rearrange("(k p) t -> p k t", p=128)),
               writes=["x0fmb"], dma=True, dsem="x0fmb")

        for j in range(4):
            p.emit(ST1, "vector",
                   lambda v, j=j: v.tensor_scalar(out=junkf[:, 0:D], in0=x0tm[:, j * D:(j + 1) * D],
                                                  scalar1=1.0, scalar2=0.0, op0=ALU.mult,
                                                  op1=ALU.add, accum_out=sc[:, j:j + 1]),
                   reads=["x0tm"], writes=["sc_s1"])
            p.emit(ST1, "vector",
                   lambda v, j=j: v.scalar_tensor_tensor(out=junkf[:, 0:D],
                                                         in0=x0tm[:, j * D:(j + 1) * D], scalar=1.0,
                                                         in1=x0tm[:, j * D:(j + 1) * D],
                                                         op0=ALU.mult, op1=ALU.mult,
                                                         accum_out=sc[:, 4 + j:5 + j]),
                   reads=["x0tm"], writes=["sc_q1"])
        p.emit(ST1, "vector",
               lambda v, s=s: v.tensor_scalar(out=m1s[:, 4 * s:4 * s + 4], in0=sc[:, 0:4],
                                              scalar1=1.0 / 1024.0, scalar2=None, op0=ALU.mult),
               reads=["sc_s1"], writes=["m1s"])
        p.emit(ST1, "vector",
               lambda v, s=s: v.scalar_tensor_tensor(out=r1s[:, 4 * s:4 * s + 4],
                                                     in0=m1s[:, 4 * s:4 * s + 4], scalar=-1.0,
                                                     in1=m1s[:, 4 * s:4 * s + 4],
                                                     op0=ALU.mult, op1=ALU.mult),
               reads=["m1s"], writes=["r1s"])
        p.emit(ST1, "vector",
               lambda v, s=s: v.scalar_tensor_tensor(out=r1s[:, 4 * s:4 * s + 4],
                                                     in0=sc[:, 4:8], scalar=1.0 / 1024.0,
                                                     in1=r1s[:, 4 * s:4 * s + 4],
                                                     op0=ALU.mult, op1=ALU.add),
               reads=["sc_q1", "r1s"], writes=["r1s"])
        rsqrt_cols(ST1, r1s[:, 4 * s:4 * s + 4], "r1s")
        p.emit(ST1, "vector",
               lambda v, s=s: v.tensor_tensor(out=strow[:, 0:4], in0=m1s[:, 4 * s:4 * s + 4],
                                              in1=r1s[:, 4 * s:4 * s + 4], op=ALU.mult),
               reads=["m1s", "r1s"], writes=["strow"])
        p.emit(ST1, "vector",
               lambda v, s=s: v.tensor_copy(strow[:, 4:8], r1s[:, 4 * s:4 * s + 4]),
               reads=["r1s", "strow"], writes=["strow"])
        tp(ST1, pC[0:4, 0:128], strow[:, 0:4], reads=["strow"], pres="pC")
        p.emit(ST1, "vector", lambda v: v.tensor_copy(mrT[:], pC[0:4, 0:128]),
               reads=["pC"], writes=["mrT"])
        tp(ST1, pD[0:4, 0:128], strow[:, 4:8], reads=["strow"], pres="pD")
        p.emit(ST1, "vector", lambda v: v.tensor_copy(rT[:], pD[0:4, 0:128]),
               reads=["pD"], writes=["rT"])
        for j in range(4):
            mm(ST1, pA[:, j * 128:(j + 1) * 128], e4[:, j * 128:(j + 1) * 128], mrT[:],
               reads=["e4", "mrT"], pres="pA", start=(j == 0), stop=(j == 3))
        for j in range(4):
            mm(ST1, pB[:, j * 128:(j + 1) * 128], e4[:, j * 128:(j + 1) * 128], rT[:],
               reads=["e4", "rT"], pres="pB", start=(j == 0), stop=(j == 3))
        p.emit(ST1, "scalar", lambda se: se.activation(out=mr1b[:], in_=pA[:], func=AF.Identity),
               reads=["pA"], writes=["mr1b"])
        p.emit(ST1, "scalar", lambda se: se.activation(out=r1b[:], in_=pB[:], func=AF.Identity),
               reads=["pB"], writes=["r1b"])

        if s == 0:
            if "d_mr1b" in dbg_h:
                p.emit(ST1, "gpsimd", lambda g: g.dma_start(out=dbg_h["d_mr1b"], in_=mr1b[:]),
                       reads=["mr1b"], dma=True, dsem="dbg")
            if "d_r1b" in dbg_h:
                p.emit(ST1, "gpsimd", lambda g: g.dma_start(out=dbg_h["d_r1b"], in_=r1b[:]),
                       reads=["r1b"], dma=True, dsem="dbg")
            if "d_x0fmb" in dbg_h:
                p.emit(ST1, "gpsimd", lambda g: g.dma_start(out=dbg_h["d_x0fmb"], in_=x0fmb[:]),
                       reads=["x0fmb"], dma=True, dsem="dbg")
        if "d_m1" in dbg_h and s == NS1 - 1:
            p.emit(ST1, "gpsimd", lambda g: g.dma_start(out=dbg_h["d_m1"], in_=m1s[:]),
                   reads=["m1s"], dma=True, dsem="dbg")
        if "d_r1" in dbg_h and s == NS1 - 1:
            p.emit(ST1, "gpsimd", lambda g: g.dma_start(out=dbg_h["d_r1"], in_=r1s[:]),
                   reads=["r1s"], dma=True, dsem="dbg")

        for k in range(8):
            mm(ST1, pC[0:H, :], winwbf[:, k * H:(k + 1) * H], x0fmb[:, k * TS1:(k + 1) * TS1],
               reads=["winwbf", "x0fmb"], pres="pC", start=(k == 0), stop=(k == 7))
        p.emit(ST1, "vector",
               lambda v: v.scalar_tensor_tensor(out=wfmA[:], in0=mr1b[0:H, :],
                                                scalar=nwinwsum[:, 0:1], op0=ALU.mult,
                                                in1=pC[0:H, :], op1=ALU.add),
               reads=["mr1b", "nwinwsum", "pC"], writes=["wfmA"])
        p.emit(ST1, "vector",
               lambda v: v.scalar_tensor_tensor(out=wfmA[:], in0=wfmA[:], scalar=wbase_c[:, 0:1],
                                                op0=ALU.mult, in1=r1b[0:H, :], op1=ALU.mult),
               reads=["wfmA", "wbase_c", "r1b"], writes=["wfmA"])
        if "d_A" in dbg_h:
            p.emit(ST1, "gpsimd",
                   lambda g, t0=t0: g.dma_start(out=dbg_h["d_A"][:, t0:t0 + TS1], in_=wfmA[:]),
                   reads=["wfmA"], dma=True, dsem="dbg")

        for c in range(8):
            p.emit(ST1, "vector",
                   lambda v, c=c: v.tensor_tensor_scan(out=acs[:, c * CH:(c + 1) * CH],
                                                       data0=wfmA[:, c * CH:(c + 1) * CH],
                                                       data1=zer[:], initial=0.0,
                                                       op0=ALU.add, op1=ALU.add),
                   reads=["wfmA", "zer"], writes=["acs"])
        for c in range(8):
            p.emit(ST1, "vector",
                   lambda v, c=c: v.tensor_tensor_scan(out=rmin[:, c * CH:(c + 1) * CH],
                                                       data0=acs[:, c * CH:(c + 1) * CH],
                                                       data1=zer[:], initial=1e30,
                                                       op0=ALU.min, op1=ALU.add),
                   reads=["acs", "zer"], writes=["rmin"])
        p.emit(ST1, "scalar", lambda se: se.activation(out=vv[:], in_=acs[:], func=AF.Exp, scale=np.float32(-1.0)),
               reads=["acs"], writes=["vv"])
        p.emit(ST1, "scalar", lambda se: se.activation(out=uu[:], in_=rmin[:], func=AF.Exp),
               reads=["rmin"], writes=["uu"])
        for c in range(8):
            p.emit(ST1, "vector",
                   lambda v, c=c: v.tensor_tensor_scan(out=cvv[:, c * CH:(c + 1) * CH],
                                                       data0=vv[:, c * CH:(c + 1) * CH],
                                                       data1=zer[:], initial=0.0,
                                                       op0=ALU.add, op1=ALU.add),
                   reads=["vv", "zer"], writes=["cvv"])
        for c in range(8):
            p.emit(ST1, "vector",
                   lambda v, c=c: v.tensor_reduce(out=sm8a[:, c:c + 1],
                                                  in_=acs[:, c * CH:(c + 1) * CH],
                                                  axis=AX.X, op=ALU.max, negate=True),
                   reads=["acs"], writes=["sm8a"])
        for c in range(8):
            p.emit(ST1, "scalar",
                   lambda se, c=c: se.activation(out=sdo[:, c * CH:(c + 1) * CH],
                                                 in_=acs[:, c * CH:(c + 1) * CH],
                                                 func=AF.Exp, bias=sm8a[:, c:c + 1]),
                   reads=["acs", "sm8a"], writes=["sdo"])
        p.emit(ST1, "vector",
               lambda v: v.tensor_copy(sm8b[:], acs[:, CH - 1::CH]),
               reads=["acs"], writes=["sm8b"])
        p.emit(ST1, "vector",
               lambda v, s=s: v.tensor_tensor_scan(out=cend_t[:, 1 + 8 * s:9 + 8 * s],
                                                   data0=sm8b[:], data1=zer[:, 0:8],
                                                   initial=cend_t[:, 8 * s:8 * s + 1],
                                                   op0=ALU.add, op1=ALU.add),
               reads=["sm8b", "zer", "cend_t"], writes=["cend_t"])
        p.emit(ST1, "vector",
               lambda v, s=s: v.tensor_tensor_scan(out=mrc_t[:, 1 + 8 * s:9 + 8 * s],
                                                   data0=cend_t[:, 8 * s:8 * s + 8],
                                                   data1=zer[:, 0:8],
                                                   initial=mrc_t[:, 8 * s:8 * s + 1],
                                                   op0=ALU.min, op1=ALU.add),
               reads=["cend_t", "mrc_t", "zer"], writes=["mrc_t"])
        p.emit(ST1, "vector",
               lambda v: v.tensor_copy(sm8b[:], rmin[:, CH - 1::CH]),
               reads=["rmin", "sm8b"], writes=["sm8b"])
        p.emit(ST1, "vector",
               lambda v, s=s: v.tensor_tensor(out=c1p_t[:, 8 * s:8 * s + 8],
                                              in0=cend_t[:, 1 + 8 * s:9 + 8 * s],
                                              in1=sm8b[:], op=ALU.subtract),
               reads=["cend_t", "sm8b"], writes=["c1p_t"])
        p.emit(ST1, "vector",
               lambda v, s=s: v.tensor_copy(ndu_t[:, 8 * s:8 * s + 8], cvv[:, CH - 1::CH]),
               reads=["cvv"], writes=["ndu_t"])
        p.emit(ST1, "gpsimd",
               lambda g, s=s: g.dma_start(out=rm_h.rearrange("(h z) -> h z", h=H),
                                          in_=mrc_t[:, 1 + 8 * s:9 + 8 * s]),
               reads=["mrc_t"], writes=["rm_hbm"], dma=True, dsem="rmw")
        p.emit(ST1, "gpsimd",
               lambda g: g.dma_start(out=rmrow[:], in_=rm_h.rearrange("(a b) -> a b", a=1)),
               reads=["rm_hbm"], writes=["rmrow"], dma=True, dsem="rmr")

        for m in range(8):
            pm, pmr = (pA, "pA") if m % 2 == 0 else (pB, "pB")
            for k in range(8):
                mm(ST1, pm[:], winbf[:, k * D + m * 128:k * D + (m + 1) * 128],
                   x0fmb[:, k * TS1:(k + 1) * TS1],
                   reads=["winbf", "x0fmb"], pres=pmr, start=(k == 0), stop=(k == 7))
            eb = m % 2
            p.emit(ST1, "vector",
                   lambda v, m=m, pm=pm, eb=eb: v.scalar_tensor_tensor(
                       out=e1t[:, eb * TS1:(eb + 1) * TS1], in0=mr1b[:],
                       scalar=nwinsum[:, m:m + 1], op0=ALU.mult, in1=pm[:], op1=ALU.add),
                   reads=["mr1b", "nwinsum", pmr], writes=[f"e1t{eb}"])
            p.emit(ST1, "gpsimd",
                   lambda g, m=m, eb=eb: g.tensor_tensor(
                       out=xib[:, m * (TS1 + 3) + 3:(m + 1) * (TS1 + 3)],
                       in0=e1t[:, eb * TS1:(eb + 1) * TS1], in1=r1b[:], op=ALU.mult),
                   reads=[f"e1t{eb}", "r1b"], writes=[f"xi{m}"])
        if "d_xi" in dbg_h:
            for m in range(8):
                p.emit(ST1, "gpsimd",
                       lambda g, m=m, t0=t0: g.dma_start(
                           out=dbg_h["d_xi"][m * 128:(m + 1) * 128, t0:t0 + TS1],
                           in_=xib[:, m * (TS1 + 3) + 3:(m + 1) * (TS1 + 3)]),
                       reads=[f"xi{m}"], dma=True, dsem="dbg")

        for c in range(8):
            pm, pmr = (pC, "pC") if c % 2 == 0 else (pD, "pD")
            for k in range(4):
                mm(ST1, pm[:], convdg[:, (c * 4 + k) * 128:(c * 4 + k + 1) * 128],
                   xib[:, c * (TS1 + 3) + k:c * (TS1 + 3) + k + TS1],
                   reads=["convdg", f"xi{c}", f"xihalo{c}"], pres=pmr,
                   start=(k == 0), stop=(k == 3))
            for j in range(4):
                r = j % 4
                p.emit(ST1, "scalar",
                       lambda se, j=j, pm=pm, r=r, c=c: se.activation(
                           out=xstg[:, r * 128:(r + 1) * 128],
                           in_=pm[:, j * 128:(j + 1) * 128], func=AF.Identity,
                           bias=convb_c[:, c:c + 1]),
                       reads=[pmr, "convb_c"], writes=[f"xstg{r}"])
                pj, pjr = (pA, "pA") if (c * 4 + j) % 2 == 0 else (pB, "pB")
                tp(ST1, pj[:, 0:128], xstg[:, r * 128:(r + 1) * 128],
                   reads=[f"xstg{r}"], pres=pjr)
                p.emit(ST1, "scalar",
                       lambda se, c=c, j=j, pj=pj: se.activation(
                           out=xtok[:, j * D + c * 128:j * D + (c + 1) * 128],
                           in_=pj[:, 0:128], func=AF.Identity),
                       reads=[pjr], writes=["xtok"])
        if "d_X" in dbg_h:
            p.emit(ST1, "gpsimd",
                   lambda g, t0=t0: g.dma_start(
                       out=dbg_h["d_X"][t0:t0 + TS1, :].rearrange("(j p) d -> p j d", p=128),
                       in_=xtok.rearrange("p (j d) -> p j d", j=4)),
                   reads=["xtok"], dma=True, dsem="dbg")
        p.emit(ST1, "vector",
               lambda v: v.tensor_copy(xib.rearrange("p (c t) -> p c t", c=8)[:, :, 0:3],
                                       xib.rearrange("p (c t) -> p c t", c=8)[:, :, TS1:TS1 + 3]),
               reads=[f"xi{m}" for m in range(8)],
               writes=[f"xihalo{m}" for m in range(8)])

        tp(ST1, pC[0:64, 0:H], c1p_t[:, 0:NCHUNK], reads=["c1p_t"], pres="pC")
        p.emit(ST1, "vector", lambda v: v.tensor_copy(c1pT[:], pC[0:64, 0:H]),
               reads=["pC"], writes=["c1pT"])
        tp(ST1, pD[0:64, 0:H], ndu_t[:, 0:NCHUNK], reads=["ndu_t"], pres="pD")
        p.emit(ST1, "vector", lambda v: v.tensor_copy(nduT[:], pD[0:64, 0:H]),
               reads=["pD"], writes=["nduT"])
        mm(ST1, pW1[0:64, 0:128], onesr[0:1, 0:64], rmrow[:],
           reads=["onesr", "rmrow"], pres="pW1")
        p.emit(ST1, "vector",
               lambda v: v.tensor_tensor(out=dD.rearrange("p (h z) -> p h z", h=H),
                                         in0=pW1[0:64, 0:128].rearrange("p (h z) -> p h z", h=H),
                                         in1=c1pT[:, :, None].to_broadcast((64, H, 8)),
                                         op=ALU.subtract),
               reads=["pW1", "c1pT"], writes=["dD"])
        p.emit(ST1, "scalar",
               lambda se: se.activation(out=dchb[:], in_=dD[:], func=AF.Exp),
               reads=["dD"], writes=["dchb"])
        p.emit(ST1, "gpsimd",
               lambda g, s=s: g.affine_select(out=dchm.rearrange("p (h z) -> p h z", h=H),
                                              in_=dchb.rearrange("p (h z) -> p h z", h=H),
                                              pattern=[[0, H], [1, 8]], compare_op=ALU.is_ge,
                                              fill=0.0, base=8 * s - 1, channel_multiplier=-1),
               reads=["dchb"], writes=["dchm"])
        p.emit(ST1, "vector",
               lambda v: v.tensor_tensor(out=prod.rearrange("p (h z) -> p h z", h=H),
                                         in0=dchm.rearrange("p (h z) -> p h z", h=H),
                                         in1=nduT[:, :, None].to_broadcast((64, H, 8)),
                                         op=ALU.mult),
               reads=["dchm", "nduT"], writes=["prod"])
        mm(ST1, pC[:, 0:1], prod[:], ones_f[0:64, 0:1], reads=["prod", "ones_f"], pres="pC")
        p.emit(ST1, "vector", lambda v: v.tensor_copy(ndcol[:], pC[:, 0:1]),
               reads=["pC"], writes=["ndcol"])
        p.emit(ST1, "gpsimd",
               lambda g: g.dma_start(out=nd_h.rearrange("(a b) -> a b", b=1), in_=ndcol[:]),
               reads=["ndcol"], writes=["nd_hbm"], dma=True, dsem="ndw")
        p.emit(ST1, "gpsimd",
               lambda g: g.dma_start(out=ndf[:], in_=nd_h.rearrange("(h z) -> h z", h=H)),
               reads=["nd_hbm"], writes=["ndf"], dma=True, dsem="ndr")
        p.emit(ST1, "vector",
               lambda v: v.tensor_tensor(out=nrm[:], in0=uu[:], in1=cvv[:], op=ALU.mult),
               reads=["uu", "cvv"], writes=["nrm"])
        p.emit(ST1, "vector",
               lambda v: v.tensor_tensor(out=noff.rearrange("p (c l) -> p c l", c=8),
                                         in0=sdo.rearrange("p (c l) -> p c l", c=8),
                                         in1=ndf[:, :, None].to_broadcast((H, 8, CH)),
                                         op=ALU.mult),
               reads=["sdo", "ndf"], writes=["noff"])
        p.emit(ST1, "vector",
               lambda v: v.tensor_tensor(out=nrm[:], in0=nrm[:], in1=noff[:], op=ALU.add),
               reads=["nrm", "noff"], writes=["nrm"])
        p.emit(ST1, "scalar", lambda se: se.activation(out=nrm[:], in_=nrm[:], func=AF.Ln),
               reads=["nrm"], writes=["nrm"])
        p.emit(ST1, "scalar", lambda se: se.activation(out=nrm[:], in_=nrm[:], func=AF.Exp, scale=np.float32(-1.0)),
               reads=["nrm"], writes=["nrm"])
        p.emit(ST1, "vector",
               lambda v: v.tensor_tensor(out=unq[:], in0=uu[:], in1=nrm[:], op=ALU.mult),
               reads=["uu", "nrm"], writes=["unq"])
        p.emit(ST1, "vector",
               lambda v: v.tensor_tensor(out=snq[:], in0=sdo[:], in1=nrm[:], op=ALU.mult),
               reads=["sdo", "nrm"], writes=["snq"])
        if "d_un" in dbg_h:
            p.emit(ST1, "gpsimd", lambda g, t0=t0: g.dma_start(out=dbg_h["d_un"][:, t0:t0 + TS1], in_=unq[:]),
                   reads=["unq"], dma=True, dsem="dbg")
            p.emit(ST1, "gpsimd", lambda g, t0=t0: g.dma_start(out=dbg_h["d_sn"][:, t0:t0 + TS1], in_=snq[:]),
                   reads=["snq"], dma=True, dsem="dbg")
        for nm, tt, rr in (("d_acs", acs, "acs"), ("d_vv", vv, "vv"), ("d_cvv", cvv, "cvv"),
                           ("d_sdo", sdo, "sdo"), ("d_nrm", nrm, "nrm")):
            if nm in dbg_h:
                p.emit(ST1, "gpsimd",
                       lambda g, nm=nm, tt=tt, t0=t0: g.dma_start(out=dbg_h[nm][:, t0:t0 + TS1], in_=tt[:]),
                       reads=[rr], dma=True, dsem="dbg")
        for qsrc, qdst, rs, ws in ((vv, vqt, "vv", "vqt"), (unq, unqt, "unq", "unqt"),
                                   (snq, snqt, "snq", "snqt")):
            for j in range(4):
                pj, pjr = (pC, "pC") if j % 2 == 0 else (pD, "pD")
                tp(ST1, pj[0:128, 0:H], qsrc[:, j * 128:(j + 1) * 128], reads=[rs], pres=pjr)
                p.emit(ST1, "vector",
                       lambda v, j=j, qd=qdst, pj=pj: v.tensor_copy(qd[:, j * H:(j + 1) * H],
                                                                    pj[0:128, 0:H]),
                       reads=[pjr], writes=[ws])

        for j in range(4):
            p.emit(ST1, "vector",
                   lambda v, j=j: v.tensor_tensor(
                       out=wtok[:, j * D:(j + 1) * D].rearrange("p (h n) -> p h n", h=H),
                       in0=xtok[:, j * D:(j + 1) * D].rearrange("p (h n) -> p h n", h=H),
                       in1=vqt[:, j * H:(j + 1) * H, None].to_broadcast((128, H, 64)),
                       op=ALU.mult),
                   reads=["xtok", "vqt"], writes=["wtok"])
        if "d_wtok" in dbg_h:
            p.emit(ST1, "gpsimd",
                   lambda g, t0=t0: g.dma_start(
                       out=dbg_h["d_wtok"][t0:t0 + TS1, :].rearrange("(j p) d -> p j d", p=128),
                       in_=wtok.rearrange("p (j d) -> p j d", j=4)),
                   reads=["wtok"], dma=True, dsem="dbg")
        for j in range(4):
            mm(ST1, pSa[:], indstrip[:, j * 8:(j + 1) * 8], wtok[:, j * D:j * D + 512],
               reads=["indstrip", "wtok"], pres="pSa", start=(j == 0), stop=(j == 3))
            mm(ST1, pSb[:], indstrip[:, j * 8:(j + 1) * 8], wtok[:, j * D + 512:(j + 1) * D],
               reads=["indstrip", "wtok"], pres="pSb", start=(j == 0), stop=(j == 3))
        p.emit(ST1, "vector", lambda v: v.tensor_copy(stsb[:, 0:512], pSa[:]),
               reads=["pSa"], writes=["stsb"])
        p.emit(ST1, "vector", lambda v: v.tensor_copy(stsb[:, 512:1024], pSb[:]),
               reads=["pSb"], writes=["stsb"])
        p.emit(ST1, "gpsimd",
               lambda g, s=s: g.dma_start(out=states_cm[8 * s:8 * s + 8, :], in_=stsb[:]),
               reads=["stsb"], writes=["states_cm"], dma=True, dsem="stc")
        if "d_st" in dbg_h:
            p.emit(ST1, "gpsimd",
                   lambda g, s=s: g.dma_start(out=dbg_h["d_st"][8 * s:8 * s + 8, :], in_=stsb[:]),
                   reads=["stsb"], dma=True, dsem="dbg")
        for h in range(H):
            pj, pjr = (pSa, "pSa") if h < 8 else (pSb, "pSb")
            mm(ST1, pj[:, (h % 8) * 64:(h % 8) * 64 + 64], dchm[:, h * 8:(h + 1) * 8],
               states_cm[:, h * 64:(h + 1) * 64],
               reads=["dchm", "states_cm"], pres=pjr,
               start=(h % 8 == 0), stop=(h % 8 == 7))
        p.emit(ST1, "vector", lambda v: v.tensor_copy(stpsb[:, 0:512], pSa[:]),
               reads=["pSa"], writes=["stpsb"])
        p.emit(ST1, "vector", lambda v: v.tensor_copy(stpsb[:, 512:1024], pSb[:]),
               reads=["pSb"], writes=["stpsb"])
        if "d_stp" in dbg_h:
            p.emit(ST1, "gpsimd",
                   lambda g, s=s: g.dma_start(out=dbg_h["d_stp"][8 * s:8 * s + 8, :], in_=stpsb[:]),
                   reads=["stpsb"], dma=True, dsem="dbg")
        for j in range(4):
            yo = (j % 2) * D
            yres = f"ytok{j % 2}"
            pc1, rc1, pc2, rc2 = (pW1, "pW1", pW2, "pW2") if j % 2 == 0 else (pC, "pC", pD, "pD")
            mm(ST1, pc1[:], triones[:], wtok[:, j * D:j * D + 512],
               reads=["triones", "wtok"], pres=rc1)
            mm(ST1, pc2[:], triones[:], wtok[:, j * D + 512:(j + 1) * D],
               reads=["triones", "wtok"], pres=rc2)
            for half, pc, rcn in ((0, pc1, rc1), (1, pc2, rc2)):
                p.emit(ST1, "vector",
                       lambda v, j=j, half=half, pc=pc, yo=yo: v.tensor_tensor(
                           out=ytok[:, yo + half * 512:yo + (half + 1) * 512].rearrange("p (h n) -> p h n", h=8),
                           in0=pc[:].rearrange("p (h n) -> p h n", h=8),
                           in1=unqt[:, j * H + half * 8:j * H + (half + 1) * 8, None].to_broadcast((128, 8, 64)),
                           op=ALU.mult),
                       reads=[rcn, "unqt"], writes=[yres])
            mm(ST1, pA[:], indloc[:, j * 128:(j + 1) * 128], stpsb[:, 0:512],
               reads=["indloc", "stpsb"], pres="pA")
            mm(ST1, pB[:], indloc[:, j * 128:(j + 1) * 128], stpsb[:, 512:1024],
               reads=["indloc", "stpsb"], pres="pB")
            for half, pc, rcn in ((0, pA, "pA"), (1, pB, "pB")):
                p.emit(ST1, "vector",
                       lambda v, j=j, half=half, pc=pc: v.tensor_tensor(
                           out=ytmp[:, half * 512:(half + 1) * 512].rearrange("p (h n) -> p h n", h=8),
                           in0=pc[:].rearrange("p (h n) -> p h n", h=8),
                           in1=snqt[:, j * H + half * 8:j * H + (half + 1) * 8, None].to_broadcast((128, 8, 64)),
                           op=ALU.mult),
                       reads=[rcn, "snqt"], writes=["ytmp"])
            p.emit(ST1, "gpsimd",
                   lambda g, yo=yo: g.tensor_tensor(out=ytok[:, yo:yo + D],
                                                    in0=ytok[:, yo:yo + D],
                                                    in1=ytmp[:], op=ALU.add),
                   reads=["ytmp", yres], writes=[yres])
            if "d_Y" in dbg_h:
                p.emit(ST1, "gpsimd",
                       lambda g, j=j, t0=t0, yo=yo: g.dma_start(
                           out=dbg_h["d_Y"][t0 + j * 128:t0 + (j + 1) * 128, :],
                           in_=ytok[:, yo:yo + D]),
                       reads=[yres], dma=True, dsem="dbg")
            for c in range(8):
                pj, pjr = (pW1, "pW1") if (j * 8 + c) % 2 == 0 else (pW2, "pW2")
                tp(ST1, pj[:, 0:128], ytok[:, yo + c * 128:yo + (c + 1) * 128],
                   reads=[yres], pres=pjr)
                p.emit(ST1, "scalar",
                       lambda se, j=j, c=c, pj=pj: se.activation(
                           out=yfmb[:, c * TS1 + j * 128:c * TS1 + (j + 1) * 128],
                           in_=pj[:, 0:128], func=AF.Identity),
                       reads=[pjr], writes=["yfmb"])

        for m in range(8):
            pm, pmr = (pA, "pA") if m % 2 == 0 else (pB, "pB")
            for k in range(8):
                mm(ST1, pm[:], wobf[:, k * D + m * 128:k * D + (m + 1) * 128],
                   yfmb[:, k * TS1:(k + 1) * TS1],
                   reads=["wobf", "yfmb"], pres=pmr, start=(k == 0), stop=(k == 7))
            p.emit(ST1, "scalar",
                   lambda se, pm=pm: se.activation(out=attfm[:], in_=pm[:], func=AF.Identity),
                   reads=[pmr], writes=["attfm"])
            for j in range(4):
                pj, pjr = (pC, "pC") if j % 2 == 0 else (pD, "pD")
                tp(ST1, pj[:, 0:128], attfm[:, j * 128:(j + 1) * 128],
                   reads=["attfm"], pres=pjr)
                p.emit(ST1, "vector",
                       lambda v, m=m, j=j, pj=pj: v.tensor_scalar(
                           out=atttm[:, j * D + m * 128:j * D + (m + 1) * 128],
                           in0=pj[:, 0:128], scalar1=1.0, scalar2=0.0,
                           op0=ALU.mult, op1=ALU.add, accum_out=sc[:, 8:9]),
                       reads=[pjr], writes=["atttm", "sc_p2"])
                if m == 0:
                    p.emit(ST1, "vector",
                           lambda v, j=j: v.tensor_copy(sc[:, 9 + j:10 + j], sc[:, 8:9]),
                           reads=["sc_p2"], writes=["sc_s2"])
                else:
                    p.emit(ST1, "vector",
                           lambda v, j=j: v.tensor_tensor(out=sc[:, 9 + j:10 + j],
                                                          in0=sc[:, 9 + j:10 + j],
                                                          in1=sc[:, 8:9], op=ALU.add),
                           reads=["sc_p2", "sc_s2"], writes=["sc_s2"])
        if "d_attn" in dbg_h:
            p.emit(ST1, "gpsimd",
                   lambda g, t0=t0: g.dma_start(
                       out=dbg_h["d_attn"][t0:t0 + TS1, :].rearrange("(j p) d -> p j d", p=128),
                       in_=atttm.rearrange("p (j d) -> p j d", j=4)),
                   reads=["atttm"], dma=True, dsem="dbg")
        for j in range(4):
            p.emit(ST1, "vector",
                   lambda v, j=j: v.scalar_tensor_tensor(out=junkf[:, 0:D],
                                                         in0=atttm[:, j * D:(j + 1) * D], scalar=1.0,
                                                         in1=atttm[:, j * D:(j + 1) * D],
                                                         op0=ALU.mult, op1=ALU.mult,
                                                         accum_out=sc[:, 13:14]),
                   reads=["atttm"], writes=["sc_q2"])
            p.emit(ST1, "vector",
                   lambda v, j=j: v.tensor_scalar(out=sc[:, 14:15], in0=sc[:, 9 + j:10 + j],
                                                  scalar1=1.0 / 1024.0, scalar2=None, op0=ALU.mult),
                   reads=["sc_s2"], writes=["sc_m2"])
            p.emit(ST1, "vector",
                   lambda v: v.scalar_tensor_tensor(out=sc[:, 15:16], in0=sc[:, 14:15], scalar=-1.0,
                                                    in1=sc[:, 14:15], op0=ALU.mult, op1=ALU.mult),
                   reads=["sc_m2"], writes=["sc_r2"])
            p.emit(ST1, "vector",
                   lambda v: v.scalar_tensor_tensor(out=sc[:, 15:16], in0=sc[:, 13:14],
                                                    scalar=1.0 / 1024.0, in1=sc[:, 15:16],
                                                    op0=ALU.mult, op1=ALU.add),
                   reads=["sc_q2", "sc_r2"], writes=["sc_r2"])
            rsqrt_cols(ST1, sc[:, 15:16], "sc_r2")
            p.emit(ST1, "vector",
                   lambda v: v.scalar_tensor_tensor(out=sc[:, 16:17], in0=sc[:, 14:15], scalar=-1.0,
                                                    in1=sc[:, 15:16], op0=ALU.mult, op1=ALU.mult),
                   reads=["sc_m2", "sc_r2"], writes=["sc_nm2"])
            p.emit(ST1, "scalar",
                   lambda se, j=j: se.activation(out=ln2t[:], in_=atttm[:, j * D:(j + 1) * D],
                                                 func=AF.Identity, scale=sc[:, 15:16],
                                                 bias=sc[:, 16:17]),
                   reads=["atttm", "sc_r2", "sc_nm2"], writes=["ln2t"])
            xb = j % 2
            p.emit(ST1, "gpsimd",
                   lambda g, j=j, xb=xb: g.tensor_tensor(out=x1t[:, xb * D:(xb + 1) * D],
                                                         in0=x0tm[:, j * D:(j + 1) * D],
                                                         in1=ln2t[:], op=ALU.add),
                   reads=["x0tm", "ln2t"], writes=[f"x1t{xb}"])
            p.emit(ST1, "gpsimd",
                   lambda g, j=j, xb=xb, t0=t0: g.dma_start(
                       out=x1_h[t0 + j * 128:t0 + (j + 1) * 128, :],
                       in_=x1t[:, xb * D:(xb + 1) * D]),
                   reads=[f"x1t{xb}"], dma=True, dsem=f"x1st{xb}")
            if "d_x1" in dbg_h:
                p.emit(ST1, "gpsimd",
                       lambda g, j=j, xb=xb, t0=t0: g.dma_start(
                           out=dbg_h["d_x1"][t0 + j * 128:t0 + (j + 1) * 128, :],
                           in_=x1t[:, xb * D:(xb + 1) * D]),
                       reads=[f"x1t{xb}"], dma=True, dsem="dbg")

    if "d_w1t" in dbg_h:
        p.emit(ST1, "sync", lambda e: e.dma_start(out=junkb[:], in_=w1t_h[0:128, 0:1024]),
               writes=["junkb_dbg"], dma=True, dsem="dbg")
        p.emit(ST1, "gpsimd", lambda g: g.dma_start(out=dbg_h["d_w1t"][:, 0:1024], in_=junkb[:]),
               reads=["junkb_dbg"], dma=True, dsem="dbg")
        p.emit(ST1, "sync", lambda e: e.dma_start(out=junkb[:], in_=w1t_h[0:128, 1024:2048]),
               writes=["junkb_dbg"], dma=True, dsem="dbg")
        p.emit(ST1, "gpsimd", lambda g: g.dma_start(out=dbg_h["d_w1t"][:, 1024:2048], in_=junkb[:]),
               reads=["junkb_dbg"], dma=True, dsem="dbg")

    run_stage(ST1)
    s1stack.close()

    w1tb = sb2("w1tb", [128, 8 * FF], BF16)
    w2tb = sb2("w2tb", [128, 32 * D], BF16)
    x1l = sb2("x1l", [128, 2 * D], F32)
    x1lnf = sb2("x1lnf", [128, 2 * D], F32)
    x1fmb = sb2("x1fmb", [128, 8 * TS2], BF16)
    hbuf = sb2("hbuf", [128, 2 * TS2], BF16)
    ffmf = sb2("ffmf", [128, TS2], F32)
    ffntm = sb2("ffntm", [128, 2 * D], F32)
    ln4t = sb2("ln4t", [128, D], F32)
    x2t = sb2("x2t", [128, 2 * D], F32)
    st4 = sb2("st4", [128, 16], F32)
    pF = [ps2(f"pF{i}", [128, 512], F32) for i in range(4)]
    pG1 = ps2("pG1", [128, 512], F32)
    pG2 = ps2("pG2", [128, 512], F32)
    pT1 = ps2("pT1", [128, 512], F32)
    pT2 = ps2("pT2", [128, 512], F32)

    barrier = p.snapshot()
    for en in ENGINES:
        if en in ("vector", "gpsimd"):
            p.emit(ST2, en, lambda e: e.engine_nop(), extra_waits=barrier)
        else:
            p.emit(ST2, en, lambda e: e.nop(), extra_waits=barrier)

    p.emit(ST2, "sync", lambda e: e.dma_start(out=w1tb.rearrange("p (k f) -> p k f", k=8),
                                              in_=w1t_h.rearrange("(k p) f -> p k f", p=128)),
           writes=["w1tb"], dma=True, dsem="w1tb")
    p.emit(ST2, "sync", lambda e: e.dma_start(out=w2tb.rearrange("p (k d) -> p k d", k=32),
                                              in_=w2t_h.rearrange("(k p) d -> p k d", p=128)),
           writes=["w2tb"], dma=True, dsem="w2tb")

    for s in range(NS2):
        t0 = s * TS2
        p.emit(ST2, "sync",
               lambda e, t0=t0: e.dma_start(out=x1l.rearrange("p (j d) -> p j d", j=2),
                                            in_=x1_h[t0:t0 + TS2, :].rearrange("(j p) d -> p j d", p=128)),
               writes=["x1l"], dma=True, dsem="x1l")
        for j in range(2):
            jj = s * 2 + j
            p.emit(ST2, "vector",
                   lambda v, j=j: v.scalar_tensor_tensor(out=junkf[:, 0:D],
                                                         in0=x1l[:, j * D:(j + 1) * D], scalar=1.0,
                                                         in1=x1l[:, j * D:(j + 1) * D],
                                                         op0=ALU.mult, op1=ALU.mult,
                                                         accum_out=st4[:, 0:1]),
                   reads=["x1l"], writes=["st_q3"])
            p.emit(ST2, "vector",
                   lambda v, jj=jj: v.scalar_tensor_tensor(out=st4[:, 1:2],
                                                           in0=m1s[:, jj:jj + 1], scalar=-1.0,
                                                           in1=m1s[:, jj:jj + 1],
                                                           op0=ALU.mult, op1=ALU.mult),
                   reads=["m1s"], writes=["st_v3"])
            p.emit(ST2, "vector",
                   lambda v: v.scalar_tensor_tensor(out=st4[:, 2:3], in0=st4[:, 0:1],
                                                    scalar=1.0 / 1024.0, in1=st4[:, 1:2],
                                                    op0=ALU.mult, op1=ALU.add),
                   reads=["st_q3", "st_v3"], writes=["st_r3"])
            rsqrt_cols(ST2, st4[:, 2:3], "st_r3")
            p.emit(ST2, "vector",
                   lambda v, jj=jj: v.scalar_tensor_tensor(out=st4[:, 3:4],
                                                           in0=m1s[:, jj:jj + 1], scalar=-1.0,
                                                           in1=st4[:, 2:3],
                                                           op0=ALU.mult, op1=ALU.mult),
                   reads=["m1s", "st_r3"], writes=["st_nm3"])
            p.emit(ST2, "scalar",
                   lambda se, j=j: se.activation(out=x1lnf[:, j * D:(j + 1) * D],
                                                 in_=x1l[:, j * D:(j + 1) * D],
                                                 func=AF.Identity, scale=st4[:, 2:3],
                                                 bias=st4[:, 3:4]),
                   reads=["x1l", "st_r3", "st_nm3"], writes=["x1lnf"])
        for j in range(2):
            for c in range(8):
                pj, pjr = (pT1, "pT1") if (j * 8 + c) % 2 == 0 else (pT2, "pT2")
                tp(ST2, pj[:, 0:128], x1lnf[:, j * D + c * 128:j * D + (c + 1) * 128],
                   reads=["x1lnf"], pres=pjr)
                p.emit(ST2, "scalar",
                       lambda se, j=j, c=c, pj=pj: se.activation(
                           out=x1fmb[:, c * TS2 + j * 128:c * TS2 + (j + 1) * 128],
                           in_=pj[:, 0:128], func=AF.Identity),
                       reads=[pjr], writes=["x1fmb"])
        if "d_x1ln" in dbg_h:
            p.emit(ST2, "gpsimd",
                   lambda g, t0=t0: g.dma_start(
                       out=dbg_h["d_x1ln"][:, t0:t0 + TS2].rearrange("(c p) t -> p c t", p=128),
                       in_=x1fmb.rearrange("p (c t) -> p c t", c=8)),
                   reads=["x1fmb"], dma=True, dsem="dbg")
        for f in range(32):
            pg, pgr = (pG1, "pG1") if f % 2 == 0 else (pG2, "pG2")
            for k in range(8):
                mm(ST2, pg[:, 0:TS2], w1tb[:, k * FF + f * 128:k * FF + (f + 1) * 128],
                   x1fmb[:, k * TS2:(k + 1) * TS2],
                   reads=["w1tb", "x1fmb"], pres=pgr, start=(k == 0), stop=(k == 7))
            fb = f % 2
            p.emit(ST2, "scalar",
                   lambda se, f=f, pg=pg, fb=fb: se.activation(
                       out=hbuf[:, fb * TS2:(fb + 1) * TS2], in_=pg[:, 0:TS2],
                       func=AF.Gelu, scale=np.float32(0.5), bias=b1_c[:, f:f + 1]),
                   reads=[pgr, "b1_c"], writes=[f"hb{fb}"])
            if "d_h" in dbg_h and s == 0 and f == 0:
                p.emit(ST2, "gpsimd",
                       lambda g, fb=fb: g.dma_start(out=dbg_h["d_h"][:, 0:TS2],
                                                    in_=hbuf[:, fb * TS2:(fb + 1) * TS2]),
                       reads=[f"hb{fb}"], dma=True, dsem="dbg")
            for m in range(8):
                mm(ST2, pF[m // 2][:, (m % 2) * TS2:(m % 2 + 1) * TS2],
                   w2tb[:, f * D + m * 128:f * D + (m + 1) * 128],
                   hbuf[:, fb * TS2:(fb + 1) * TS2],
                   reads=["w2tb", f"hb{fb}"], pres=f"pF{m // 2}",
                   start=(f == 0 and m % 2 == 0), stop=(f == 31))
        for m in range(8):
            p.emit(ST2, "scalar",
                   lambda se, m=m: se.activation(out=ffmf[:],
                                                 in_=pF[m // 2][:, (m % 2) * TS2:(m % 2 + 1) * TS2],
                                                 func=AF.Identity, scale=np.float32(0.5),
                                                 bias=b2_c[:, m:m + 1]),
                   reads=[f"pF{m // 2}", "b2_c"], writes=["ffmf"])
            for j in range(2):
                pj, pjr = (pT1, "pT1") if (m * 2 + j) % 2 == 0 else (pT2, "pT2")
                tp(ST2, pj[:, 0:128], ffmf[:, j * 128:(j + 1) * 128], reads=["ffmf"], pres=pjr)
                p.emit(ST2, "vector",
                       lambda v, m=m, j=j, pj=pj: v.tensor_scalar(
                           out=ffntm[:, j * D + m * 128:j * D + (m + 1) * 128],
                           in0=pj[:, 0:128], scalar1=1.0, scalar2=0.0,
                           op0=ALU.mult, op1=ALU.add, accum_out=st4[:, 8:9]),
                       reads=[pjr], writes=["ffntm", "st_p4"])
                if m == 0:
                    p.emit(ST2, "vector",
                           lambda v, j=j: v.tensor_copy(st4[:, 4 + j:5 + j], st4[:, 8:9]),
                           reads=["st_p4"], writes=["st_s4"])
                else:
                    p.emit(ST2, "vector",
                           lambda v, j=j: v.tensor_tensor(out=st4[:, 4 + j:5 + j],
                                                          in0=st4[:, 4 + j:5 + j],
                                                          in1=st4[:, 8:9], op=ALU.add),
                           reads=["st_p4", "st_s4"], writes=["st_s4"])
        for j in range(2):
            p.emit(ST2, "vector",
                   lambda v, j=j: v.scalar_tensor_tensor(out=junkf[:, 0:D],
                                                         in0=ffntm[:, j * D:(j + 1) * D], scalar=1.0,
                                                         in1=ffntm[:, j * D:(j + 1) * D],
                                                         op0=ALU.mult, op1=ALU.mult,
                                                         accum_out=st4[:, 6:7]),
                   reads=["ffntm"], writes=["st_q4"])
            p.emit(ST2, "vector",
                   lambda v, j=j: v.tensor_scalar(out=st4[:, 9:10], in0=st4[:, 4 + j:5 + j],
                                                  scalar1=1.0 / 1024.0, scalar2=None, op0=ALU.mult),
                   reads=["st_s4"], writes=["st_m4"])
            p.emit(ST2, "vector",
                   lambda v: v.scalar_tensor_tensor(out=st4[:, 10:11], in0=st4[:, 9:10],
                                                    scalar=-1.0, in1=st4[:, 9:10],
                                                    op0=ALU.mult, op1=ALU.mult),
                   reads=["st_m4"], writes=["st_r4"])
            p.emit(ST2, "vector",
                   lambda v: v.scalar_tensor_tensor(out=st4[:, 10:11], in0=st4[:, 6:7],
                                                    scalar=1.0 / 1024.0, in1=st4[:, 10:11],
                                                    op0=ALU.mult, op1=ALU.add),
                   reads=["st_q4", "st_r4"], writes=["st_r4"])
            rsqrt_cols(ST2, st4[:, 10:11], "st_r4")
            p.emit(ST2, "vector",
                   lambda v: v.scalar_tensor_tensor(out=st4[:, 11:12], in0=st4[:, 9:10],
                                                    scalar=-1.0, in1=st4[:, 10:11],
                                                    op0=ALU.mult, op1=ALU.mult),
                   reads=["st_m4", "st_r4"], writes=["st_nm4"])
            p.emit(ST2, "scalar",
                   lambda se, j=j: se.activation(out=ln4t[:], in_=ffntm[:, j * D:(j + 1) * D],
                                                 func=AF.Identity, scale=st4[:, 10:11],
                                                 bias=st4[:, 11:12]),
                   reads=["ffntm", "st_r4", "st_nm4"], writes=["ln4t"])
            p.emit(ST2, "gpsimd",
                   lambda g, j=j: g.tensor_tensor(out=x2t[:, j * D:(j + 1) * D],
                                                  in0=x1l[:, j * D:(j + 1) * D],
                                                  in1=ln4t[:], op=ALU.add),
                   reads=["x1l", "ln4t"], writes=["x2t"])
            p.emit(ST2, "gpsimd",
                   lambda g, j=j, t0=t0: g.dma_start(out=x2_h[t0 + j * 128:t0 + (j + 1) * 128, :],
                                                     in_=x2t[:, j * D:(j + 1) * D]),
                   reads=["x2t"], dma=True, dsem=f"x2st{j}")

    final_waits = [(sem, v) for sem, v in p.cnt.items() if sem.startswith("q_")]
    p.emit(ST2, "gpsimd", lambda g: g.engine_nop(), extra_waits=final_waits)

    run_stage(ST2)
    s2stack.close()
    stack0.close()
    return nc


def _host_inputs(inputs):
    F = np.float32
    x = np.asarray(inputs["L0_attn_x"], F)
    ipw = np.asarray(inputs["L0_attn_in_proj_w"], F)
    base = {
        "winT": np.ascontiguousarray(ipw[:1024].T),
        "winwT": np.ascontiguousarray(ipw[1024:].T),
        "woT": np.ascontiguousarray(np.asarray(inputs["L0_attn_out_proj_w"], F).T),
        "w1T": np.ascontiguousarray(np.asarray(inputs["L0_ffn_W1"], F).T),
        "w2T": np.ascontiguousarray(np.asarray(inputs["L0_ffn_W2"], F).T),
        "convw_sb": np.ascontiguousarray(
            np.asarray(inputs["L0_attn_conv_w"], F).reshape(8, 128, 4)
            .transpose(1, 0, 2).reshape(128, 32)),
        "convb_sb": np.ascontiguousarray(np.asarray(inputs["L0_attn_conv_b"], F).reshape(8, 128).T),
        "wbase_sb": np.ascontiguousarray(np.asarray(inputs["L0_attn_w_base"], F).reshape(H, 1)),
        "b1_sb": np.ascontiguousarray(np.asarray(inputs["L0_ffn_b1"], F).reshape(32, 128).T),
        "b2_sb": np.ascontiguousarray(np.asarray(inputs["L0_ffn_b2"], F).reshape(8, 128).T),
    }
    maps = []
    for b in range(x.shape[0]):
        m = dict(base)
        m["x0"] = np.ascontiguousarray(x[b])
        m["x0T"] = np.ascontiguousarray(x[b].T)
        maps.append(m)
    return maps


def kernel(**inputs):
    in_maps = _host_inputs(inputs)
    nc = build_program()
    res = run_bass_kernel_spmd(nc, in_maps, core_ids=list(range(8)))
    return np.stack([np.asarray(r["x2"], np.float32) for r in res.results])


# version 04 (final)
# speedup vs baseline: 1.9212x; 1.1503x over previous
import numpy as np
import ml_dtypes
import concourse.bass as bass
import concourse.mybir as mybir
from concourse.bass_utils import run_bass_kernel_spmd

F32 = mybir.dt.float32
BF16 = mybir.dt.bfloat16
AF = mybir.ActivationFunctionType
ALU = mybir.AluOpType
AX = mybir.AxisListType
BF = ml_dtypes.bfloat16

S, D, H, CH = 4096, 1024, 16, 64
NCHUNK = S // CH
TS1, TS2 = 512, 256
NS1, NS2 = S // TS1, S // TS2
FF = 4096
EPS = 1e-5
ENGINES = ("sync", "scalar", "vector", "tensor", "gpsimd")


class Plan:

    def __init__(self):
        self.ops = []
        self.cnt = {}
        self.res = {}

    def emit(self, stage, engine, fn, reads=(), writes=(), dma=False, extra_waits=(),
             dsem=None):
        if dma:
            sem = "q_" + (dsem if dsem is not None else engine)
        else:
            sem = "e_" + engine
        waits = {}

        def addw(mark, war=False):
            if mark is None:
                return
            s, v = mark
            if war and (not dma) and s == "e_" + engine:
                return
            if v > waits.get(s, -1):
                waits[s] = v

        wset = set(writes)
        for r in reads:
            rr = self.res.get(r)
            if rr:
                addw(rr["w"])
        for r in writes:
            rr = self.res.get(r)
            if rr:
                addw(rr["w"], war=True)
                for m in rr["r"]:
                    addw(m, war=True)
        for m in extra_waits:
            addw(m)
        if engine == "gpsimd" and not dma:
            prev = self.cnt.get("e_gpsimd", 0)
            if prev > waits.get("e_gpsimd", -1):
                waits["e_gpsimd"] = prev
        inc_by = 16 if dma else 1
        self.cnt[sem] = self.cnt.get(sem, 0) + inc_by
        mark = (sem, self.cnt[sem])
        self.ops.append((stage, engine, sorted(waits.items()), fn, sem, inc_by))
        for r in reads:
            if r in wset:
                continue
            self.res.setdefault(r, {"w": None, "r": []})["r"].append(mark)
        for r in writes:
            self.res[r] = {"w": mark, "r": []}
        return mark

    def snapshot(self):
        return [(s, v) for s, v in self.cnt.items()]


def build_program(debug=()):
    nc = bass.Bass("TRN2", target_bir_lowering=False)
    p = Plan()

    x0_h = nc.dram_tensor("x0", [S, D], F32, kind="ExternalInput").ap()
    x0T_h = nc.dram_tensor("x0T", [D, S], F32, kind="ExternalInput").ap()
    winT_h = nc.dram_tensor("winT", [D, D], F32, kind="ExternalInput").ap()
    winwT_h = nc.dram_tensor("winwT", [D, H], F32, kind="ExternalInput").ap()
    woT_h = nc.dram_tensor("woT", [D, D], F32, kind="ExternalInput").ap()
    w1T_h = nc.dram_tensor("w1T", [D, FF], F32, kind="ExternalInput").ap()
    w2T_h = nc.dram_tensor("w2T", [FF, D], F32, kind="ExternalInput").ap()
    convw_h = nc.dram_tensor("convw_sb", [128, 32], F32, kind="ExternalInput").ap()
    convb_h = nc.dram_tensor("convb_sb", [128, 8], F32, kind="ExternalInput").ap()
    wbase_h = nc.dram_tensor("wbase_sb", [H, 1], F32, kind="ExternalInput").ap()
    b1_h = nc.dram_tensor("b1_sb", [128, 32], F32, kind="ExternalInput").ap()
    b2_h = nc.dram_tensor("b2_sb", [128, 8], F32, kind="ExternalInput").ap()
    x2_h = nc.dram_tensor("x2", [S, D], F32, kind="ExternalOutput").ap()
    x1_h = nc.dram_tensor("x1_scr", [S, D], F32, kind="Internal").ap()
    w1t_h = nc.dram_tensor("w1t_scr", [D, FF], BF16, kind="Internal").ap()
    w2t_h = nc.dram_tensor("w2t_scr", [FF, D], BF16, kind="Internal").ap()
    rm_h = nc.dram_tensor("rm_scr", [H * 8], F32, kind="Internal").ap()
    nd_h = nc.dram_tensor("nd_scr", [128], F32, kind="Internal").ap()
    dbg_h = {}
    for k, shape, dt in (
        ("d_xi", [D, S], BF16), ("d_X", [S, D], F32), ("d_A", [H, S], F32),
        ("d_Y", [S, D], F32), ("d_attn", [S, D], F32), ("d_x1", [S, D], F32),
        ("d_un", [H, S], F32), ("d_sn", [H, S], F32), ("d_st", [NCHUNK, D], BF16),
        ("d_stp", [NCHUNK, D], BF16), ("d_w1t", [128, 2048], BF16),
        ("d_x1ln", [D, S], BF16), ("d_h", [128, 2048], BF16),
        ("d_m1", [128, 32], F32), ("d_r1", [128, 32], F32),
        ("d_mr1b", [128, TS1], F32), ("d_r1b", [128, TS1], F32),
        ("d_x0fmb", [128, 8 * TS1], BF16), ("d_acs", [H, S], F32),
        ("d_vv", [H, S], F32), ("d_cvv", [H, S], F32), ("d_sdo", [H, S], F32),
        ("d_nrm", [H, S], F32), ("d_wtok", [S, D], BF16), ("d_ytmp", [TS1, D], F32),
    ):
        if k in debug:
            dbg_h[k] = nc.dram_tensor(k, shape, dt, kind="ExternalOutput").ap()

    from contextlib import ExitStack
    stack0 = ExitStack()
    s1stack = ExitStack()
    s2stack = ExitStack()
    DSEMS = ("sync", "gpsimd", "x0tm", "x0fmb", "wstage0", "wstage1", "winbf",
             "winwbf", "wobf", "convw", "convb", "wbase", "b1", "b2", "x1l",
             "w1tb", "w2tb", "stc", "rmw", "rmr", "ndw", "ndr", "st", "dbg",
             "x1st0", "x1st1", "x2st0", "x2st1", "ternst", "xf0", "xf1")
    sems = {}
    for en in ENGINES:
        sems["e_" + en] = stack0.enter_context(nc.semaphore("e_" + en))
    for dn in DSEMS:
        sems["q_" + dn] = stack0.enter_context(nc.semaphore("q_" + dn))

    ST1, ST2 = 1, 2

    def sb0(name, shape, dt):
        return stack0.enter_context(nc.sbuf_tensor(name, shape, dt))

    def sb1(name, shape, dt):
        return s1stack.enter_context(nc.sbuf_tensor(name, shape, dt))

    def sb2(name, shape, dt):
        return s2stack.enter_context(nc.sbuf_tensor(name, shape, dt))

    def ps1(name, shape, dt):
        return s1stack.enter_context(nc.psum_tensor(name, shape, dt))

    def ps2(name, shape, dt):
        return s2stack.enter_context(nc.psum_tensor(name, shape, dt))

    block = stack0.enter_context(nc.Block())
    hw = {e: {} for e in ENGINES}

    def run_stage(stage):
        def mk(engine):
            def runner(eng):
                for stg, en, waits, fn, sem, inc_by in p.ops:
                    if stg != stage or en != engine:
                        continue
                    for sname, val in waits:
                        if hw[engine].get(sname, 0) >= val:
                            continue
                        hw[engine][sname] = val
                        eng.wait_ge(sems[sname], val)
                    fn(eng).then_inc(sems[sem], inc_by)
            return runner
        block.sync(mk("sync"))
        block.scalar(mk("scalar"))
        block.vector(mk("vector"))
        block.tensor(mk("tensor"))
        block.gpsimd(mk("gpsimd"))

    ones_f = sb0("ones_f", [128, 1], F32)
    onesr = sb0("onesr", [1, 128], F32)
    ones_b = sb0("ones_b", [128, 1], BF16)
    iden_f = sb0("iden_f", [128, 128], F32)
    convw_r = sb0("convw_r", [128, 32], F32)
    convb_c = sb0("convb_c", [128, 8], F32)
    wbase_c = sb0("wbase_c", [H, 1], F32)
    b1_c = sb0("b1_c", [128, 32], F32)
    b2_c = sb0("b2_c", [128, 8], F32)
    m1s = sb0("m1s", [128, 32], F32)
    junkf = sb0("junkf", [128, 1024], F32)
    junkb = sb0("junkb", [128, 1024], BF16)

    triones = sb1("triones", [128, 128], BF16)
    indstrip = sb1("indstrip", [128, 32], BF16)
    indloc = sb1("indloc", [8, 4 * 128], BF16)
    e4 = sb1("e4", [4, 4 * 128], F32)
    convdg = sb1("convdg", [128, 32 * 128], BF16)
    winbf = sb1("winbf", [128, 8 * D], BF16)
    winwbf = sb1("winwbf", [128, 8 * H], BF16)
    wobf = sb1("wobf", [128, 8 * D], BF16)
    nwinsum = sb1("nwinsum", [128, 8], F32)
    nwinwsum = sb1("nwinwsum", [H, 1], F32)
    r1s = sb1("r1s", [128, 32], F32)
    states_cm = sb1("states_cm", [NCHUNK, D], BF16)
    cend_t = sb1("cend_t", [H, 1 + NCHUNK], F32)
    mrc_t = sb1("mrc_t", [H, 1 + NCHUNK], F32)
    c1p_t = sb1("c1p_t", [H, NCHUNK], F32)
    ndu_t = sb1("ndu_t", [H, NCHUNK], F32)
    x0tm = sb1("x0tm", [128, 4 * D], F32)
    x0fmb = sb1("x0fmb", [128, 8 * TS1], BF16)
    xib = sb1("xib", [128, 8 * (TS1 + 3)], BF16)
    xstg = sb1("xstg", [128, 4 * 128], F32)
    xtok = sb1("xtok", [128, 4 * D], BF16)
    wtok = sb1("wtok", [128, 4 * D], BF16)
    ytok = sb1("ytok", [128, 2 * D], F32)
    ytmp = sb1("ytmp", [128, D], F32)
    yfmb = sb1("yfmb", [128, 8 * TS1], BF16)
    attfm = sb1("attfm", [128, TS1], F32)
    atttm = sb1("atttm", [128, 4 * D], F32)
    ln2t = sb1("ln2t", [128, D], F32)
    x1t = sb1("x1t", [128, 2 * D], F32)
    mr1b = sb1("mr1b", [128, TS1], F32)
    r1b = sb1("r1b", [128, TS1], F32)
    e1t = sb1("e1t", [128, 2 * TS1], F32)
    wfmA = sb1("wfmA", [H, TS1], F32)
    acs = sb1("acs", [H, TS1], F32)
    rmin = sb1("rmin", [H, TS1], F32)
    vv = sb1("vv", [H, TS1], F32)
    cvv = sb1("cvv", [H, TS1], F32)
    uu = sb1("uu", [H, TS1], F32)
    sdo = sb1("sdo", [H, TS1], F32)
    nrm = sb1("nrm", [H, TS1], F32)
    noff = sb1("noff", [H, TS1], F32)
    unq = sb1("unq", [H, TS1], F32)
    snq = sb1("snq", [H, TS1], F32)
    zer = sb1("zer", [H, CH], F32)
    sm8a = sb1("sm8a", [H, 8], F32)
    sm8b = sb1("sm8b", [H, 8], F32)
    ndf = sb1("ndf", [H, 8], F32)
    vqt = sb1("vqt", [128, 4 * H], F32)
    unqt = sb1("unqt", [128, 4 * H], F32)
    snqt = sb1("snqt", [128, 4 * H], F32)
    c1pT = sb1("c1pT", [NCHUNK, H], F32)
    nduT = sb1("nduT", [NCHUNK, H], F32)
    rmrow = sb1("rmrow", [1, 128], F32)
    dD = sb1("dD", [NCHUNK, 128], F32)
    dchb = sb1("dchb", [NCHUNK, 128], BF16)
    dchm = sb1("dchm", [NCHUNK, 128], BF16)
    prod = sb1("prod", [NCHUNK, 128], F32)
    ndcol = sb1("ndcol", [128, 1], F32)
    stsb = sb1("stsb", [8, D], BF16)
    stpsb = sb1("stpsb", [8, D], BF16)
    strow = sb1("strow", [128, 8], F32)
    mrT = sb1("mrT", [4, 128], F32)
    rT = sb1("rT", [4, 128], F32)
    sc = sb1("sc", [128, 24], F32)
    sc2 = sb1("sc2", [1, 8], F32)
    wstage = sb1("wstage", [128, 2 * 1024], F32)
    tsg1 = sb1("tsg1", [128, 1024], BF16)
    tsg2 = sb1("tsg2", [128, 1024], BF16)
    tout = sb1("tout", [128, 1024], BF16)
    absacc = sb1("absacc", [128, 2], F32)
    abscol = sb1("abscol", [128, 2], F32)
    invsc = sb1("invsc", [128, 2], F32)

    pA = ps1("pA", [128, 512], F32)
    pB = ps1("pB", [128, 512], F32)
    pC = ps1("pC", [128, 512], F32)
    pD = ps1("pD", [128, 512], F32)
    pW1 = ps1("pW1", [128, 512], F32)
    pW2 = ps1("pW2", [128, 512], F32)
    pSa = ps1("pSa", [8, 512], F32)
    pSb = ps1("pSb", [8, 512], F32)

    def mm(stage, out_ap, lhsT_ap, rhs_ap, reads, pres, start=True, stop=True,
           extra=()):
        return p.emit(stage, "tensor",
                      lambda t, o=out_ap, l=lhsT_ap, r=rhs_ap, s1=start, s2=stop:
                      t.matmul(o, l, r, start=s1, stop=s2, skip_group_check=True),
                      reads=reads, writes=[pres], extra_waits=extra)

    def tp(stage, out_ap, in_ap, reads, pres):
        k = in_ap.shape[0]
        return p.emit(stage, "tensor",
                      lambda t, o=out_ap, i=in_ap, k=k: t.transpose(o, i, iden_f[0:k, 0:k]),
                      reads=list(reads) + ["iden_f"], writes=[pres])

    def rsqrt_cols(stage, io_ap, rname):
        p.emit(stage, "scalar",
               lambda se: se.activation(out=io_ap, in_=io_ap, func=AF.Ln, bias=np.float32(EPS)),
               reads=[rname], writes=[rname])
        p.emit(stage, "scalar",
               lambda se: se.activation(out=io_ap, in_=io_ap, func=AF.Exp, scale=np.float32(-0.5)),
               reads=[rname], writes=[rname])

    PH = ST1
    for dst, src, res, dsn in ((convw_r[:], convw_h, "convw_r", "convw"),
                               (convb_c[:], convb_h, "convb_c", "convb"),
                               (wbase_c[:], wbase_h, "wbase_c", "wbase"),
                               (b1_c[:], b1_h, "b1_c", "b1"),
                               (b2_c[:], b2_h, "b2_c", "b2")):
        p.emit(PH, "sync", lambda e, d=dst, s=src: e.dma_start(out=d, in_=s),
               writes=[res], dma=True, dsem=dsn)
    for tt, rr, val in ((ones_f, "ones_f", 1.0), (onesr, "onesr", 1.0),
                        (ones_b, "ones_b", 1.0), (zer, "zer", 0.0),
                        (cend_t, "cend_t", 0.0), (mrc_t, "mrc_t", 0.0),
                        (c1p_t, "c1p_t", 0.0), (ndu_t, "ndu_t", 0.0),
                        (absacc, "absacc", 0.0), (indstrip, "indstrip", 0.0)):
        p.emit(PH, "vector", lambda v, t=tt, x=val: v.memset(t[:], x), writes=[rr])
    p.emit(PH, "vector", lambda v: v.memset(states_cm[:], 0.0), writes=["states_cm"])
    p.emit(PH, "vector", lambda v: v.memset(xib[:], 0.0),
           writes=[f"xi{m}" for m in range(8)] + [f"xihalo{m}" for m in range(8)])
    p.emit(PH, "gpsimd",
           lambda g: g.affine_select(out=iden_f[:], in_=ones_f[:, 0:1].to_broadcast((128, 128)),
                                     pattern=[[1, 128]], compare_op=ALU.is_equal,
                                     fill=0.0, base=0, channel_multiplier=-1),
           reads=["ones_f"], writes=["iden_f"])
    p.emit(PH, "gpsimd",
           lambda g: g.affine_select(out=triones[:], in_=ones_b[:, 0:1].to_broadcast((128, 128)),
                                     pattern=[[1, 128]], compare_op=ALU.is_ge,
                                     fill=0.0, base=0, channel_multiplier=-1),
           reads=["ones_b"], writes=["triones"])
    p.emit(PH, "vector", lambda v: v.memset(triones[0:64, 64:128], 0.0), writes=["triones"])
    for cp in range(4):
        p.emit(PH, "vector",
               lambda v, cp=cp: v.memset(indstrip[0:64, cp * 8 + 2 * cp:cp * 8 + 2 * cp + 1], 1.0),
               writes=["indstrip"])
        p.emit(PH, "vector",
               lambda v, cp=cp: v.memset(indstrip[64:128, cp * 8 + 2 * cp + 1:cp * 8 + 2 * cp + 2], 1.0),
               writes=["indstrip"])
        p.emit(PH, "gpsimd",
               lambda g, cp=cp: g.affine_select(
                   out=indloc[:, cp * 128:(cp + 1) * 128].rearrange("p (a b) -> p a b", a=2),
                   in_=ones_b[0:8, 0:1].to_broadcast((8, 2, 64)),
                   pattern=[[1, 2], [0, 64]], compare_op=ALU.is_equal,
                   fill=0.0, base=2 * cp, channel_multiplier=-1),
               reads=["ones_b"], writes=["indloc"])
    for j in range(4):
        p.emit(PH, "gpsimd",
               lambda g, j=j: g.affine_select(out=e4[:, j * 128:(j + 1) * 128],
                                              in_=ones_f[0:4, 0:1].to_broadcast((4, 128)),
                                              pattern=[[0, 128]], compare_op=ALU.is_equal,
                                              fill=0.0, base=-j, channel_multiplier=1),
               reads=["ones_f"], writes=["e4"])
    for ck in range(32):
        p.emit(PH, "vector",
               lambda v, ck=ck: v.tensor_tensor(out=convdg[:, ck * 128:(ck + 1) * 128],
                                                in0=iden_f[:],
                                                in1=convw_r[:, ck:ck + 1].to_broadcast((128, 128)),
                                                op=ALU.mult),
               reads=["iden_f", "convw_r"], writes=["convdg"])
    p.emit(PH, "gpsimd",
           lambda g: g.dma_start(out=winbf.rearrange("p (k e) -> p k e", k=8),
                                 in_=winT_h.rearrange("(k p) e -> p k e", p=128)),
           writes=["winbf"], dma=True, dsem="winbf")
    p.emit(PH, "gpsimd",
           lambda g: g.dma_start(out=winwbf.rearrange("p (k e) -> p k e", k=8),
                                 in_=winwT_h.rearrange("(k p) e -> p k e", p=128)),
           writes=["winwbf"], dma=True, dsem="winwbf")
    p.emit(PH, "gpsimd",
           lambda g: g.dma_start(out=wobf.rearrange("p (k e) -> p k e", k=8),
                                 in_=woT_h.rearrange("(k p) e -> p k e", p=128)),
           writes=["wobf"], dma=True, dsem="wobf")
    for m in range(8):
        for k in range(8):
            mm(PH, pC[:, m:m + 1], winbf[:, k * D + m * 128:k * D + (m + 1) * 128],
               ones_b[:], reads=["winbf", "ones_b"], pres="pC",
               start=(k == 0), stop=(k == 7))
    p.emit(PH, "vector",
           lambda v: v.tensor_scalar(out=nwinsum[:], in0=pC[:, 0:8], scalar1=-1.0,
                                     scalar2=None, op0=ALU.mult),
           reads=["pC"], writes=["nwinsum"])
    for k in range(8):
        mm(PH, pD[0:H, 0:1], winwbf[:, k * H:(k + 1) * H], ones_b[:],
           reads=["winwbf", "ones_b"], pres="pD", start=(k == 0), stop=(k == 7))
    p.emit(PH, "vector",
           lambda v: v.tensor_scalar(out=nwinwsum[:], in0=pD[0:H, 0:1], scalar1=-1.0,
                                     scalar2=None, op0=ALU.mult),
           reads=["pD"], writes=["nwinwsum"])

    tern_tiles = [(0, k) for k in range(32)] + [(1, k) for k in range(32)]

    def tern_abs(wi, k):
        wh = w1T_h if wi == 0 else w2T_h
        if wi == 0:
            src = wh[(k // 4) * 128:(k // 4 + 1) * 128, (k % 4) * 1024:(k % 4 + 1) * 1024]
        else:
            src = wh[k * 128:(k + 1) * 128, :]
        hb_ = k % 2
        p.emit(ST1, "sync",
               lambda e, s=src, hb_=hb_: e.dma_start(out=wstage[:, hb_ * 1024:(hb_ + 1) * 1024], in_=s),
               writes=[f"wstage{hb_}"], dma=True, dsem=f"wstage{hb_}")
        p.emit(ST1, "vector",
               lambda v, hb_=hb_: v.tensor_reduce(out=abscol[:, hb_:hb_ + 1],
                                                  in_=wstage[:, hb_ * 1024:(hb_ + 1) * 1024],
                                                  axis=AX.X, op=ALU.add,
                                                  apply_absolute_value=True),
               reads=[f"wstage{hb_}"], writes=[f"abscol{hb_}"])
        p.emit(ST1, "vector",
               lambda v, wi=wi, hb_=hb_: v.tensor_tensor(out=absacc[:, wi:wi + 1],
                                                         in0=absacc[:, wi:wi + 1],
                                                         in1=abscol[:, hb_:hb_ + 1], op=ALU.add),
               reads=[f"abscol{hb_}", "absacc"], writes=["absacc"])

    def tern_scale(wi):
        mm(ST1, pC[0:1, 0:1], absacc[:, wi:wi + 1], ones_f[:],
           reads=["absacc", "ones_f"], pres="pC")
        p.emit(ST1, "scalar",
               lambda se: se.activation(out=sc2[0:1, 0:1], in_=pC[0:1, 0:1],
                                        func=AF.Identity, scale=np.float32(1.0 / (4096.0 * 1024.0))),
               reads=["pC"], writes=["sc2"])
        p.emit(ST1, "vector",
               lambda v: v.tensor_scalar(out=sc2[0:1, 0:1], in0=sc2[0:1, 0:1],
                                         scalar1=1e-8, scalar2=None, op0=ALU.max),
               reads=["sc2"], writes=["sc2"])
        p.emit(ST1, "scalar",
               lambda se: se.activation(out=sc2[0:1, 1:2], in_=sc2[0:1, 0:1], func=AF.Ln),
               reads=["sc2"], writes=["sc2"])
        p.emit(ST1, "scalar",
               lambda se: se.activation(out=sc2[0:1, 2:3], in_=sc2[0:1, 1:2],
                                        func=AF.Exp, scale=np.float32(-1.0)),
               reads=["sc2"], writes=["sc2"])
        mm(ST1, pC[:, 0:1], onesr[:], sc2[0:1, 2:3], reads=["sc2", "onesr"], pres="pC")
        p.emit(ST1, "vector",
               lambda v, wi=wi: v.tensor_copy(invsc[:, wi:wi + 1], pC[:, 0:1]),
               reads=["pC"], writes=["invsc"])

    def tern_sign(wi, k):
        wh = w1T_h if wi == 0 else w2T_h
        wth = w1t_h if wi == 0 else w2t_h
        if wi == 0:
            sl = ((k // 4) * 128, (k % 4) * 1024)
        else:
            sl = (k * 128, 0)
        src = wh[sl[0]:sl[0] + 128, sl[1]:sl[1] + 1024]
        dst = wth[sl[0]:sl[0] + 128, sl[1]:sl[1] + 1024]
        hb_ = k % 2
        o = hb_ * 1024
        p.emit(ST1, "sync",
               lambda e, s=src, o=o: e.dma_start(out=wstage[:, o:o + 1024], in_=s),
               writes=[f"wstage{hb_}"], dma=True, dsem=f"wstage{hb_}")
        p.emit(ST1, "scalar",
               lambda se, o=o, wi=wi: se.activation(out=tsg1[:], in_=wstage[:, o:o + 1024],
                                                    func=AF.Sign,
                                                    scale=invsc[:, wi:wi + 1], bias=np.float32(-0.5)),
               reads=[f"wstage{hb_}", "invsc"], writes=["tsg1"])
        p.emit(ST1, "scalar",
               lambda se, o=o, wi=wi: se.activation(out=tsg2[:], in_=wstage[:, o:o + 1024],
                                                    func=AF.Sign,
                                                    scale=invsc[:, wi:wi + 1], bias=np.float32(0.5)),
               reads=[f"wstage{hb_}", "invsc"], writes=["tsg2"])
        p.emit(ST1, "vector",
               lambda v: v.tensor_tensor(out=tout[:], in0=tsg1[:], in1=tsg2[:], op=ALU.add),
               reads=["tsg1", "tsg2"], writes=["tout"])
        p.emit(ST1, "gpsimd",
               lambda g, d=dst: g.dma_start(out=d, in_=tout[:]),
               reads=["tout"], dma=True, dsem="ternst")

    def tern_step(s):
        if s in (0, 1):
            for k in range(16 * s, 16 * s + 16):
                tern_abs(0, k)
        if s == 2:
            tern_scale(0)
        if s in (2, 3):
            for k in range(16 * (s - 2), 16 * (s - 2) + 16):
                tern_abs(1, k)
        if 2 <= s <= 5:
            for k in range(8 * (s - 2), 8 * (s - 2) + 8):
                tern_sign(0, k)
        if s == 4:
            tern_scale(1)
        if 4 <= s <= 7:
            for k in range(8 * (s - 4), 8 * (s - 4) + 8):
                tern_sign(1, k)

    for s in range(NS1):
        t0 = s * TS1
        tern_step(s)

        p.emit(ST1, "sync",
               lambda e, t0=t0: e.dma_start(out=x0tm.rearrange("p (j d) -> p j d", j=4),
                                            in_=x0_h[t0:t0 + TS1, :].rearrange("(j p) d -> p j d", p=128)),
               writes=["x0tm"], dma=True, dsem="x0tm")
        p.emit(ST1, "gpsimd",
               lambda g, t0=t0: g.dma_start(out=x0fmb.rearrange("p (k t) -> p k t", k=8),
                                            in_=x0T_h[:, t0:t0 + TS1].rearrange("(k p) t -> p k t", p=128)),
               writes=["x0fmb"], dma=True, dsem="x0fmb")

        for j in range(4):
            p.emit(ST1, "vector",
                   lambda v, j=j: v.tensor_scalar(out=junkf[:, 0:D], in0=x0tm[:, j * D:(j + 1) * D],
                                                  scalar1=1.0, scalar2=0.0, op0=ALU.mult,
                                                  op1=ALU.add, accum_out=sc[:, j:j + 1]),
                   reads=["x0tm"], writes=["sc_s1"])
            p.emit(ST1, "vector",
                   lambda v, j=j: v.scalar_tensor_tensor(out=junkf[:, 0:D],
                                                         in0=x0tm[:, j * D:(j + 1) * D], scalar=1.0,
                                                         in1=x0tm[:, j * D:(j + 1) * D],
                                                         op0=ALU.mult, op1=ALU.mult,
                                                         accum_out=sc[:, 4 + j:5 + j]),
                   reads=["x0tm"], writes=["sc_q1"])
        p.emit(ST1, "vector",
               lambda v, s=s: v.tensor_scalar(out=m1s[:, 4 * s:4 * s + 4], in0=sc[:, 0:4],
                                              scalar1=1.0 / 1024.0, scalar2=None, op0=ALU.mult),
               reads=["sc_s1"], writes=["m1s"])
        p.emit(ST1, "vector",
               lambda v, s=s: v.scalar_tensor_tensor(out=r1s[:, 4 * s:4 * s + 4],
                                                     in0=m1s[:, 4 * s:4 * s + 4], scalar=-1.0,
                                                     in1=m1s[:, 4 * s:4 * s + 4],
                                                     op0=ALU.mult, op1=ALU.mult),
               reads=["m1s"], writes=["r1s"])
        p.emit(ST1, "vector",
               lambda v, s=s: v.scalar_tensor_tensor(out=r1s[:, 4 * s:4 * s + 4],
                                                     in0=sc[:, 4:8], scalar=1.0 / 1024.0,
                                                     in1=r1s[:, 4 * s:4 * s + 4],
                                                     op0=ALU.mult, op1=ALU.add),
               reads=["sc_q1", "r1s"], writes=["r1s"])
        rsqrt_cols(ST1, r1s[:, 4 * s:4 * s + 4], "r1s")
        p.emit(ST1, "vector",
               lambda v, s=s: v.tensor_tensor(out=strow[:, 0:4], in0=m1s[:, 4 * s:4 * s + 4],
                                              in1=r1s[:, 4 * s:4 * s + 4], op=ALU.mult),
               reads=["m1s", "r1s"], writes=["strow"])
        p.emit(ST1, "vector",
               lambda v, s=s: v.tensor_copy(strow[:, 4:8], r1s[:, 4 * s:4 * s + 4]),
               reads=["r1s", "strow"], writes=["strow"])
        tp(ST1, pC[0:4, 0:128], strow[:, 0:4], reads=["strow"], pres="pC")
        p.emit(ST1, "vector", lambda v: v.tensor_copy(mrT[:], pC[0:4, 0:128]),
               reads=["pC"], writes=["mrT"])
        tp(ST1, pD[0:4, 0:128], strow[:, 4:8], reads=["strow"], pres="pD")
        p.emit(ST1, "vector", lambda v: v.tensor_copy(rT[:], pD[0:4, 0:128]),
               reads=["pD"], writes=["rT"])
        for j in range(4):
            mm(ST1, pA[:, j * 128:(j + 1) * 128], e4[:, j * 128:(j + 1) * 128], mrT[:],
               reads=["e4", "mrT"], pres="pA", start=(j == 0), stop=(j == 3))
        for j in range(4):
            mm(ST1, pB[:, j * 128:(j + 1) * 128], e4[:, j * 128:(j + 1) * 128], rT[:],
               reads=["e4", "rT"], pres="pB", start=(j == 0), stop=(j == 3))
        p.emit(ST1, "scalar", lambda se: se.activation(out=mr1b[:], in_=pA[:], func=AF.Identity),
               reads=["pA"], writes=["mr1b"])
        p.emit(ST1, "scalar", lambda se: se.activation(out=r1b[:], in_=pB[:], func=AF.Identity),
               reads=["pB"], writes=["r1b"])

        if s == 0:
            if "d_mr1b" in dbg_h:
                p.emit(ST1, "gpsimd", lambda g: g.dma_start(out=dbg_h["d_mr1b"], in_=mr1b[:]),
                       reads=["mr1b"], dma=True, dsem="dbg")
            if "d_r1b" in dbg_h:
                p.emit(ST1, "gpsimd", lambda g: g.dma_start(out=dbg_h["d_r1b"], in_=r1b[:]),
                       reads=["r1b"], dma=True, dsem="dbg")
            if "d_x0fmb" in dbg_h:
                p.emit(ST1, "gpsimd", lambda g: g.dma_start(out=dbg_h["d_x0fmb"], in_=x0fmb[:]),
                       reads=["x0fmb"], dma=True, dsem="dbg")
        if "d_m1" in dbg_h and s == NS1 - 1:
            p.emit(ST1, "gpsimd", lambda g: g.dma_start(out=dbg_h["d_m1"], in_=m1s[:]),
                   reads=["m1s"], dma=True, dsem="dbg")
        if "d_r1" in dbg_h and s == NS1 - 1:
            p.emit(ST1, "gpsimd", lambda g: g.dma_start(out=dbg_h["d_r1"], in_=r1s[:]),
                   reads=["r1s"], dma=True, dsem="dbg")

        for k in range(8):
            mm(ST1, pC[0:H, :], winwbf[:, k * H:(k + 1) * H], x0fmb[:, k * TS1:(k + 1) * TS1],
               reads=["winwbf", "x0fmb"], pres="pC", start=(k == 0), stop=(k == 7))
        p.emit(ST1, "vector",
               lambda v: v.scalar_tensor_tensor(out=wfmA[:], in0=mr1b[0:H, :],
                                                scalar=nwinwsum[:, 0:1], op0=ALU.mult,
                                                in1=pC[0:H, :], op1=ALU.add),
               reads=["mr1b", "nwinwsum", "pC"], writes=["wfmA"])
        p.emit(ST1, "vector",
               lambda v: v.scalar_tensor_tensor(out=wfmA[:], in0=wfmA[:], scalar=wbase_c[:, 0:1],
                                                op0=ALU.mult, in1=r1b[0:H, :], op1=ALU.mult),
               reads=["wfmA", "wbase_c", "r1b"], writes=["wfmA"])
        if "d_A" in dbg_h:
            p.emit(ST1, "gpsimd",
                   lambda g, t0=t0: g.dma_start(out=dbg_h["d_A"][:, t0:t0 + TS1], in_=wfmA[:]),
                   reads=["wfmA"], dma=True, dsem="dbg")

        for c in range(8):
            p.emit(ST1, "vector",
                   lambda v, c=c: v.tensor_tensor_scan(out=acs[:, c * CH:(c + 1) * CH],
                                                       data0=wfmA[:, c * CH:(c + 1) * CH],
                                                       data1=zer[:], initial=0.0,
                                                       op0=ALU.add, op1=ALU.add),
                   reads=["wfmA", "zer"], writes=["acs"])
        for c in range(8):
            p.emit(ST1, "vector",
                   lambda v, c=c: v.tensor_tensor_scan(out=rmin[:, c * CH:(c + 1) * CH],
                                                       data0=acs[:, c * CH:(c + 1) * CH],
                                                       data1=zer[:], initial=1e30,
                                                       op0=ALU.min, op1=ALU.add),
                   reads=["acs", "zer"], writes=["rmin"])
        p.emit(ST1, "scalar", lambda se: se.activation(out=vv[:], in_=acs[:], func=AF.Exp, scale=np.float32(-1.0)),
               reads=["acs"], writes=["vv"])
        p.emit(ST1, "scalar", lambda se: se.activation(out=uu[:], in_=rmin[:], func=AF.Exp),
               reads=["rmin"], writes=["uu"])
        for c in range(8):
            p.emit(ST1, "vector",
                   lambda v, c=c: v.tensor_tensor_scan(out=cvv[:, c * CH:(c + 1) * CH],
                                                       data0=vv[:, c * CH:(c + 1) * CH],
                                                       data1=zer[:], initial=0.0,
                                                       op0=ALU.add, op1=ALU.add),
                   reads=["vv", "zer"], writes=["cvv"])
        for c in range(8):
            p.emit(ST1, "vector",
                   lambda v, c=c: v.tensor_reduce(out=sm8a[:, c:c + 1],
                                                  in_=acs[:, c * CH:(c + 1) * CH],
                                                  axis=AX.X, op=ALU.max, negate=True),
                   reads=["acs"], writes=["sm8a"])
        for c in range(8):
            p.emit(ST1, "scalar",
                   lambda se, c=c: se.activation(out=sdo[:, c * CH:(c + 1) * CH],
                                                 in_=acs[:, c * CH:(c + 1) * CH],
                                                 func=AF.Exp, bias=sm8a[:, c:c + 1]),
                   reads=["acs", "sm8a"], writes=["sdo"])
        p.emit(ST1, "vector",
               lambda v: v.tensor_copy(sm8b[:], acs[:, CH - 1::CH]),
               reads=["acs"], writes=["sm8b"])
        p.emit(ST1, "vector",
               lambda v, s=s: v.tensor_tensor_scan(out=cend_t[:, 1 + 8 * s:9 + 8 * s],
                                                   data0=sm8b[:], data1=zer[:, 0:8],
                                                   initial=cend_t[:, 8 * s:8 * s + 1],
                                                   op0=ALU.add, op1=ALU.add),
               reads=["sm8b", "zer", "cend_t"], writes=["cend_t"])
        p.emit(ST1, "vector",
               lambda v, s=s: v.tensor_tensor_scan(out=mrc_t[:, 1 + 8 * s:9 + 8 * s],
                                                   data0=cend_t[:, 8 * s:8 * s + 8],
                                                   data1=zer[:, 0:8],
                                                   initial=mrc_t[:, 8 * s:8 * s + 1],
                                                   op0=ALU.min, op1=ALU.add),
               reads=["cend_t", "mrc_t", "zer"], writes=["mrc_t"])
        p.emit(ST1, "vector",
               lambda v: v.tensor_copy(sm8b[:], rmin[:, CH - 1::CH]),
               reads=["rmin", "sm8b"], writes=["sm8b"])
        p.emit(ST1, "vector",
               lambda v, s=s: v.tensor_tensor(out=c1p_t[:, 8 * s:8 * s + 8],
                                              in0=cend_t[:, 1 + 8 * s:9 + 8 * s],
                                              in1=sm8b[:], op=ALU.subtract),
               reads=["cend_t", "sm8b"], writes=["c1p_t"])
        p.emit(ST1, "vector",
               lambda v, s=s: v.tensor_copy(ndu_t[:, 8 * s:8 * s + 8], cvv[:, CH - 1::CH]),
               reads=["cvv"], writes=["ndu_t"])
        p.emit(ST1, "gpsimd",
               lambda g, s=s: g.dma_start(out=rm_h.rearrange("(h z) -> h z", h=H),
                                          in_=mrc_t[:, 1 + 8 * s:9 + 8 * s]),
               reads=["mrc_t"], writes=["rm_hbm"], dma=True, dsem="rmw")
        p.emit(ST1, "gpsimd",
               lambda g: g.dma_start(out=rmrow[:], in_=rm_h.rearrange("(a b) -> a b", a=1)),
               reads=["rm_hbm"], writes=["rmrow"], dma=True, dsem="rmr")

        for m in range(8):
            pm, pmr = (pA, "pA") if m % 2 == 0 else (pB, "pB")
            for k in range(8):
                mm(ST1, pm[:], winbf[:, k * D + m * 128:k * D + (m + 1) * 128],
                   x0fmb[:, k * TS1:(k + 1) * TS1],
                   reads=["winbf", "x0fmb"], pres=pmr, start=(k == 0), stop=(k == 7))
            eb = m % 2
            p.emit(ST1, "vector",
                   lambda v, m=m, pm=pm, eb=eb: v.scalar_tensor_tensor(
                       out=e1t[:, eb * TS1:(eb + 1) * TS1], in0=mr1b[:],
                       scalar=nwinsum[:, m:m + 1], op0=ALU.mult, in1=pm[:], op1=ALU.add),
                   reads=["mr1b", "nwinsum", pmr], writes=[f"e1t{eb}"])
            p.emit(ST1, "gpsimd",
                   lambda g, m=m, eb=eb: g.tensor_tensor(
                       out=xib[:, m * (TS1 + 3) + 3:(m + 1) * (TS1 + 3)],
                       in0=e1t[:, eb * TS1:(eb + 1) * TS1], in1=r1b[:], op=ALU.mult),
                   reads=[f"e1t{eb}", "r1b"], writes=[f"xi{m}"])
        if "d_xi" in dbg_h:
            for m in range(8):
                p.emit(ST1, "gpsimd",
                       lambda g, m=m, t0=t0: g.dma_start(
                           out=dbg_h["d_xi"][m * 128:(m + 1) * 128, t0:t0 + TS1],
                           in_=xib[:, m * (TS1 + 3) + 3:(m + 1) * (TS1 + 3)]),
                       reads=[f"xi{m}"], dma=True, dsem="dbg")

        for c in range(8):
            pm, pmr = (pC, "pC") if c % 2 == 0 else (pD, "pD")
            for k in range(4):
                mm(ST1, pm[:], convdg[:, (c * 4 + k) * 128:(c * 4 + k + 1) * 128],
                   xib[:, c * (TS1 + 3) + k:c * (TS1 + 3) + k + TS1],
                   reads=["convdg", f"xi{c}", f"xihalo{c}"], pres=pmr,
                   start=(k == 0), stop=(k == 3))
            for j in range(4):
                r = j % 4
                p.emit(ST1, "scalar",
                       lambda se, j=j, pm=pm, r=r, c=c: se.activation(
                           out=xstg[:, r * 128:(r + 1) * 128],
                           in_=pm[:, j * 128:(j + 1) * 128], func=AF.Identity,
                           bias=convb_c[:, c:c + 1]),
                       reads=[pmr, "convb_c"], writes=[f"xstg{r}"])
                pj, pjr = (pA, "pA") if (c * 4 + j) % 2 == 0 else (pB, "pB")
                tp(ST1, pj[:, 0:128], xstg[:, r * 128:(r + 1) * 128],
                   reads=[f"xstg{r}"], pres=pjr)
                p.emit(ST1, "scalar",
                       lambda se, c=c, j=j, pj=pj: se.activation(
                           out=xtok[:, j * D + c * 128:j * D + (c + 1) * 128],
                           in_=pj[:, 0:128], func=AF.Identity),
                       reads=[pjr], writes=["xtok"])
        if "d_X" in dbg_h:
            p.emit(ST1, "gpsimd",
                   lambda g, t0=t0: g.dma_start(
                       out=dbg_h["d_X"][t0:t0 + TS1, :].rearrange("(j p) d -> p j d", p=128),
                       in_=xtok.rearrange("p (j d) -> p j d", j=4)),
                   reads=["xtok"], dma=True, dsem="dbg")
        p.emit(ST1, "vector",
               lambda v: v.tensor_copy(xib.rearrange("p (c t) -> p c t", c=8)[:, :, 0:3],
                                       xib.rearrange("p (c t) -> p c t", c=8)[:, :, TS1:TS1 + 3]),
               reads=[f"xi{m}" for m in range(8)],
               writes=[f"xihalo{m}" for m in range(8)])

        tp(ST1, pC[0:64, 0:H], c1p_t[:, 0:NCHUNK], reads=["c1p_t"], pres="pC")
        p.emit(ST1, "vector", lambda v: v.tensor_copy(c1pT[:], pC[0:64, 0:H]),
               reads=["pC"], writes=["c1pT"])
        tp(ST1, pD[0:64, 0:H], ndu_t[:, 0:NCHUNK], reads=["ndu_t"], pres="pD")
        p.emit(ST1, "vector", lambda v: v.tensor_copy(nduT[:], pD[0:64, 0:H]),
               reads=["pD"], writes=["nduT"])
        mm(ST1, pW1[0:64, 0:128], onesr[0:1, 0:64], rmrow[:],
           reads=["onesr", "rmrow"], pres="pW1")
        p.emit(ST1, "vector",
               lambda v: v.tensor_tensor(out=dD.rearrange("p (h z) -> p h z", h=H),
                                         in0=pW1[0:64, 0:128].rearrange("p (h z) -> p h z", h=H),
                                         in1=c1pT[:, :, None].to_broadcast((64, H, 8)),
                                         op=ALU.subtract),
               reads=["pW1", "c1pT"], writes=["dD"])
        p.emit(ST1, "scalar",
               lambda se: se.activation(out=dchb[:], in_=dD[:], func=AF.Exp),
               reads=["dD"], writes=["dchb"])
        p.emit(ST1, "gpsimd",
               lambda g, s=s: g.affine_select(out=dchm.rearrange("p (h z) -> p h z", h=H),
                                              in_=dchb.rearrange("p (h z) -> p h z", h=H),
                                              pattern=[[0, H], [1, 8]], compare_op=ALU.is_ge,
                                              fill=0.0, base=8 * s - 1, channel_multiplier=-1),
               reads=["dchb"], writes=["dchm"])
        p.emit(ST1, "vector",
               lambda v: v.tensor_tensor(out=prod.rearrange("p (h z) -> p h z", h=H),
                                         in0=dchm.rearrange("p (h z) -> p h z", h=H),
                                         in1=nduT[:, :, None].to_broadcast((64, H, 8)),
                                         op=ALU.mult),
               reads=["dchm", "nduT"], writes=["prod"])
        mm(ST1, pC[:, 0:1], prod[:], ones_f[0:64, 0:1], reads=["prod", "ones_f"], pres="pC")
        p.emit(ST1, "vector", lambda v: v.tensor_copy(ndcol[:], pC[:, 0:1]),
               reads=["pC"], writes=["ndcol"])
        p.emit(ST1, "gpsimd",
               lambda g: g.dma_start(out=nd_h.rearrange("(a b) -> a b", b=1), in_=ndcol[:]),
               reads=["ndcol"], writes=["nd_hbm"], dma=True, dsem="ndw")
        p.emit(ST1, "gpsimd",
               lambda g: g.dma_start(out=ndf[:], in_=nd_h.rearrange("(h z) -> h z", h=H)),
               reads=["nd_hbm"], writes=["ndf"], dma=True, dsem="ndr")
        p.emit(ST1, "vector",
               lambda v: v.tensor_tensor(out=nrm[:], in0=uu[:], in1=cvv[:], op=ALU.mult),
               reads=["uu", "cvv"], writes=["nrm"])
        p.emit(ST1, "vector",
               lambda v: v.tensor_tensor(out=noff.rearrange("p (c l) -> p c l", c=8),
                                         in0=sdo.rearrange("p (c l) -> p c l", c=8),
                                         in1=ndf[:, :, None].to_broadcast((H, 8, CH)),
                                         op=ALU.mult),
               reads=["sdo", "ndf"], writes=["noff"])
        p.emit(ST1, "vector",
               lambda v: v.tensor_tensor(out=nrm[:], in0=nrm[:], in1=noff[:], op=ALU.add),
               reads=["nrm", "noff"], writes=["nrm"])
        p.emit(ST1, "scalar", lambda se: se.activation(out=nrm[:], in_=nrm[:], func=AF.Ln),
               reads=["nrm"], writes=["nrm"])
        p.emit(ST1, "scalar", lambda se: se.activation(out=nrm[:], in_=nrm[:], func=AF.Exp, scale=np.float32(-1.0)),
               reads=["nrm"], writes=["nrm"])
        p.emit(ST1, "vector",
               lambda v: v.tensor_tensor(out=unq[:], in0=uu[:], in1=nrm[:], op=ALU.mult),
               reads=["uu", "nrm"], writes=["unq"])
        p.emit(ST1, "vector",
               lambda v: v.tensor_tensor(out=snq[:], in0=sdo[:], in1=nrm[:], op=ALU.mult),
               reads=["sdo", "nrm"], writes=["snq"])
        if "d_un" in dbg_h:
            p.emit(ST1, "gpsimd", lambda g, t0=t0: g.dma_start(out=dbg_h["d_un"][:, t0:t0 + TS1], in_=unq[:]),
                   reads=["unq"], dma=True, dsem="dbg")
            p.emit(ST1, "gpsimd", lambda g, t0=t0: g.dma_start(out=dbg_h["d_sn"][:, t0:t0 + TS1], in_=snq[:]),
                   reads=["snq"], dma=True, dsem="dbg")
        for nm, tt, rr in (("d_acs", acs, "acs"), ("d_vv", vv, "vv"), ("d_cvv", cvv, "cvv"),
                           ("d_sdo", sdo, "sdo"), ("d_nrm", nrm, "nrm")):
            if nm in dbg_h:
                p.emit(ST1, "gpsimd",
                       lambda g, nm=nm, tt=tt, t0=t0: g.dma_start(out=dbg_h[nm][:, t0:t0 + TS1], in_=tt[:]),
                       reads=[rr], dma=True, dsem="dbg")
        for qsrc, qdst, rs, ws in ((vv, vqt, "vv", "vqt"), (unq, unqt, "unq", "unqt"),
                                   (snq, snqt, "snq", "snqt")):
            for j in range(4):
                pj, pjr = (pC, "pC") if j % 2 == 0 else (pD, "pD")
                tp(ST1, pj[0:128, 0:H], qsrc[:, j * 128:(j + 1) * 128], reads=[rs], pres=pjr)
                p.emit(ST1, "vector",
                       lambda v, j=j, qd=qdst, pj=pj: v.tensor_copy(qd[:, j * H:(j + 1) * H],
                                                                    pj[0:128, 0:H]),
                       reads=[pjr], writes=[ws])

        for j in range(4):
            p.emit(ST1, "vector",
                   lambda v, j=j: v.tensor_tensor(
                       out=wtok[:, j * D:(j + 1) * D].rearrange("p (h n) -> p h n", h=H),
                       in0=xtok[:, j * D:(j + 1) * D].rearrange("p (h n) -> p h n", h=H),
                       in1=vqt[:, j * H:(j + 1) * H, None].to_broadcast((128, H, 64)),
                       op=ALU.mult),
                   reads=["xtok", "vqt"], writes=["wtok"])
        if "d_wtok" in dbg_h:
            p.emit(ST1, "gpsimd",
                   lambda g, t0=t0: g.dma_start(
                       out=dbg_h["d_wtok"][t0:t0 + TS1, :].rearrange("(j p) d -> p j d", p=128),
                       in_=wtok.rearrange("p (j d) -> p j d", j=4)),
                   reads=["wtok"], dma=True, dsem="dbg")
        for j in range(4):
            mm(ST1, pSa[:], indstrip[:, j * 8:(j + 1) * 8], wtok[:, j * D:j * D + 512],
               reads=["indstrip", "wtok"], pres="pSa", start=(j == 0), stop=(j == 3))
            mm(ST1, pSb[:], indstrip[:, j * 8:(j + 1) * 8], wtok[:, j * D + 512:(j + 1) * D],
               reads=["indstrip", "wtok"], pres="pSb", start=(j == 0), stop=(j == 3))
        p.emit(ST1, "vector", lambda v: v.tensor_copy(stsb[:, 0:512], pSa[:]),
               reads=["pSa"], writes=["stsb"])
        p.emit(ST1, "vector", lambda v: v.tensor_copy(stsb[:, 512:1024], pSb[:]),
               reads=["pSb"], writes=["stsb"])
        p.emit(ST1, "gpsimd",
               lambda g, s=s: g.dma_start(out=states_cm[8 * s:8 * s + 8, :], in_=stsb[:]),
               reads=["stsb"], writes=["states_cm"], dma=True, dsem="stc")
        if "d_st" in dbg_h:
            p.emit(ST1, "gpsimd",
                   lambda g, s=s: g.dma_start(out=dbg_h["d_st"][8 * s:8 * s + 8, :], in_=stsb[:]),
                   reads=["stsb"], dma=True, dsem="dbg")
        for h in range(H):
            pj, pjr = (pSa, "pSa") if h < 8 else (pSb, "pSb")
            mm(ST1, pj[:, (h % 8) * 64:(h % 8) * 64 + 64], dchm[:, h * 8:(h + 1) * 8],
               states_cm[:, h * 64:(h + 1) * 64],
               reads=["dchm", "states_cm"], pres=pjr,
               start=(h % 8 == 0), stop=(h % 8 == 7))
        p.emit(ST1, "vector", lambda v: v.tensor_copy(stpsb[:, 0:512], pSa[:]),
               reads=["pSa"], writes=["stpsb"])
        p.emit(ST1, "vector", lambda v: v.tensor_copy(stpsb[:, 512:1024], pSb[:]),
               reads=["pSb"], writes=["stpsb"])
        if "d_stp" in dbg_h:
            p.emit(ST1, "gpsimd",
                   lambda g, s=s: g.dma_start(out=dbg_h["d_stp"][8 * s:8 * s + 8, :], in_=stpsb[:]),
                   reads=["stpsb"], dma=True, dsem="dbg")
        for j in range(4):
            yo = (j % 2) * D
            yres = f"ytok{j % 2}"
            pc1, rc1, pc2, rc2 = (pW1, "pW1", pW2, "pW2") if j % 2 == 0 else (pC, "pC", pD, "pD")
            mm(ST1, pc1[:], triones[:], wtok[:, j * D:j * D + 512],
               reads=["triones", "wtok"], pres=rc1)
            mm(ST1, pc2[:], triones[:], wtok[:, j * D + 512:(j + 1) * D],
               reads=["triones", "wtok"], pres=rc2)
            for half, pc, rcn in ((0, pc1, rc1), (1, pc2, rc2)):
                p.emit(ST1, "vector",
                       lambda v, j=j, half=half, pc=pc, yo=yo: v.tensor_tensor(
                           out=ytok[:, yo + half * 512:yo + (half + 1) * 512].rearrange("p (h n) -> p h n", h=8),
                           in0=pc[:].rearrange("p (h n) -> p h n", h=8),
                           in1=unqt[:, j * H + half * 8:j * H + (half + 1) * 8, None].to_broadcast((128, 8, 64)),
                           op=ALU.mult),
                       reads=[rcn, "unqt"], writes=[yres])
            mm(ST1, pA[:], indloc[:, j * 128:(j + 1) * 128], stpsb[:, 0:512],
               reads=["indloc", "stpsb"], pres="pA")
            mm(ST1, pB[:], indloc[:, j * 128:(j + 1) * 128], stpsb[:, 512:1024],
               reads=["indloc", "stpsb"], pres="pB")
            for half, pc, rcn in ((0, pA, "pA"), (1, pB, "pB")):
                p.emit(ST1, "vector",
                       lambda v, j=j, half=half, pc=pc: v.tensor_tensor(
                           out=ytmp[:, half * 512:(half + 1) * 512].rearrange("p (h n) -> p h n", h=8),
                           in0=pc[:].rearrange("p (h n) -> p h n", h=8),
                           in1=snqt[:, j * H + half * 8:j * H + (half + 1) * 8, None].to_broadcast((128, 8, 64)),
                           op=ALU.mult),
                       reads=[rcn, "snqt"], writes=["ytmp"])
            p.emit(ST1, "gpsimd",
                   lambda g, yo=yo: g.tensor_tensor(out=ytok[:, yo:yo + D],
                                                    in0=ytok[:, yo:yo + D],
                                                    in1=ytmp[:], op=ALU.add),
                   reads=["ytmp", yres], writes=[yres])
            if "d_Y" in dbg_h:
                p.emit(ST1, "gpsimd",
                       lambda g, j=j, t0=t0, yo=yo: g.dma_start(
                           out=dbg_h["d_Y"][t0 + j * 128:t0 + (j + 1) * 128, :],
                           in_=ytok[:, yo:yo + D]),
                       reads=[yres], dma=True, dsem="dbg")
            for c in range(8):
                pj, pjr = (pW1, "pW1") if (j * 8 + c) % 2 == 0 else (pW2, "pW2")
                tp(ST1, pj[:, 0:128], ytok[:, yo + c * 128:yo + (c + 1) * 128],
                   reads=[yres], pres=pjr)
                p.emit(ST1, "scalar",
                       lambda se, j=j, c=c, pj=pj: se.activation(
                           out=yfmb[:, c * TS1 + j * 128:c * TS1 + (j + 1) * 128],
                           in_=pj[:, 0:128], func=AF.Identity),
                       reads=[pjr], writes=["yfmb"])

        for m in range(8):
            pm, pmr = (pA, "pA") if m % 2 == 0 else (pB, "pB")
            for k in range(8):
                mm(ST1, pm[:], wobf[:, k * D + m * 128:k * D + (m + 1) * 128],
                   yfmb[:, k * TS1:(k + 1) * TS1],
                   reads=["wobf", "yfmb"], pres=pmr, start=(k == 0), stop=(k == 7))
            p.emit(ST1, "scalar",
                   lambda se, pm=pm: se.activation(out=attfm[:], in_=pm[:], func=AF.Identity),
                   reads=[pmr], writes=["attfm"])
            for j in range(4):
                pj, pjr = (pC, "pC") if j % 2 == 0 else (pD, "pD")
                tp(ST1, pj[:, 0:128], attfm[:, j * 128:(j + 1) * 128],
                   reads=["attfm"], pres=pjr)
                p.emit(ST1, "vector",
                       lambda v, m=m, j=j, pj=pj: v.tensor_scalar(
                           out=atttm[:, j * D + m * 128:j * D + (m + 1) * 128],
                           in0=pj[:, 0:128], scalar1=1.0, scalar2=0.0,
                           op0=ALU.mult, op1=ALU.add, accum_out=sc[:, 8:9]),
                       reads=[pjr], writes=["atttm", "sc_p2"])
                if m == 0:
                    p.emit(ST1, "vector",
                           lambda v, j=j: v.tensor_copy(sc[:, 9 + j:10 + j], sc[:, 8:9]),
                           reads=["sc_p2"], writes=["sc_s2"])
                else:
                    p.emit(ST1, "vector",
                           lambda v, j=j: v.tensor_tensor(out=sc[:, 9 + j:10 + j],
                                                          in0=sc[:, 9 + j:10 + j],
                                                          in1=sc[:, 8:9], op=ALU.add),
                           reads=["sc_p2", "sc_s2"], writes=["sc_s2"])
        if "d_attn" in dbg_h:
            p.emit(ST1, "gpsimd",
                   lambda g, t0=t0: g.dma_start(
                       out=dbg_h["d_attn"][t0:t0 + TS1, :].rearrange("(j p) d -> p j d", p=128),
                       in_=atttm.rearrange("p (j d) -> p j d", j=4)),
                   reads=["atttm"], dma=True, dsem="dbg")
        for j in range(4):
            p.emit(ST1, "vector",
                   lambda v, j=j: v.scalar_tensor_tensor(out=junkf[:, 0:D],
                                                         in0=atttm[:, j * D:(j + 1) * D], scalar=1.0,
                                                         in1=atttm[:, j * D:(j + 1) * D],
                                                         op0=ALU.mult, op1=ALU.mult,
                                                         accum_out=sc[:, 13:14]),
                   reads=["atttm"], writes=["sc_q2"])
            p.emit(ST1, "vector",
                   lambda v, j=j: v.tensor_scalar(out=sc[:, 14:15], in0=sc[:, 9 + j:10 + j],
                                                  scalar1=1.0 / 1024.0, scalar2=None, op0=ALU.mult),
                   reads=["sc_s2"], writes=["sc_m2"])
            p.emit(ST1, "vector",
                   lambda v: v.scalar_tensor_tensor(out=sc[:, 15:16], in0=sc[:, 14:15], scalar=-1.0,
                                                    in1=sc[:, 14:15], op0=ALU.mult, op1=ALU.mult),
                   reads=["sc_m2"], writes=["sc_r2"])
            p.emit(ST1, "vector",
                   lambda v: v.scalar_tensor_tensor(out=sc[:, 15:16], in0=sc[:, 13:14],
                                                    scalar=1.0 / 1024.0, in1=sc[:, 15:16],
                                                    op0=ALU.mult, op1=ALU.add),
                   reads=["sc_q2", "sc_r2"], writes=["sc_r2"])
            rsqrt_cols(ST1, sc[:, 15:16], "sc_r2")
            p.emit(ST1, "vector",
                   lambda v: v.scalar_tensor_tensor(out=sc[:, 16:17], in0=sc[:, 14:15], scalar=-1.0,
                                                    in1=sc[:, 15:16], op0=ALU.mult, op1=ALU.mult),
                   reads=["sc_m2", "sc_r2"], writes=["sc_nm2"])
            p.emit(ST1, "scalar",
                   lambda se, j=j: se.activation(out=ln2t[:], in_=atttm[:, j * D:(j + 1) * D],
                                                 func=AF.Identity, scale=sc[:, 15:16],
                                                 bias=sc[:, 16:17]),
                   reads=["atttm", "sc_r2", "sc_nm2"], writes=["ln2t"])
            xb = j % 2
            p.emit(ST1, "gpsimd",
                   lambda g, j=j, xb=xb: g.tensor_tensor(out=x1t[:, xb * D:(xb + 1) * D],
                                                         in0=x0tm[:, j * D:(j + 1) * D],
                                                         in1=ln2t[:], op=ALU.add),
                   reads=["x0tm", "ln2t"], writes=[f"x1t{xb}"])
            p.emit(ST1, "gpsimd",
                   lambda g, j=j, xb=xb, t0=t0: g.dma_start(
                       out=x1_h[t0 + j * 128:t0 + (j + 1) * 128, :],
                       in_=x1t[:, xb * D:(xb + 1) * D]),
                   reads=[f"x1t{xb}"], dma=True, dsem=f"x1st{xb}")
            if "d_x1" in dbg_h:
                p.emit(ST1, "gpsimd",
                       lambda g, j=j, xb=xb, t0=t0: g.dma_start(
                           out=dbg_h["d_x1"][t0 + j * 128:t0 + (j + 1) * 128, :],
                           in_=x1t[:, xb * D:(xb + 1) * D]),
                       reads=[f"x1t{xb}"], dma=True, dsem="dbg")

    if "d_w1t" in dbg_h:
        p.emit(ST1, "sync", lambda e: e.dma_start(out=junkb[:], in_=w1t_h[0:128, 0:1024]),
               writes=["junkb_dbg"], dma=True, dsem="dbg")
        p.emit(ST1, "gpsimd", lambda g: g.dma_start(out=dbg_h["d_w1t"][:, 0:1024], in_=junkb[:]),
               reads=["junkb_dbg"], dma=True, dsem="dbg")
        p.emit(ST1, "sync", lambda e: e.dma_start(out=junkb[:], in_=w1t_h[0:128, 1024:2048]),
               writes=["junkb_dbg"], dma=True, dsem="dbg")
        p.emit(ST1, "gpsimd", lambda g: g.dma_start(out=dbg_h["d_w1t"][:, 1024:2048], in_=junkb[:]),
               reads=["junkb_dbg"], dma=True, dsem="dbg")

    run_stage(ST1)
    s1stack.close()

    w1tb = sb2("w1tb", [128, 8 * FF], BF16)
    w2tb = sb2("w2tb", [128, 32 * D], BF16)
    x1l = sb2("x1l", [128, 2 * 2 * D], F32)
    x1lntm = sb2("x1lntm", [128, 2 * D], BF16)
    x1fmb = sb2("x1fmb", [128, 2 * 8 * TS2], BF16)
    hbuf = sb2("hbuf", [128, 2 * TS2], BF16)
    ffmf = sb2("ffmf", [128, TS2], F32)
    ffntm = sb2("ffntm", [128, 2 * D], F32)
    ln4t = sb2("ln4t", [128, D], F32)
    x2t = sb2("x2t", [128, 2 * D], F32)
    st4 = sb2("st4", [128, 16], F32)
    pF = [ps2(f"pF{i}", [128, 512], F32) for i in range(4)]
    pG1 = ps2("pG1", [128, 512], F32)
    pG2 = ps2("pG2", [128, 512], F32)
    pT1 = ps2("pT1", [128, 512], F32)
    pT2 = ps2("pT2", [128, 512], F32)

    barrier = p.snapshot()
    for en in ENGINES:
        if en in ("vector", "gpsimd"):
            p.emit(ST2, en, lambda e: e.engine_nop(), extra_waits=barrier)
        else:
            p.emit(ST2, en, lambda e: e.nop(), extra_waits=barrier)

    p.emit(ST2, "sync", lambda e: e.dma_start(out=w1tb.rearrange("p (k f) -> p k f", k=8),
                                              in_=w1t_h.rearrange("(k p) f -> p k f", p=128)),
           writes=["w1tb"], dma=True, dsem="w1tb")
    p.emit(ST2, "sync", lambda e: e.dma_start(out=w2tb.rearrange("p (k d) -> p k d", k=32),
                                              in_=w2t_h.rearrange("(k p) d -> p k d", p=128)),
           writes=["w2tb"], dma=True, dsem="w2tb")

    def st2_pre(s):
        t0 = s * TS2
        par = s % 2
        xo = par * 2 * D
        p.emit(ST2, "sync",
               lambda e, t0=t0, xo=xo: e.dma_start(
                   out=x1l[:, xo:xo + 2 * D].rearrange("p (j d) -> p j d", j=2),
                   in_=x1_h[t0:t0 + TS2, :].rearrange("(j p) d -> p j d", p=128)),
               writes=[f"x1l{par}"], dma=True, dsem="x1l")
        for j in range(2):
            jj = s * 2 + j
            p.emit(ST2, "vector",
                   lambda v, j=j, xo=xo: v.scalar_tensor_tensor(out=junkf[:, 0:D],
                                                                in0=x1l[:, xo + j * D:xo + (j + 1) * D], scalar=1.0,
                                                                in1=x1l[:, xo + j * D:xo + (j + 1) * D],
                                                                op0=ALU.mult, op1=ALU.mult,
                                                                accum_out=st4[:, 0:1]),
                   reads=[f"x1l{par}"], writes=["st_q3"])
            p.emit(ST2, "vector",
                   lambda v, jj=jj: v.scalar_tensor_tensor(out=st4[:, 1:2],
                                                           in0=m1s[:, jj:jj + 1], scalar=-1.0,
                                                           in1=m1s[:, jj:jj + 1],
                                                           op0=ALU.mult, op1=ALU.mult),
                   reads=["m1s"], writes=["st_v3"])
            p.emit(ST2, "vector",
                   lambda v: v.scalar_tensor_tensor(out=st4[:, 2:3], in0=st4[:, 0:1],
                                                    scalar=1.0 / 1024.0, in1=st4[:, 1:2],
                                                    op0=ALU.mult, op1=ALU.add),
                   reads=["st_q3", "st_v3"], writes=["st_r3"])
            rsqrt_cols(ST2, st4[:, 2:3], "st_r3")
            p.emit(ST2, "vector",
                   lambda v, jj=jj: v.scalar_tensor_tensor(out=st4[:, 3:4],
                                                           in0=m1s[:, jj:jj + 1], scalar=-1.0,
                                                           in1=st4[:, 2:3],
                                                           op0=ALU.mult, op1=ALU.mult),
                   reads=["m1s", "st_r3"], writes=["st_nm3"])
            p.emit(ST2, "scalar",
                   lambda se, j=j, xo=xo: se.activation(out=x1lntm[:, j * D:(j + 1) * D],
                                                        in_=x1l[:, xo + j * D:xo + (j + 1) * D],
                                                        func=AF.Identity, scale=st4[:, 2:3],
                                                        bias=st4[:, 3:4]),
                   reads=[f"x1l{par}", "st_r3", "st_nm3"], writes=["x1lntm"])
        fo = par * 8 * TS2
        for j in range(2):
            for c in range(8):
                p.emit(ST2, "sync",
                       lambda e, j=j, c=c, fo=fo: e.dma_start(
                           out=x1fmb[:, fo + c * TS2 + j * 128:fo + c * TS2 + (j + 1) * 128],
                           in_=x1lntm[:, j * D + c * 128:j * D + (c + 1) * 128],
                           transpose=True),
                       reads=["x1lntm"], writes=[f"x1fmb{par}"], dma=True,
                       dsem=f"xf{par}")

    def st2_main(s):
        t0 = s * TS2
        par = s % 2
        xo = par * 2 * D
        fo = par * 8 * TS2
        for f in range(33):
            if f < 32:
                pg, pgr = (pG1, "pG1") if f % 2 == 0 else (pG2, "pG2")
                for k in range(8):
                    mm(ST2, pg[:, 0:TS2], w1tb[:, k * FF + f * 128:k * FF + (f + 1) * 128],
                       x1fmb[:, fo + k * TS2:fo + (k + 1) * TS2],
                       reads=["w1tb", f"x1fmb{par}"], pres=pgr, start=(k == 0), stop=(k == 7))
                fb = f % 2
                p.emit(ST2, "scalar",
                       lambda se, f=f, pg=pg, fb=fb: se.activation(
                           out=hbuf[:, fb * TS2:(fb + 1) * TS2], in_=pg[:, 0:TS2],
                           func=AF.Gelu, scale=np.float32(0.5), bias=b1_c[:, f:f + 1]),
                       reads=[pgr, "b1_c"], writes=[f"hb{fb}"])
            if f >= 1:
                fw = f - 1
                fb = fw % 2
                for m in range(8):
                    mm(ST2, pF[m // 2][:, (m % 2) * TS2:(m % 2 + 1) * TS2],
                       w2tb[:, fw * D + m * 128:fw * D + (m + 1) * 128],
                       hbuf[:, fb * TS2:(fb + 1) * TS2],
                       reads=["w2tb", f"hb{fb}"], pres=f"pF{m // 2}",
                       start=(fw == 0 and m % 2 == 0), stop=(fw == 31))
        for m in range(8):
            p.emit(ST2, "scalar",
                   lambda se, m=m: se.activation(out=ffmf[:],
                                                 in_=pF[m // 2][:, (m % 2) * TS2:(m % 2 + 1) * TS2],
                                                 func=AF.Identity, scale=np.float32(0.5),
                                                 bias=b2_c[:, m:m + 1]),
                   reads=[f"pF{m // 2}", "b2_c"], writes=["ffmf"])
            for j in range(2):
                pj, pjr = (pT1, "pT1") if (m * 2 + j) % 2 == 0 else (pT2, "pT2")
                tp(ST2, pj[:, 0:128], ffmf[:, j * 128:(j + 1) * 128], reads=["ffmf"], pres=pjr)
                p.emit(ST2, "vector",
                       lambda v, m=m, j=j, pj=pj: v.tensor_scalar(
                           out=ffntm[:, j * D + m * 128:j * D + (m + 1) * 128],
                           in0=pj[:, 0:128], scalar1=1.0, scalar2=0.0,
                           op0=ALU.mult, op1=ALU.add, accum_out=st4[:, 8:9]),
                       reads=[pjr], writes=["ffntm", "st_p4"])
                if m == 0:
                    p.emit(ST2, "vector",
                           lambda v, j=j: v.tensor_copy(st4[:, 4 + j:5 + j], st4[:, 8:9]),
                           reads=["st_p4"], writes=["st_s4"])
                else:
                    p.emit(ST2, "vector",
                           lambda v, j=j: v.tensor_tensor(out=st4[:, 4 + j:5 + j],
                                                          in0=st4[:, 4 + j:5 + j],
                                                          in1=st4[:, 8:9], op=ALU.add),
                           reads=["st_p4", "st_s4"], writes=["st_s4"])
        for j in range(2):
            p.emit(ST2, "vector",
                   lambda v, j=j: v.scalar_tensor_tensor(out=junkf[:, 0:D],
                                                         in0=ffntm[:, j * D:(j + 1) * D], scalar=1.0,
                                                         in1=ffntm[:, j * D:(j + 1) * D],
                                                         op0=ALU.mult, op1=ALU.mult,
                                                         accum_out=st4[:, 6:7]),
                   reads=["ffntm"], writes=["st_q4"])
            p.emit(ST2, "vector",
                   lambda v, j=j: v.tensor_scalar(out=st4[:, 9:10], in0=st4[:, 4 + j:5 + j],
                                                  scalar1=1.0 / 1024.0, scalar2=None, op0=ALU.mult),
                   reads=["st_s4"], writes=["st_m4"])
            p.emit(ST2, "vector",
                   lambda v: v.scalar_tensor_tensor(out=st4[:, 10:11], in0=st4[:, 9:10],
                                                    scalar=-1.0, in1=st4[:, 9:10],
                                                    op0=ALU.mult, op1=ALU.mult),
                   reads=["st_m4"], writes=["st_r4"])
            p.emit(ST2, "vector",
                   lambda v: v.scalar_tensor_tensor(out=st4[:, 10:11], in0=st4[:, 6:7],
                                                    scalar=1.0 / 1024.0, in1=st4[:, 10:11],
                                                    op0=ALU.mult, op1=ALU.add),
                   reads=["st_q4", "st_r4"], writes=["st_r4"])
            rsqrt_cols(ST2, st4[:, 10:11], "st_r4")
            p.emit(ST2, "vector",
                   lambda v: v.scalar_tensor_tensor(out=st4[:, 11:12], in0=st4[:, 9:10],
                                                    scalar=-1.0, in1=st4[:, 10:11],
                                                    op0=ALU.mult, op1=ALU.mult),
                   reads=["st_m4", "st_r4"], writes=["st_nm4"])
            p.emit(ST2, "scalar",
                   lambda se, j=j: se.activation(out=ln4t[:], in_=ffntm[:, j * D:(j + 1) * D],
                                                 func=AF.Identity, scale=st4[:, 10:11],
                                                 bias=st4[:, 11:12]),
                   reads=["ffntm", "st_r4", "st_nm4"], writes=["ln4t"])
            p.emit(ST2, "gpsimd",
                   lambda g, j=j, xo=xo: g.tensor_tensor(out=x2t[:, j * D:(j + 1) * D],
                                                         in0=x1l[:, xo + j * D:xo + (j + 1) * D],
                                                         in1=ln4t[:], op=ALU.add),
                   reads=[f"x1l{par}", "ln4t"], writes=["x2t"])
            p.emit(ST2, "gpsimd",
                   lambda g, j=j, t0=t0: g.dma_start(out=x2_h[t0 + j * 128:t0 + (j + 1) * 128, :],
                                                     in_=x2t[:, j * D:(j + 1) * D]),
                   reads=["x2t"], dma=True, dsem=f"x2st{j}")

    st2_pre(0)
    for s in range(NS2):
        if s + 1 < NS2:
            st2_pre(s + 1)
        st2_main(s)

    final_waits = [(sem, v) for sem, v in p.cnt.items() if sem.startswith("q_")]
    p.emit(ST2, "gpsimd", lambda g: g.engine_nop(), extra_waits=final_waits)

    run_stage(ST2)
    s2stack.close()
    stack0.close()
    return nc


def _host_inputs(inputs):
    F = np.float32
    x = np.asarray(inputs["L0_attn_x"], F)
    ipw = np.asarray(inputs["L0_attn_in_proj_w"], F)
    base = {
        "winT": np.ascontiguousarray(ipw[:1024].T),
        "winwT": np.ascontiguousarray(ipw[1024:].T),
        "woT": np.ascontiguousarray(np.asarray(inputs["L0_attn_out_proj_w"], F).T),
        "w1T": np.ascontiguousarray(np.asarray(inputs["L0_ffn_W1"], F).T),
        "w2T": np.ascontiguousarray(np.asarray(inputs["L0_ffn_W2"], F).T),
        "convw_sb": np.ascontiguousarray(
            np.asarray(inputs["L0_attn_conv_w"], F).reshape(8, 128, 4)
            .transpose(1, 0, 2).reshape(128, 32)),
        "convb_sb": np.ascontiguousarray(np.asarray(inputs["L0_attn_conv_b"], F).reshape(8, 128).T),
        "wbase_sb": np.ascontiguousarray(np.asarray(inputs["L0_attn_w_base"], F).reshape(H, 1)),
        "b1_sb": np.ascontiguousarray(np.asarray(inputs["L0_ffn_b1"], F).reshape(32, 128).T),
        "b2_sb": np.ascontiguousarray(np.asarray(inputs["L0_ffn_b2"], F).reshape(8, 128).T),
    }
    maps = []
    for b in range(x.shape[0]):
        m = dict(base)
        m["x0"] = np.ascontiguousarray(x[b])
        m["x0T"] = np.ascontiguousarray(x[b].T)
        maps.append(m)
    return maps


def kernel(**inputs):
    in_maps = _host_inputs(inputs)
    nc = build_program()
    res = run_bass_kernel_spmd(nc, in_maps, core_ids=list(range(8)))
    return np.stack([np.asarray(r["x2"], np.float32) for r in res.results])
